# version 04
# baseline (speedup 1.0000x reference)
import numpy as np
from contextlib import ExitStack
import concourse.bass as bass
import concourse.mybir as mybir
from concourse.bass_utils import run_bass_kernel_spmd

F32 = mybir.dt.float32
BF16 = mybir.dt.bfloat16
AF = mybir.ActivationFunctionType
ALU = mybir.AluOpType
AX = mybir.AxisListType

DM = 2048
SEQ = 4096
TLOC = 2048
HL = 8
NCH = 32
GRP = 4
NG = NCH // GRP
EPS_ATT = 1e-6
EPS_LN = 1e-5


def build(debug=0, phases='full', ngroups=NG, do_rec=5, do_cc=True):
    nc = bass.Bass("TRN2", target_bir_lowering=False)

    xT = nc.dram_tensor("xT", [DM, SEQ], F32, kind="ExternalInput").ap()
    xR = nc.dram_tensor("xR", [DM, TLOC], F32, kind="ExternalInput").ap()
    wq_d = nc.dram_tensor("wq", [DM, 1024], F32, kind="ExternalInput").ap()
    wk_d = nc.dram_tensor("wk", [DM, 1024], F32, kind="ExternalInput").ap()
    wv_d = nc.dram_tensor("wv", [DM, 1024], F32, kind="ExternalInput").ap()
    bq_d = nc.dram_tensor("bq_col", [128, 8], F32, kind="ExternalInput").ap()
    bk_d = nc.dram_tensor("bk_row", [1, 1024], F32, kind="ExternalInput").ap()
    bv_d = nc.dram_tensor("bv_row", [1, 1024], F32, kind="ExternalInput").ap()
    wo_d = nc.dram_tensor("wo", [DM, DM], F32, kind="ExternalInput").ap()
    bo_d = nc.dram_tensor("bo_col", [128, 16], F32, kind="ExternalInput").ap()
    rw_d = nc.dram_tensor("rw_t", [128, 128], F32, kind="ExternalInput").ap()
    rb_d = nc.dram_tensor("rb_row", [1, 8], F32, kind="ExternalInput").ap()
    ew_d = nc.dram_tensor("ew", [8, DM, DM], F32, kind="ExternalInput").ap()
    eb_d = nc.dram_tensor("eb_row", [1, 8 * DM], F32, kind="ExternalInput").ap()
    ow_d = nc.dram_tensor("ow", [DM, DM], F32, kind="ExternalInput").ap()
    ob_d = nc.dram_tensor("ob_col", [128, 16], F32, kind="ExternalInput").ap()
    mask_d = nc.dram_tensor("maskup", [128, 128], F32, kind="ExternalInput").ap()
    eye_d = nc.dram_tensor("eye", [128, 128], F32, kind="ExternalInput").ap()
    sel_d = nc.dram_tensor("sel_col", [128, 2], F32, kind="ExternalInput").ap()
    out_d = nc.dram_tensor("out_fm", [DM, TLOC], F32, kind="ExternalOutput").ap()

    cc_in = nc.dram_tensor("cc_in", [DM, TLOC], BF16, kind="Internal")
    cc_out = nc.dram_tensor("cc_out", [2 * DM, TLOC], BF16, kind="Internal")

    dbg = {}
    if debug & 1:
        dbg["ao"] = nc.dram_tensor("dbg_ao", [2 * DM, TLOC], F32, kind="ExternalOutput").ap()
    if debug & 2:
        dbg["h"] = nc.dram_tensor("dbg_h", [128, 16 * TLOC], F32, kind="ExternalOutput").ap()
    if debug & 4:
        dbg["gates"] = nc.dram_tensor("dbg_gates", [128, 16 * 8], F32, kind="ExternalOutput").ap()
    if debug & 8:
        dbg["acc"] = nc.dram_tensor("dbg_acc", [128, 16 * DM], F32, kind="ExternalOutput").ap()

    with ExitStack() as ctx:
        ab = ctx.enter_context(nc.sbuf_tensor("ab", [128, 93320], BF16))
        af = ctx.enter_context(nc.sbuf_tensor("af", [128, 5648], F32))
        eye_b = ctx.enter_context(nc.sbuf_tensor("eye_b", [128, 128], BF16))
        mask_f = ctx.enter_context(nc.sbuf_tensor("mask_f", [128, 128], F32))
        rw_b = ctx.enter_context(nc.sbuf_tensor("rw_b", [128, 128], BF16))
        gates_sb = ctx.enter_context(nc.sbuf_tensor("gates_sb", [128, 128], F32))
        bq_sb = ctx.enter_context(nc.sbuf_tensor("bq_sb", [128, 8], F32))
        bo_sb = ctx.enter_context(nc.sbuf_tensor("bo_sb", [128, 16], F32))
        ob_sb = ctx.enter_context(nc.sbuf_tensor("ob_sb", [128, 16], F32))
        sel_sb = ctx.enter_context(nc.sbuf_tensor("sel_sb", [128, 2], F32))
        onec_b = ctx.enter_context(nc.sbuf_tensor("onec_b", [128, 1], BF16))

        A_WQ, A_WK = 0, 16384
        A_H = 0
        B_WV = 32768
        B_AO, B_XS = 32768, 40960
        B_WE0, B_WE1 = 32768, 40960
        B_COMB, B_S2 = 32768, 40960
        C_X0, C_X1 = 49152, 57344
        C_QF, C_KF, C_VC = 65536, 69632, 73728
        C_AOT0, C_AOT1 = 77888, 78912
        C_WO = 49152
        C_ACC = 49152
        D_SNAP = 81920
        D_KFF0, D_KFF1 = 82952, 83976
        D_AOF0, D_AOF1 = 85000, 86024
        D_ASB0, D_ASB1 = 87048, 87176
        D_S1 = 81920
        D_SQ0, D_SQ1 = 90112, 90624
        D_GL0, D_GL1 = 81920, 82432
        D_OW0, D_OW1 = 81920, 83968
        S_EBST = 91136
        AO_W0A, AO_W1A = 91136, 91648
        AO_W0B, AO_W1B = 92160, 92672
        S_BK = 91136
        S_BV = 92160
        S_ONER = 93320 - 136
        S_RB = 93320 - 8

        F_XST = 0
        F_WST = 1024
        F_RE = 3072
        ST0, ST1, ST2 = 4096, 4608, 5120
        F_Z = 5632

        ps_state = ctx.enter_context(nc.psum_tensor("ps_state", [128, 1536], F32))
        ps_proj = ctx.enter_context(nc.psum_tensor("ps_proj", [128, 1024], F32))
        ps_small = ctx.enter_context(nc.psum_tensor("ps_small", [128, 1024], F32))
        ps_bf = ctx.enter_context(nc.psum_tensor("ps_bf", [128, 512], BF16))

        sems = {}
        SEM_NAMES = ("T", "V", "S", "C", "ST", "W0", "W1", "W2", "W3",
                     "X0", "X1", "P0", "P1", "P2", "P3", "O0", "O1", "A0", "A1", "EB", "DG0", "DG1")
        for s in SEM_NAMES:
            sems[s] = ctx.enter_context(nc.semaphore("sem" + s))

        SEME = {"tensor": "T", "vector": "V", "scalar": "S"}
        cnt = {s: 0 for s in SEM_NAMES}
        ops = {k: [] for k in ("sync", "gpsimd", "tensor", "vector", "scalar")}
        lastw = {}

        def _wait(eng, sem, val):
            if val <= 0:
                return
            if lastw.get((eng, sem), 0) >= val:
                return
            lastw[(eng, sem)] = val
            ops[eng].append(lambda e, s=sem, v=val: e.wait_ge(sems[s], v))

        def W(eng, *toks):
            best = {}
            for t in toks:
                if not t:
                    continue
                for u in (t if isinstance(t, list) else [t]):
                    if u:
                        best[u[0]] = max(best.get(u[0], 0), u[1])
            for s, v in best.items():
                _wait(eng, s, v)

        BUCK = 64
        track = {}
        BSZ = {}

        def _rng(ap):
            t = ap.tensor
            nm = t.name
            is_dram = "DRam" in type(t).__name__
            if nm not in BSZ:
                tot = 1
                for d in t.shape:
                    tot *= int(d)
                BSZ[nm] = max(BUCK, tot >> 6) if is_dram else BUCK
            rowlen = int(t.shape[-1])
            lo = int(ap.offset)
            hi = lo + 1
            skipped = is_dram
            for step, count in ap.ap:
                step = int(step); count = int(count)
                if not skipped and step == rowlen and len(t.shape) > 1:
                    skipped = True
                    continue
                hi += (count - 1) * step
            return (nm, lo, hi)

        def _collect(entry, key, deps):
            d = entry.get(key)
            if d:
                for s_, v_ in d.items():
                    if deps.get(s_, 0) < v_:
                        deps[s_] = v_

        def _deps_and_reg(rd_aps, wr_aps, token, wr_extra=None):
            deps = {}
            rranges = [_rng(ap) for ap in rd_aps]
            wranges = [_rng(ap) for ap in wr_aps]
            if wr_extra is not None:
                wranges = [wr_extra] + [w for w in wranges]
            for nm, lo, hi in rranges:
                d = track.get(nm)
                if d:
                    bz = BSZ[nm]
                    for b in range(lo // bz, (hi - 1) // bz + 1):
                        e = d.get(b)
                        if e:
                            _collect(e, "w", deps)
            for nm, lo, hi in wranges:
                d = track.get(nm)
                if d:
                    bz = BSZ[nm]
                    for b in range(lo // bz, (hi - 1) // bz + 1):
                        e = d.get(b)
                        if e:
                            _collect(e, "w", deps)
                            _collect(e, "r", deps)
            s_, v_ = token
            for nm, lo, hi in wranges:
                d = track.setdefault(nm, {})
                bz = BSZ[nm]
                for b in range(lo // bz, (hi - 1) // bz + 1):
                    d[b] = {"w": {s_: v_}, "r": {}}
            for nm, lo, hi in rranges:
                d = track.setdefault(nm, {})
                bz = BSZ[nm]
                for b in range(lo // bz, (hi - 1) // bz + 1):
                    e = d.setdefault(b, {"w": {}, "r": {}})
                    if e["r"].get(s_, 0) < v_:
                        e["r"][s_] = v_
            return deps

        RW = {
            "tensor_copy": (1, (1,)), "tensor_tensor": (1, (1, 2)),
            "tensor_scalar": (1, (1, 2, 3)), "tensor_single_scalar": (1, (1, 2)),
            "scalar_tensor_tensor": (1, (1, 2, 3)),
            "tensor_scalar_add": (1, (1, 2)), "tensor_scalar_sub": (1, (1, 2)),
            "tensor_scalar_mul": (1, (1, 2)), "tensor_scalar_min": (1, (1, 2)),
            "tensor_scalar_max": (1, (1, 2)),
            "reduce_max": (1, (1,)), "reduce_sum": (1, (1,)), "reciprocal": (1, (1,)),
            "memset": (1, ()), "memzero": (1, ()),
            "activation": (1, (1,)), "copy": (1, (1,)), "add": (1, (1, 2)),
            "mul": (1, (1, 2)), "square": (1, (1,)), "sqrt": (1, (1,)),
            "matmul": (1, (0, 1, 2)), "transpose": (1, (1, 2)),
        }

        def _is_ap(x):
            return hasattr(x, "tensor") and hasattr(x, "offset")

        def OP(eng, meth, *a, inc=True, **kw):
            s = SEME[eng]
            cnt[s] += 1
            v = cnt[s]
            token = (s, v)
            nw, rds = RW[meth]
            wr_aps = [x for x in a[:nw] if _is_ap(x)]
            rd_aps = [a[i] for i in rds if i < len(a) and _is_ap(a[i])]
            rd_aps += [x for x in kw.values() if _is_ap(x) and not isinstance(x, bool)]
            wr_extra = None
            if meth in ("matmul", "transpose") and kw.get("start", True):
                t = wr_aps[0].tensor
                if "PSum" in type(t).__name__:
                    esz = mybir.dt.size(t.dtype)
                    region = 2048 // esz
                    nm, lo, hi = _rng(wr_aps[0])
                    lo = (lo // region) * region
                    hi = ((hi + region - 1) // region) * region
                    wr_extra = (nm, lo, hi)
            deps = _deps_and_reg(rd_aps, wr_aps, token, wr_extra)
            if eng == "tensor":
                deps.pop("T", None)
            for s_, v_ in deps.items():
                _wait(eng, s_, v_)
            ops[eng].append(lambda e, m=meth, a=a, kw=kw, s=s: getattr(e, m)(*a, **kw).then_inc(sems[s], 1))
            return token

        def DMA(q, s, out, in_, record=True):
            cnt[s] += 16
            v = cnt[s]
            token = (s, v)
            if record:
                deps = _deps_and_reg([in_], [out], token)
            else:
                deps = _deps_and_reg([in_], [], token)
            for s_, v_ in deps.items():
                _wait(q, s_, v_)
            ops[q].append(lambda e, o=out, i=in_, s=s: e.dma_start(out=o, in_=i).then_inc(sems[s], 16))
            return token

        def REG(rd_aps, wr_aps, token, eng):
            deps = _deps_and_reg(rd_aps, wr_aps, token)
            for s_, v_ in deps.items():
                _wait(eng, s_, v_)

        def REREG_W(aps, token):
            s_, v_ = token
            for ap in aps:
                nm, lo, hi = _rng(ap)
                d = track.setdefault(nm, {})
                bz = BSZ[nm]
                for b in range(lo // bz, (hi - 1) // bz + 1):
                    d[b] = {"w": {s_: v_}, "r": {}}

        class Rot:
            def __init__(self, items):
                self.items = items
                self.busy = [[] for _ in items]
                self.i = 0

            def acquire(self):
                j = self.i % len(self.items)
                self.i += 1
                b = self.busy[j]
                self.busy[j] = []
                return j, self.items[j], b

            def use(self, j, tok):
                if tok:
                    self.busy[j].append(tok)

        def abr(c0, n):
            return ab[:, c0:c0 + n]

        def afr(c0, n):
            return af[:, c0:c0 + n]

        t_eye_l = DMA("sync", "ST", af[:, F_WST:F_WST + 128], eye_d)
        t_mask_l = DMA("sync", "ST", mask_f[:, :], mask_d)
        t_rw_l = DMA("sync", "ST", af[:, F_WST + 128:F_WST + 256], rw_d)
        DMA("sync", "ST", bq_sb[:, :], bq_d)
        DMA("sync", "ST", bo_sb[:, :], bo_d)
        DMA("sync", "ST", ob_sb[:, :], ob_d)
        t_sel_l = DMA("sync", "ST", sel_sb[:, :], sel_d)
        t_bk_l = DMA("sync", "ST", af[0:1, F_WST + 256:F_WST + 1280], bk_d)
        t_bv_l = DMA("sync", "ST", af[0:1, F_XST:F_XST + 1024], bv_d)
        t_rb_l = DMA("sync", "ST", af[0:1, F_Z + 8:F_Z + 16], rb_d)
        st_all = ("ST", cnt["ST"])
        t_eye_l = t_mask_l = t_rw_l = t_sel_l = t_bk_l = t_bv_l = t_rb_l = t_bq_l = st_all
        REREG_W([af[:, F_WST:F_WST + 256], mask_f[:, :], bq_sb[:, :], bo_sb[:, :],
                 ob_sb[:, :], sel_sb[:, :], af[0:1, F_WST + 256:F_WST + 1280],
                 af[0:1, F_XST:F_XST + 1024], af[0:1, F_Z + 8:F_Z + 16]], st_all)
        W("vector", st_all)
        t_eye_c = OP("vector", "tensor_copy", eye_b[:, :], af[:, F_WST:F_WST + 128], inc=True)
        t_rw_c = OP("vector", "tensor_copy", rw_b[:, :], af[:, F_WST + 128:F_WST + 256], inc=True)
        t_bk_c = OP("vector", "tensor_copy", ab[0:1, S_BK:S_BK + 1024], af[0:1, F_WST + 256:F_WST + 1280], inc=True)
        t_bv_c = OP("vector", "tensor_copy", ab[0:1, S_BV:S_BV + 1024], af[0:1, F_XST:F_XST + 1024], inc=True)
        t_rb_c = OP("vector", "tensor_copy", ab[0:1, S_RB:S_RB + 8], af[0:1, F_Z + 8:F_Z + 16], inc=True)
        t_oner = OP("vector", "memset", ab[0:1, S_ONER:S_ONER + 128], 1.0, inc=True)
        t_onec = OP("vector", "memset", onec_b[:, :], 1.0, inc=True)
        t_eps = OP("vector", "memset", af[0:1, F_Z + 4:F_Z + 5], EPS_LN, inc=True)
        vc_ones = abr(C_VC, 32 * 129).rearrange("p (s c) -> p s c", c=129)[:, :, 128:129]
        t_vcone = OP("vector", "memset", vc_ones, 1.0, inc=True)

        wst = Rot([afr(F_WST, 512), afr(F_WST + 512, 512),
                   afr(F_WST + 1024, 512), afr(F_WST + 1536, 512)])
        w_tok = {}
        first_w = True
        for name, wd, dst in (("q", wq_d, A_WQ), ("k", wk_d, A_WK), ("v", wv_d, B_WV)):
            for k in range(16):
                for half in range(2):
                    j, st, free = wst.acquire()
                    if first_w:
                        W("sync", t_bk_c, t_bv_c, t_eye_c, t_rw_c)
                        first_w = False
                    W("sync", free)
                    tl = DMA("sync", "W%d" % j, st, wd[k * 128:(k + 1) * 128, half * 512:(half + 1) * 512])
                    W("vector", tl)
                    tc = OP("vector", "tensor_copy", abr(dst + k * 1024 + half * 512, 512), st, inc=True)
                    wst.use(j, tc)
                    w_tok[(name, k, half)] = tc

        xst = Rot([afr(F_XST, 512), afr(F_XST + 512, 512)])
        xbuf = Rot([C_X0, C_X1])
        proj = Rot([ps_proj[:, 0:512], ps_proj[:, 512:1024]])
        re_rot = Rot([(afr(F_RE, 512), afr(F_RE + 512, 512))])
        kfT = Rot([ps_bf[:, 0:128], ps_bf[:, 128:256], ps_bf[:, 256:384], ps_bf[:, 384:512]])
        aT = Rot([ps_small[:, 0:128], ps_small[:, 128:256]])
        outp = Rot([ps_small[:, 256:385], ps_small[:, 512:641]])
        kff = Rot([D_KFF0, D_KFF1])
        aofm = Rot([D_AOF0, D_AOF1])
        asb = Rot([D_ASB0, D_ASB1])
        aotm = Rot([C_AOT0, C_AOT1])
        zscr = Rot([(af[:, F_Z:F_Z + 1], af[:, F_Z + 1:F_Z + 2]),
                    (af[:, F_Z + 2:F_Z + 3], af[:, F_Z + 3:F_Z + 4])])

        snap_ap = abr(D_SNAP, 1032)

        def state_slot(h):
            o = (h // 3) * 512 + (h % 3) * 129
            return ps_state[:, o:o + 129]

        def snap_slot(h):
            return abr(D_SNAP + h * 129, 129)

        qkv_read_tok = []
        snap_read_tok = []
        kv_mm_tok = []

        for g in range(ngroups):
            jx, xbase, xfree = xbuf.acquire()
            xcast_tok = []
            for k in range(16):
                js, st, sfree = xst.acquire()
                W("sync", sfree, t_bv_c if g == 0 and k == 0 else None)
                tl = DMA("sync", "X%d" % js, st, xT[k * 128:(k + 1) * 128, g * 512:(g + 1) * 512])
                W("vector", tl)
                if k == 0:
                    W("vector", xfree)
                tc = OP("vector", "tensor_copy", abr(xbase + k * 512, 512), st, inc=True)
                xst.use(js, tc)
                xcast_tok.append(tc)

            qf_writers = []
            for m in range(8):
                jp, pp, pfree = proj.acquire()
                W("tensor", pfree, xcast_tok)
                tmm = None
                for k in range(16):
                    W("tensor", w_tok[("q", k, m // 4)])
                    tmm = OP("tensor", "matmul", pp,
                             abr(A_WQ + k * 1024 + m * 128, 128),
                             abr(xbase + k * 512, 512),
                             start=(k == 0), stop=(k == 15),
                             skip_group_check=True, inc=(k == 15))
                jr, (rr, ee), rfree = re_rot.acquire()
                W("scalar", tmm, rfree, t_bq_l)
                ts_r = OP("scalar", "activation", rr, pp, AF.Relu, bias=bq_sb[:, m:m + 1], inc=True)
                ts_e = OP("scalar", "activation", ee, pp, AF.Exp, bias=bq_sb[:, m:m + 1], inc=True)
                W("vector", ts_r, ts_e, qkv_read_tok)
                tq = OP("vector", "scalar_tensor_tensor",
                        abr(C_QF + m * 512, 512), ee, 1.0, rr, ALU.min, ALU.add, inc=True)
                re_rot.use(jr, tq)
                proj.use(jp, ts_r)
                proj.use(jp, ts_e)
                qf_writers.append(tq)

            kf_w = [None] * (GRP * 2)
            vc_w = [None] * (GRP * 2)
            for cc in range(GRP):
                gc = g * GRP + cc
                for dt in range(2):
                    jp, pp, pfree = proj.acquire()
                    W("tensor", pfree, t_oner, t_bk_c)
                    OP("tensor", "matmul", pp,
                       ab[0:1, S_ONER:S_ONER + 128],
                       ab[0:1, S_BK + dt * 512:S_BK + dt * 512 + 512],
                       start=True, stop=False, skip_group_check=True)
                    tmm = None
                    for k in range(16):
                        W("tensor", w_tok[("k", k, dt)])
                        tmm = OP("tensor", "matmul", pp,
                                 abr(xbase + k * 512 + cc * 128, 128),
                                 abr(A_WK + k * 1024 + dt * 512, 512),
                                 start=False, stop=(k == 15),
                                 skip_group_check=True, inc=(k == 15))
                    jr, (rr, ee), rfree = re_rot.acquire()
                    W("scalar", tmm, rfree)
                    ts_r = OP("scalar", "activation", rr, pp, AF.Relu, inc=True)
                    ts_e = OP("scalar", "activation", ee, pp, AF.Exp, inc=True)
                    W("vector", ts_r, ts_e, qkv_read_tok)
                    tk = OP("vector", "scalar_tensor_tensor",
                            abr(C_KF + (cc * 8 + dt * 4) * 128, 512), ee, 1.0, rr,
                            ALU.min, ALU.add, inc=True)
                    re_rot.use(jr, tk)
                    proj.use(jp, ts_r)
                    proj.use(jp, ts_e)
                    kf_w[cc * 2 + dt] = tk
                for dt in range(2):
                    jp, pp, pfree = proj.acquire()
                    W("tensor", pfree, t_oner, t_bv_c)
                    OP("tensor", "matmul", pp,
                       ab[0:1, S_ONER:S_ONER + 128],
                       ab[0:1, S_BV + dt * 512:S_BV + dt * 512 + 512],
                       start=True, stop=False, skip_group_check=True)
                    tmm = None
                    for k in range(16):
                        W("tensor", w_tok[("v", k, dt)])
                        tmm = OP("tensor", "matmul", pp,
                                 abr(xbase + k * 512 + cc * 128, 128),
                                 abr(B_WV + k * 1024 + dt * 512, 512),
                                 start=False, stop=(k == 15),
                                 skip_group_check=True, inc=(k == 15))
                    W("vector", tmm, t_vcone, qkv_read_tok)
                    vbase = C_VC + (cc * 8 + dt * 4) * 129
                    vview = abr(vbase, 4 * 129).rearrange("p (s c) -> p s c", c=129)[:, :, 0:128]
                    tv = OP("vector", "tensor_copy", vview,
                            pp.rearrange("p (s c) -> p s c", c=128), inc=True)
                    proj.use(jp, tv)
                    vc_w[cc * 2 + dt] = tv

                if do_rec < 1:
                    continue
                W("vector", kv_mm_tok, snap_read_tok)
                if gc == 0:
                    tsn = [OP("vector", "memset", snap_ap, 0.0, inc=True)]
                else:
                    tsn = [
                        OP("vector", "tensor_copy", abr(D_SNAP, 387), ps_state[:, 0:387], inc=True),
                        OP("vector", "tensor_copy", abr(D_SNAP + 387, 387), ps_state[:, 512:899], inc=True),
                        OP("vector", "tensor_copy", abr(D_SNAP + 774, 258), ps_state[:, 1024:1282], inc=True),
                    ]
                snap_read_tok = []
                kv_mm_tok = []

                for h in range(HL):
                    W("tensor", tsn, kf_w[cc * 2 + h // 4], vc_w[cc * 2 + h // 4])
                    tkv = OP("tensor", "matmul", state_slot(h),
                             abr(C_KF + (cc * 8 + h) * 128, 128),
                             abr(C_VC + (cc * 8 + h) * 129, 129),
                             start=(gc == 0 and h in (0, 3, 6)),
                             stop=(gc == NCH - 1 and h in (2, 5, 7)),
                             skip_group_check=True, inc=True)
                    kv_mm_tok.append(tkv)

                if do_rec < 2:
                    continue
                ja, aob, aofree = aotm.acquire()
                norm_toks = []
                for h in range(HL):
                    jt, kps, ktfree = kfT.acquire()
                    W("tensor", ktfree, kf_w[cc * 2 + h // 4], t_eye_c)
                    ttr = OP("tensor", "transpose", kps,
                             abr(C_KF + (cc * 8 + h) * 128, 128), eye_b[:, :], inc=True)
                    jf, kfb, kffree = kff.acquire()
                    W("vector", ttr, kffree)
                    tkc = OP("vector", "tensor_copy", abr(kfb, 128), kps, inc=True)
                    kfT.use(jt, tkc)
                    jA, Aps, Afree = aT.acquire()
                    W("tensor", Afree, tkc, qf_writers[h])
                    tA = OP("tensor", "matmul", Aps,
                            abr(kfb, 128),
                            abr(C_QF + h * 512 + cc * 128, 128),
                            start=True, stop=True, skip_group_check=True, inc=True)
                    kff.use(jf, tA)
                    jm, Ab, Afree2 = asb.acquire()
                    W("vector", tA, Afree2, t_mask_l)
                    tAm = OP("vector", "tensor_tensor", abr(Ab, 128), Aps, mask_f[:, :], ALU.mult, inc=True)
                    aT.use(jA, tAm)
                    if do_rec < 3:
                        norm_toks.append(tAm)
                        continue
                    jo, ops_, ofree = outp.acquire()
                    W("tensor", ofree, tAm, tsn, qf_writers[h], vc_w[cc * 2 + h // 4])
                    OP("tensor", "matmul", ops_,
                       abr(C_QF + h * 512 + cc * 128, 128), snap_slot(h),
                       start=True, stop=False, skip_group_check=True)
                    tio = OP("tensor", "matmul", ops_,
                             abr(Ab, 128),
                             abr(C_VC + (cc * 8 + h) * 129, 129),
                             start=False, stop=True, skip_group_check=True, inc=True)
                    snap_read_tok.append(tio)
                    asb.use(jm, tio)
                    jz, (zb, rzb), zfree = zscr.acquire()
                    W("vector", tio, zfree)
                    if h == 0:
                        W("vector", aofree)
                    OP("vector", "tensor_scalar_add", zb, ops_[:, 128:129], EPS_ATT)
                    OP("vector", "reciprocal", rzb, zb)
                    tn = OP("vector", "tensor_scalar_mul",
                            abr(aob + h * 128, 128), ops_[:, 0:128], rzb, inc=True)
                    zscr.use(jz, tn)
                    outp.use(jo, tn)
                    norm_toks.append(tn)

                if do_rec < 4:
                    continue
                jaf, afb, affree = aofm.acquire()
                wj = gc // 16
                lcol = (gc % 16) * 128
                for h in range(HL):
                    jt, kps, ktfree = kfT.acquire()
                    W("tensor", ktfree, norm_toks[h], t_eye_c)
                    ttr = OP("tensor", "transpose", kps, abr(aob + h * 128, 128), eye_b[:, :], inc=True)
                    aotm.use(ja, ttr)
                    W("vector", ttr)
                    if h == 0:
                        W("vector", affree)
                    tac = OP("vector", "tensor_copy", abr(afb + h * 128, 128), kps, inc=True)
                    kfT.use(jt, tac)
                    if do_rec < 5:
                        aofm.use(jaf, tac)
                        continue
                    W("gpsimd", tac)
                    td = DMA("gpsimd", "A%d" % jaf,
                             cc_in.ap()[wj * 1024 + h * 128:wj * 1024 + (h + 1) * 128, lcol:lcol + 128],
                             abr(afb + h * 128, 128))
                    aofm.use(jaf, td)
                    if h == HL - 1:
                        REG([abr(afb, 1024)], [], td, "gpsimd")

            last_t = ("T", cnt["T"])
            xbuf.use(jx, last_t)
            qkv_read_tok = [last_t]

        _wait("gpsimd", "A0", cnt["A0"])
        _wait("gpsimd", "A1", cnt["A1"])
        if do_cc:
            cnt["C"] += 1
            REG([cc_in.ap()], [cc_out.ap()], ("C", cnt["C"]), "gpsimd")
            ops["gpsimd"].append(lambda e: e.collective_compute(
                "AllGather", ALU.bypass,
                replica_groups=[[0, 1], [2, 3], [4, 5], [6, 7]],
                ins=[cc_in.ap()], outs=[cc_out.ap()],
            ).then_inc(sems["C"], 1))
        else:
            nrows = 1024 if ngroups <= 4 else 2048
            ncols = min(ngroups, 4) * 512
            cnt["C"] += 16
            REG([cc_in.ap()], [cc_out.ap()], ("C", cnt["C"]), "gpsimd")
            ops["gpsimd"].append(lambda e, nr=nrows, nc_=ncols: e.dma_start(
                out=cc_out.ap()[0:nr, 0:nc_], in_=cc_in.ap()[0:nr, 0:nc_]).then_inc(sems["C"], 16))
            cnt["C"] += 16
            ops["gpsimd"].append(lambda e, nr=nrows, nc_=ncols: e.dma_start(
                out=cc_out.ap()[DM:DM + nr, 0:nc_], in_=cc_in.ap()[0:nr, 0:nc_]).then_inc(sems["C"], 16))
        t_cc0 = ("C", cnt["C"])
        t_cc = ("C", cnt["C"])
        attn_end = [("T", cnt["T"]), ("V", cnt["V"]), ("S", cnt["S"])]

        if debug & 1:
            _dbsc = [D_KFF0, D_AOF0]
            _dfsc = [F_RE, F_RE + 512]
            _prevc = [None, None]
            _prevs = [None, None]
            for _r in range(32):
                for _c in range(4):
                    _j = (_r * 4 + _c) % 2
                    W("gpsimd", t_cc, attn_end, _prevc[_j])
                    _tl = DMA("gpsimd", "DG%d" % _j, abr(_dbsc[_j], 512),
                              cc_out.ap()[_r * 128:(_r + 1) * 128, _c * 512:(_c + 1) * 512])
                    W("vector", _tl, _prevs[_j])
                    _tc = OP("vector", "tensor_copy", afr(_dfsc[_j], 512), abr(_dbsc[_j], 512), inc=True)
                    _prevc[_j] = _tc
                    W("gpsimd", _tc)
                    _ts = DMA("gpsimd", "DG%d" % _j, dbg["ao"][_r * 128:(_r + 1) * 128, _c * 512:(_c + 1) * 512],
                              afr(_dfsc[_j], 512))
                    _prevs[_j] = _ts


        if phases == "attn":
            W("vector", attn_end)
            tz = OP("vector", "memset", afr(F_RE, 512), 0.0, inc=True)
            W("sync", tz, t_cc)
            for m_ in range(16):
                for tq_ in range(4):
                    DMA("sync", "O%d" % (tq_ % 2), out_d[m_ * 128:(m_ + 1) * 128, tq_ * 512:(tq_ + 1) * 512], afr(F_RE, 512))

        if phases == "full":
            def ao_rows(j, k):
                r0 = (k // 8) * 2048 + j * 1024 + (k % 8) * 128
                return cc_out.ap()[r0:r0 + 128, :]

            wo_tok = {}
            first_post = True
            for k in range(16):
                for half in range(4):
                    j, st, free = wst.acquire()
                    if first_post:
                        W("sync", attn_end)
                        first_post = False
                    W("sync", free)
                    tl = DMA("sync", "W%d" % j, st, wo_d[k * 128:(k + 1) * 128, half * 512:(half + 1) * 512])
                    W("vector", tl)
                    if k == 0 and half == 0:
                        W("vector", attn_end)
                    tc = OP("vector", "tensor_copy", abr(C_WO + k * 2048 + half * 512, 512), st, inc=True)
                    wst.use(j, tc)
                    wo_tok[(k, half)] = tc

            aopair = Rot([(AO_W0A, AO_W1A), (AO_W0B, AO_W1B)])
            xst2 = Rot([afr(F_XST, 512), afr(F_XST + 512, 512)])
            sqr = Rot([D_SQ0, D_SQ1])
            outr = Rot([afr(F_RE, 512), afr(F_RE + 512, 512)])
            stats_s = ps_small[0:1, 0:512]
            stats_q = ps_small[0:1, 512:1024]
            bc_r = ps_state[:, 0:512]
            bc_mr = ps_state[:, 512:1024]

            h_tok = {}
            prev_b1_tmm = None
            for tt in range(4):
                ao_tok = {}
                x_tok = {}
                for k in range(16):
                    jj, (w0c, w1c), pfree = aopair.acquire()
                    W("sync", pfree, t_cc)
                    if tt == 0 and k == 0:
                        W("sync", attn_end)
                    t0l = DMA("sync", "P%d" % (2 * jj), abr(w0c, 512), ao_rows(0, k)[:, tt * 512:(tt + 1) * 512])
                    t1l = DMA("sync", "P%d" % (2 * jj + 1), abr(w1c, 512), ao_rows(1, k)[:, tt * 512:(tt + 1) * 512])
                    W("vector", t0l, t1l, t_sel_l, prev_b1_tmm)
                    OP("vector", "tensor_scalar_mul", abr(B_AO + k * 512, 512), abr(w0c, 512), sel_sb[:, 0:1])
                    tcmb = OP("vector", "scalar_tensor_tensor", abr(B_AO + k * 512, 512),
                              abr(w1c, 512), sel_sb[:, 1:2], abr(B_AO + k * 512, 512),
                              ALU.mult, ALU.add, inc=True)
                    aopair.use(jj, tcmb)
                    ao_tok[k] = tcmb
                    js, st, sfree = xst2.acquire()
                    W("sync", sfree)
                    tl = DMA("sync", "X%d" % js, st, xR[k * 128:(k + 1) * 128, tt * 512:(tt + 1) * 512])
                    W("vector", tl)
                    tc = OP("vector", "tensor_copy", abr(B_XS + k * 512, 512), st, inc=True)
                    xst2.use(js, tc)
                    x_tok[k] = tc

                stat_mm_s = stat_mm_q = None
                s1_toks = []
                for m in range(16):
                    jp, pp, pfree = proj.acquire()
                    W("tensor", pfree, x_tok[m], t_eye_c)
                    OP("tensor", "matmul", pp, eye_b[:, :], abr(B_XS + m * 512, 512),
                       start=True, stop=False, skip_group_check=True)
                    tmm = None
                    for k in range(16):
                        W("tensor", ao_tok[k], wo_tok[(k, m // 4)])
                        tmm = OP("tensor", "matmul", pp,
                                 abr(C_WO + k * 2048 + m * 128, 128),
                                 abr(B_AO + k * 512, 512),
                                 start=False, stop=(k == 15),
                                 skip_group_check=True, inc=(k == 15))
                    if m == 15:
                        prev_b1_tmm = tmm
                    W("scalar", tmm)
                    ts1 = OP("scalar", "activation", abr(D_S1 + m * 512, 512), pp, AF.Identity,
                             bias=bo_sb[:, m:m + 1], inc=True)
                    jq, sqb, sqfree = sqr.acquire()
                    W("scalar", sqfree)
                    tsq = OP("scalar", "activation", abr(sqb, 512), pp, AF.Square,
                             bias=bo_sb[:, m:m + 1], inc=True)
                    proj.use(jp, ts1)
                    proj.use(jp, tsq)
                    W("tensor", ts1, tsq, t_onec)
                    stat_mm_s = OP("tensor", "matmul", stats_s, onec_b[:, :], abr(D_S1 + m * 512, 512),
                                   start=(m == 0), stop=(m == 15), skip_group_check=True, inc=True)
                    stat_mm_q = OP("tensor", "matmul", stats_q, onec_b[:, :], abr(sqb, 512),
                                   start=(m == 0), stop=(m == 15), skip_group_check=True, inc=True)
                    sqr.use(jq, stat_mm_q)
                    s1_toks.append(ts1)

                W("scalar", stat_mm_s, stat_mm_q)
                OP("scalar", "copy", af[0:1, ST0:ST0 + 512], stats_s)
                OP("scalar", "copy", af[0:1, ST1:ST1 + 512], stats_q)
                OP("vector", "tensor_scalar_mul", af[0:1, ST0:ST0 + 512], af[0:1, ST0:ST0 + 512], 1.0 / DM)
                OP("vector", "tensor_scalar_mul", af[0:1, ST1:ST1 + 512], af[0:1, ST1:ST1 + 512], 1.0 / DM)
                OP("vector", "tensor_tensor", af[0:1, ST2:ST2 + 512], af[0:1, ST0:ST0 + 512],
                   af[0:1, ST0:ST0 + 512], ALU.mult)
                tvv = OP("vector", "tensor_tensor", af[0:1, ST1:ST1 + 512], af[0:1, ST1:ST1 + 512],
                         af[0:1, ST2:ST2 + 512], ALU.subtract, inc=True)
                W("scalar", tvv, t_eps)
                tsv = OP("scalar", "activation", af[0:1, ST1:ST1 + 512], af[0:1, ST1:ST1 + 512],
                         AF.Sqrt, bias=af[0:1, F_Z + 4:F_Z + 5], inc=True)
                W("vector", tsv)
                OP("vector", "reciprocal", af[0:1, ST1:ST1 + 512], af[0:1, ST1:ST1 + 512])
                tmr = OP("vector", "tensor_tensor", af[0:1, ST2:ST2 + 512], af[0:1, ST0:ST0 + 512],
                         af[0:1, ST1:ST1 + 512], ALU.mult, inc=True)
                W("tensor", tmr, t_mask_l)
                tb1 = OP("tensor", "matmul", bc_r, mask_f[0:1, 0:128], af[0:1, ST1:ST1 + 512],
                         start=True, stop=True, skip_group_check=True, inc=True)
                tb2 = OP("tensor", "matmul", bc_mr, mask_f[0:1, 0:128], af[0:1, ST2:ST2 + 512],
                         start=True, stop=True, skip_group_check=True, inc=True)
                for m in range(16):
                    W("vector", tb1, tb2, s1_toks[m])
                    OP("vector", "tensor_tensor", abr(B_XS + m * 512, 512),
                       abr(D_S1 + m * 512, 512), bc_r, ALU.mult)
                    th = OP("vector", "tensor_tensor", abr(A_H + m * 2048 + tt * 512, 512),
                            abr(B_XS + m * 512, 512), bc_mr, ALU.subtract, inc=True)
                    h_tok[(m, tt)] = th

            if debug & 2:
                _prevs = [None, None]
                for m in range(16):
                    for _c in range(4):
                        _j = _c % 2
                        W("vector", [h_tok[(m, q)] for q in range(4)], _prevs[_j])
                        _tc = OP("vector", "tensor_copy", afr(F_RE + _j * 512, 512),
                                 abr(A_H + m * 2048 + _c * 512, 512), inc=True)
                        W("sync", _tc)
                        _ts = DMA("sync", "DG%d" % _j, dbg["h"][:, m * 2048 + _c * 512:m * 2048 + (_c + 1) * 512],
                                  afr(F_RE + _j * 512, 512))
                        _prevs[_j] = _ts

            gate_tok = {}
            prev_texp = None
            for cc in range(16):
                jp, pp, pfree = proj.acquire()
                pp8 = pp[:, 0:8]
                W("tensor", pfree, t_oner, t_rb_c, t_rw_c,
                  [h_tok[(k, cc // 4)] for k in range(16)])
                OP("tensor", "matmul", pp8, ab[0:1, S_ONER:S_ONER + 128], ab[0:1, S_RB:S_RB + 8],
                   start=True, stop=False, skip_group_check=True)
                tmm = None
                for k in range(16):
                    tmm = OP("tensor", "matmul", pp8,
                             abr(A_H + k * 2048 + cc * 128, 128),
                             rw_b[:, k * 8:(k + 1) * 8],
                             start=False, stop=(k == 15), skip_group_check=True, inc=(k == 15))
                W("vector", tmm, prev_texp)
                m1n = af[:, F_Z:F_Z + 1]
                tm1 = OP("vector", "reduce_max", m1n, pp8, axis=AX.X, negate=True, inc=True)
                W("scalar", tm1)
                texp = OP("scalar", "activation", afr(ST0, 8), pp8, AF.Exp, bias=m1n, inc=True)
                prev_texp = texp
                proj.use(jp, tm1)
                proj.use(jp, texp)
                t_sb = afr(ST0, 8)
                W("vector", texp)
                e1 = af[:, F_Z + 1:F_Z + 2]
                OP("vector", "reduce_max", e1, t_sb, axis=AX.X)
                lt = afr(ST0 + 8, 8)
                OP("vector", "tensor_single_scalar", lt, t_sb, e1, ALU.is_lt)
                tmsk = afr(ST0 + 16, 8)
                OP("vector", "tensor_tensor", tmsk, t_sb, lt, ALU.mult)
                e2 = af[:, F_Z + 2:F_Z + 3]
                OP("vector", "reduce_max", e2, tmsk, axis=AX.X)
                keep = afr(ST0 + 8, 8)
                OP("vector", "tensor_single_scalar", keep, t_sb, e2, ALU.is_ge)
                den = af[:, F_Z + 3:F_Z + 4]
                OP("vector", "tensor_tensor", den, e1, e2, ALU.add)
                OP("vector", "reciprocal", den, den)
                OP("vector", "tensor_tensor", t_sb, t_sb, keep, ALU.mult)
                tg = OP("vector", "tensor_scalar_mul", gates_sb[:, cc * 8:cc * 8 + 8], t_sb, den, inc=True)
                gate_tok[cc] = tg

            if debug & 4:
                W("sync", [gate_tok[c] for c in range(16)])
                DMA("sync", "DG0", dbg["gates"], gates_sb[:, :])

            web = Rot([B_WE0, B_WE1])
            glr = Rot([D_GL0, D_GL1])
            acc_tok = {}
            prev_teb = None
            prev_e_last_mm = None
            for e in range(8):
                W("gpsimd", gate_tok[15] if e == 0 else None, prev_teb)
                teb_l1 = DMA("gpsimd", "EB", af[0:1, F_XST:F_XST + 1024], eb_d[0:1, e * 2048:e * 2048 + 1024])
                W("vector", teb_l1, prev_e_last_mm)
                teb1 = OP("vector", "tensor_copy", ab[0:1, S_EBST:S_EBST + 1024],
                          af[0:1, F_XST:F_XST + 1024], inc=True)
                W("gpsimd", teb1)
                teb_l2 = DMA("gpsimd", "EB", af[0:1, F_XST:F_XST + 1024], eb_d[0:1, e * 2048 + 1024:(e + 1) * 2048])
                W("vector", teb_l2)
                teb = OP("vector", "tensor_copy", ab[0:1, S_EBST + 1024:S_EBST + 2048],
                         af[0:1, F_XST:F_XST + 1024], inc=True)
                prev_teb = teb
                teb = [teb1, teb]
                for dt in range(4):
                    jw, wbase, wfree = web.acquire()
                    wk_toks = []
                    for k in range(16):
                        js, st, sfree = wst.acquire()
                        W("sync", sfree)
                        if k == 0:
                            W("sync", wfree)
                        tl = DMA("sync", "W%d" % js, st, ew_d[e, k * 128:(k + 1) * 128, dt * 512:(dt + 1) * 512])
                        W("vector", tl)
                        tc = OP("vector", "tensor_copy", abr(wbase + k * 512, 512), st, inc=True)
                        wst.use(js, tc)
                        wk_toks.append(tc)
                    for cc in range(16):
                        jp, pp, pfree = proj.acquire()
                        W("tensor", pfree, teb, t_oner, wk_toks)
                        OP("tensor", "matmul", pp, ab[0:1, S_ONER:S_ONER + 128],
                           ab[0:1, S_EBST + dt * 512:S_EBST + dt * 512 + 512],
                           start=True, stop=False, skip_group_check=True)
                        tmm = None
                        for k in range(16):
                            tmm = OP("tensor", "matmul", pp,
                                     abr(A_H + k * 2048 + cc * 128, 128),
                                     abr(wbase + k * 512, 512),
                                     start=False, stop=(k == 15), skip_group_check=True, inc=(k == 15))
                        web.use(jw, tmm)
                        prev_e_last_mm = tmm
                        jg, glb, glfree = glr.acquire()
                        W("scalar", tmm, glfree)
                        tgl = OP("scalar", "activation", abr(glb, 512), pp, AF.Gelu_apprx_tanh, inc=True)
                        proj.use(jp, tgl)
                        W("vector", tgl, gate_tok[cc], acc_tok.get((cc, dt)))
                        gcol = gates_sb[:, cc * 8 + e:cc * 8 + e + 1]
                        dst = abr(C_ACC + cc * 2048 + dt * 512, 512)
                        if e == 0:
                            ta = OP("vector", "tensor_scalar_mul", dst, abr(glb, 512), gcol, inc=True)
                        else:
                            ta = OP("vector", "scalar_tensor_tensor", dst, abr(glb, 512), gcol, dst,
                                    ALU.mult, ALU.add, inc=True)
                        glr.use(jg, ta)
                        acc_tok[(cc, dt)] = ta

            if debug & 8:
                _prevs = [None, None]
                for _m in range(16):
                    for _c in range(4):
                        _j = _c % 2
                        W("vector", acc_tok[(_m, _c)], _prevs[_j])
                        _tc = OP("vector", "tensor_copy", afr(F_RE + _j * 512, 512),
                                 abr(C_ACC + _m * 2048 + _c * 512, 512), inc=True)
                        W("sync", _tc)
                        _ts = DMA("sync", "DG%d" % _j, dbg["acc"][:, _m * 2048 + _c * 512:_m * 2048 + (_c + 1) * 512],
                                  afr(F_RE + _j * 512, 512))
                        _prevs[_j] = _ts

            owr = Rot([D_OW0, D_OW1])
            cfT = Rot([ps_bf[:, 0:128], ps_bf[:, 128:256], ps_bf[:, 256:384], ps_bf[:, 384:512]])
            comb_tok = {}
            for tt in range(4):
                for d in range(16):
                    for ccl in range(4):
                        cc = tt * 4 + ccl
                        jt, kps, ktfree = cfT.acquire()
                        W("tensor", ktfree, acc_tok[(cc, d // 4)], t_eye_c)
                        ttr = OP("tensor", "transpose", kps,
                                 abr(C_ACC + cc * 2048 + d * 128, 128), eye_b[:, :], inc=True)
                        W("vector", ttr)
                        tcp = OP("vector", "tensor_copy", abr(B_COMB + d * 512 + ccl * 128, 128), kps, inc=True)
                        cfT.use(jt, tcp)
                        comb_tok[(d, ccl)] = tcp
                comb_all = [comb_tok[(d, c)] for d in range(16) for c in range(4)]
                stat_mm_s = stat_mm_q = None
                s2_toks = []
                for m in range(16):
                    jw, owb, owfree = owr.acquire()
                    ow_toks = []
                    for piece in range(4):
                        js, st, sfree = wst.acquire()
                        W("sync", sfree)
                        if piece == 0:
                            W("sync", owfree)
                        tl = DMA("sync", "W%d" % js, st.rearrange("p (k c) -> p k c", c=128),
                                 ow_d[piece * 512:(piece + 1) * 512, m * 128:(m + 1) * 128].rearrange(
                                     "(k p) c -> p k c", p=128))
                        W("vector", tl)
                        tc = OP("vector", "tensor_copy", abr(owb + piece * 512, 512), st, inc=True)
                        wst.use(js, tc)
                        ow_toks.append(tc)
                    jp, pp, pfree = proj.acquire()
                    W("tensor", pfree, t_eye_c, h_tok[(m, tt)], ow_toks, comb_all)
                    OP("tensor", "matmul", pp, eye_b[:, :], abr(A_H + m * 2048 + tt * 512, 512),
                       start=True, stop=False, skip_group_check=True)
                    tmm = None
                    for k in range(16):
                        tmm = OP("tensor", "matmul", pp,
                                 abr(owb + k * 128, 128),
                                 abr(B_COMB + k * 512, 512),
                                 start=False, stop=(k == 15), skip_group_check=True, inc=(k == 15))
                    owr.use(jw, tmm)
                    W("scalar", tmm)
                    ts2 = OP("scalar", "activation", abr(B_S2 + m * 512, 512), pp, AF.Identity,
                             bias=ob_sb[:, m:m + 1], inc=True)
                    jq, sqb, sqfree = sqr.acquire()
                    W("scalar", sqfree)
                    tsq = OP("scalar", "activation", abr(sqb, 512), pp, AF.Square,
                             bias=ob_sb[:, m:m + 1], inc=True)
                    proj.use(jp, ts2)
                    proj.use(jp, tsq)
                    W("tensor", ts2, tsq, t_onec)
                    stat_mm_s = OP("tensor", "matmul", stats_s, onec_b[:, :], abr(B_S2 + m * 512, 512),
                                   start=(m == 0), stop=(m == 15), skip_group_check=True, inc=True)
                    stat_mm_q = OP("tensor", "matmul", stats_q, onec_b[:, :], abr(sqb, 512),
                                   start=(m == 0), stop=(m == 15), skip_group_check=True, inc=True)
                    sqr.use(jq, stat_mm_q)
                    s2_toks.append(ts2)

                W("scalar", stat_mm_s, stat_mm_q)
                OP("scalar", "copy", af[0:1, ST0:ST0 + 512], stats_s)
                OP("scalar", "copy", af[0:1, ST1:ST1 + 512], stats_q)
                OP("vector", "tensor_scalar_mul", af[0:1, ST0:ST0 + 512], af[0:1, ST0:ST0 + 512], 1.0 / DM)
                OP("vector", "tensor_scalar_mul", af[0:1, ST1:ST1 + 512], af[0:1, ST1:ST1 + 512], 1.0 / DM)
                OP("vector", "tensor_tensor", af[0:1, ST2:ST2 + 512], af[0:1, ST0:ST0 + 512],
                   af[0:1, ST0:ST0 + 512], ALU.mult)
                tvv = OP("vector", "tensor_tensor", af[0:1, ST1:ST1 + 512], af[0:1, ST1:ST1 + 512],
                         af[0:1, ST2:ST2 + 512], ALU.subtract, inc=True)
                W("scalar", tvv, t_eps)
                tsv = OP("scalar", "activation", af[0:1, ST1:ST1 + 512], af[0:1, ST1:ST1 + 512],
                         AF.Sqrt, bias=af[0:1, F_Z + 4:F_Z + 5], inc=True)
                W("vector", tsv)
                OP("vector", "reciprocal", af[0:1, ST1:ST1 + 512], af[0:1, ST1:ST1 + 512])
                tmr = OP("vector", "tensor_tensor", af[0:1, ST2:ST2 + 512], af[0:1, ST0:ST0 + 512],
                         af[0:1, ST1:ST1 + 512], ALU.mult, inc=True)
                W("tensor", tmr, t_mask_l)
                tb1 = OP("tensor", "matmul", bc_r, mask_f[0:1, 0:128], af[0:1, ST1:ST1 + 512],
                         start=True, stop=True, skip_group_check=True, inc=True)
                tb2 = OP("tensor", "matmul", bc_mr, mask_f[0:1, 0:128], af[0:1, ST2:ST2 + 512],
                         start=True, stop=True, skip_group_check=True, inc=True)
                for m in range(16):
                    jo, ob_, ofree = outr.acquire()
                    W("vector", tb1, tb2, s2_toks[m], ofree)
                    OP("vector", "tensor_tensor", ob_, abr(B_S2 + m * 512, 512), bc_r, ALU.mult)
                    tf = OP("vector", "tensor_tensor", ob_, ob_, bc_mr, ALU.subtract, inc=True)
                    W("sync", tf)
                    tso = DMA("sync", "O%d" % jo, out_d[m * 128:(m + 1) * 128, tt * 512:(tt + 1) * 512], ob_)
                    outr.use(jo, tso)


        _wait("sync", "O0", cnt["O0"])
        _wait("sync", "O1", cnt["O1"])

        with nc.Block() as block:
            @block.sync
            def _(e):
                for f in ops["sync"]:
                    f(e)

            @block.gpsimd
            def _(e):
                for f in ops["gpsimd"]:
                    f(e)

            @block.tensor
            def _(e):
                for f in ops["tensor"]:
                    f(e)

            @block.vector
            def _(e):
                for f in ops["vector"]:
                    f(e)

            @block.scalar
            def _(e):
                for f in ops["scalar"]:
                    f(e)

    return nc


def make_in_maps(inputs):
    x = np.asarray(inputs["L0_attn_x"], np.float32)
    Wq = np.asarray(inputs["L0_attn_Wq"], np.float32)
    bq = np.asarray(inputs["L0_attn_bq"], np.float32)
    Wk = np.asarray(inputs["L0_attn_Wk"], np.float32)
    bk = np.asarray(inputs["L0_attn_bk"], np.float32)
    Wv = np.asarray(inputs["L0_attn_Wv"], np.float32)
    bv = np.asarray(inputs["L0_attn_bv"], np.float32)
    Wo = np.ascontiguousarray(np.asarray(inputs["L0_attn_Wo"], np.float32))
    bo = np.asarray(inputs["L0_attn_bo"], np.float32)
    rw = np.asarray(inputs["L0_ffn_router_w"], np.float32)
    rb = np.asarray(inputs["L0_ffn_router_b"], np.float32)
    ew = np.ascontiguousarray(np.asarray(inputs["L0_ffn_expert_w"], np.float32))
    eb = np.asarray(inputs["L0_ffn_expert_b"], np.float32)
    ow = np.ascontiguousarray(np.asarray(inputs["L0_ffn_out_w"], np.float32))
    ob = np.asarray(inputs["L0_ffn_out_b"], np.float32)

    eye = np.eye(128, dtype=np.float32)
    maskup = np.ascontiguousarray(np.triu(np.ones((128, 128), np.float32)))
    rw_t = np.ascontiguousarray(rw.reshape(16, 128, 8).transpose(1, 0, 2).reshape(128, 128))
    bo_col = np.ascontiguousarray(bo.reshape(16, 128).T)
    ob_col = np.ascontiguousarray(ob.reshape(16, 128).T)
    eb_row = np.ascontiguousarray(eb.reshape(1, 8 * 2048))
    rb_row = np.ascontiguousarray(rb.reshape(1, 8))

    in_maps = []
    for core in range(8):
        b, hh = core // 2, core % 2
        cols = slice(hh * 1024, (hh + 1) * 1024)
        sel = np.zeros((128, 2), np.float32)
        sel[:, hh] = 1.0
        in_maps.append(dict(
            xT=np.ascontiguousarray(x[b].T),
            xR=np.ascontiguousarray(x[b, hh * 2048:(hh + 1) * 2048].T),
            wq=np.ascontiguousarray(Wq[:, cols]),
            wk=np.ascontiguousarray(Wk[:, cols]),
            wv=np.ascontiguousarray(Wv[:, cols]),
            bq_col=np.ascontiguousarray(bq[cols].reshape(8, 128).T),
            bk_row=np.ascontiguousarray(bk[cols].reshape(1, 1024)),
            bv_row=np.ascontiguousarray(bv[cols].reshape(1, 1024)),
            wo=Wo,
            bo_col=bo_col,
            rw_t=rw_t,
            rb_row=rb_row,
            ew=ew,
            eb_row=eb_row,
            ow=ow,
            ob_col=ob_col,
            maskup=maskup,
            eye=eye,
            sel_col=sel,
        ))
    return in_maps


def kernel(**inputs):
    in_maps = make_in_maps(inputs)
    nc = build(debug=0)
    res = run_bass_kernel_spmd(nc, in_maps, core_ids=list(range(8)))
    out = np.empty((4, 4096, 2048), np.float32)
    for core in range(8):
        b, hh = core // 2, core % 2
        out[b, hh * 2048:(hh + 1) * 2048, :] = res.results[core]["out_fm"].T
    return out


# version 08
# speedup vs baseline: 1.0322x; 1.0322x over previous
import numpy as np
from contextlib import ExitStack
import concourse.bass as bass
import concourse.mybir as mybir
from concourse.bass_utils import run_bass_kernel_spmd

F32 = mybir.dt.float32
BF16 = mybir.dt.bfloat16
AF = mybir.ActivationFunctionType
ALU = mybir.AluOpType
AX = mybir.AxisListType

DM = 2048
SEQ = 4096
TLOC = 2048
HL = 8
NCH = 32
GRP = 4
NG = NCH // GRP
EPS_ATT = 1e-6
EPS_LN = 1e-5


def build(debug=0, phases='full', ngroups=NG, do_rec=5, do_cc=True,
          PSUM_ROUND=True, WO_PRELOAD=False):
    nc = bass.Bass("TRN2", target_bir_lowering=False)

    xT = nc.dram_tensor("xT", [DM, SEQ], F32, kind="ExternalInput").ap()
    xR = nc.dram_tensor("xR", [DM, TLOC], F32, kind="ExternalInput").ap()
    wq_d = nc.dram_tensor("wq", [DM, 1024], F32, kind="ExternalInput").ap()
    wk_d = nc.dram_tensor("wk", [DM, 1024], F32, kind="ExternalInput").ap()
    wv_d = nc.dram_tensor("wv", [DM, 1024], F32, kind="ExternalInput").ap()
    bq_d = nc.dram_tensor("bq_col", [128, 8], F32, kind="ExternalInput").ap()
    bk_d = nc.dram_tensor("bk_row", [1, 1024], F32, kind="ExternalInput").ap()
    bv_d = nc.dram_tensor("bv_row", [1, 1024], F32, kind="ExternalInput").ap()
    wo_d = nc.dram_tensor("wo", [DM, DM], F32, kind="ExternalInput").ap()
    bo_d = nc.dram_tensor("bo_col", [128, 16], F32, kind="ExternalInput").ap()
    rw_d = nc.dram_tensor("rw_t", [128, 128], F32, kind="ExternalInput").ap()
    rb_d = nc.dram_tensor("rb_row", [1, 8], F32, kind="ExternalInput").ap()
    ew_d = nc.dram_tensor("ew", [8, DM, DM], F32, kind="ExternalInput").ap()
    eb_d = nc.dram_tensor("eb_row", [1, 8 * DM], F32, kind="ExternalInput").ap()
    ow_d = nc.dram_tensor("ow", [DM, DM], F32, kind="ExternalInput").ap()
    ob_d = nc.dram_tensor("ob_col", [128, 16], F32, kind="ExternalInput").ap()
    mask_d = nc.dram_tensor("maskup", [128, 128], F32, kind="ExternalInput").ap()
    eye_d = nc.dram_tensor("eye", [128, 128], F32, kind="ExternalInput").ap()
    sel_d = nc.dram_tensor("sel_col", [128, 2], F32, kind="ExternalInput").ap()
    out_d = nc.dram_tensor("out_fm", [DM, TLOC], F32, kind="ExternalOutput").ap()

    cc_in = nc.dram_tensor("cc_in", [DM, TLOC], BF16, kind="Internal")
    cc_out = nc.dram_tensor("cc_out", [2 * DM, TLOC], BF16, kind="Internal")

    dbg = {}
    if debug & 1:
        dbg["ao"] = nc.dram_tensor("dbg_ao", [2 * DM, TLOC], F32, kind="ExternalOutput").ap()
    if debug & 2:
        dbg["h"] = nc.dram_tensor("dbg_h", [128, 16 * TLOC], F32, kind="ExternalOutput").ap()
    if debug & 4:
        dbg["gates"] = nc.dram_tensor("dbg_gates", [128, 16 * 8], F32, kind="ExternalOutput").ap()
    if debug & 8:
        dbg["acc"] = nc.dram_tensor("dbg_acc", [128, 16 * DM], F32, kind="ExternalOutput").ap()

    with ExitStack() as ctx:
        ab = ctx.enter_context(nc.sbuf_tensor("ab", [128, 93320], BF16))
        af = ctx.enter_context(nc.sbuf_tensor("af", [128, 5648], F32))
        eye_b = ctx.enter_context(nc.sbuf_tensor("eye_b", [128, 128], BF16))
        mask_f = ctx.enter_context(nc.sbuf_tensor("mask_f", [128, 128], F32))
        rw_b = ctx.enter_context(nc.sbuf_tensor("rw_b", [128, 128], BF16))
        gates_sb = ctx.enter_context(nc.sbuf_tensor("gates_sb", [128, 128], F32))
        bq_sb = ctx.enter_context(nc.sbuf_tensor("bq_sb", [128, 8], F32))
        bo_sb = ctx.enter_context(nc.sbuf_tensor("bo_sb", [128, 16], F32))
        ob_sb = ctx.enter_context(nc.sbuf_tensor("ob_sb", [128, 16], F32))
        sel_sb = ctx.enter_context(nc.sbuf_tensor("sel_sb", [128, 2], F32))
        onec_b = ctx.enter_context(nc.sbuf_tensor("onec_b", [128, 1], BF16))

        A_WQ, A_WK = 0, 16384
        A_H = 0
        B_WV = 32768
        B_AO, B_XS = 32768, 40960
        B_WE0, B_WE1 = 32768, 40960
        B_COMB, B_S2 = 32768, 40960
        C_X0, C_X1 = 49152, 57344
        C_QF, C_KF, C_VC = 65536, 69632, 73728
        C_AOT0, C_AOT1 = 77888, 78912
        C_WO = 49152
        C_ACC = 49152
        D_SNAP = 81920
        D_KFF0, D_KFF1 = 83008, 83136
        D_AOF = (83264, 84288, 85312, 86336)
        D_ASB0, D_ASB1 = 87424, 87616
        D_S1 = 81920
        D_SQ0, D_SQ1 = 90112, 90624
        D_GL0, D_GL1 = 81920, 82432
        D_OW0, D_OW1 = 81920, 83968
        S_EBST = 91136
        AO_W0A, AO_W1A = 91136, 91648
        AO_W0B, AO_W1B = 92160, 92672
        S_BK = 91136
        S_BV = 92160
        S_ONER = 93320 - 136
        S_RB = 93320 - 8

        F_XST = 0
        F_WST = 1024
        F_RE = 3072
        ST0, ST1, ST2 = 4096, 4608, 5120
        F_Z = 5632

        ps_state = ctx.enter_context(nc.psum_tensor("ps_state", [128, 1536], F32))
        ps_proj = ctx.enter_context(nc.psum_tensor("ps_proj", [128, 1024], F32))
        ps_small = ctx.enter_context(nc.psum_tensor("ps_small", [128, 1024], F32))
        ps_bf = ctx.enter_context(nc.psum_tensor("ps_bf", [128, 512], BF16))

        sems = {}
        SEM_NAMES = ("T", "V", "S", "C", "ST", "W0", "W1", "W2", "W3",
                     "X0", "X1", "P0", "P1", "P2", "P3", "O0", "O1", "A0", "A1", "EB", "DG0", "DG1")
        for s in SEM_NAMES:
            sems[s] = ctx.enter_context(nc.semaphore("sem" + s))

        SEME = {"tensor": "T", "vector": "V", "scalar": "S"}
        cnt = {s: 0 for s in SEM_NAMES}
        ops = {k: [] for k in ("sync", "gpsimd", "tensor", "vector", "scalar")}
        lastw = {}

        def _wait(eng, sem, val):
            if val <= 0:
                return
            if lastw.get((eng, sem), 0) >= val:
                return
            lastw[(eng, sem)] = val
            ops[eng].append(lambda e, s=sem, v=val: e.wait_ge(sems[s], v))

        def W(eng, *toks):
            best = {}
            for t in toks:
                if not t:
                    continue
                for u in (t if isinstance(t, list) else [t]):
                    if u:
                        best[u[0]] = max(best.get(u[0], 0), u[1])
            for s, v in best.items():
                _wait(eng, s, v)

        BUCK = 64
        track = {}
        BSZ = {}

        def _rng(ap):
            t = ap.tensor
            nm = t.name
            is_dram = "DRam" in type(t).__name__
            if nm not in BSZ:
                tot = 1
                for d in t.shape:
                    tot *= int(d)
                BSZ[nm] = max(BUCK, tot >> 6) if is_dram else BUCK
            rowlen = int(t.shape[-1])
            lo = int(ap.offset)
            hi = lo + 1
            skipped = is_dram
            for step, count in ap.ap:
                step = int(step); count = int(count)
                if not skipped and step == rowlen and len(t.shape) > 1:
                    skipped = True
                    continue
                hi += (count - 1) * step
            return (nm, lo, hi)

        def _collect(entry, key, deps):
            d = entry.get(key)
            if d:
                for s_, v_ in d.items():
                    if deps.get(s_, 0) < v_:
                        deps[s_] = v_

        def _deps_and_reg(rd_aps, wr_aps, token, wr_extra=None):
            deps = {}
            rranges = [_rng(ap) for ap in rd_aps]
            wranges = [_rng(ap) for ap in wr_aps]
            if wr_extra is not None:
                wranges = [wr_extra] + [w for w in wranges]
            for nm, lo, hi in rranges:
                d = track.get(nm)
                if d:
                    bz = BSZ[nm]
                    for b in range(lo // bz, (hi - 1) // bz + 1):
                        e = d.get(b)
                        if e:
                            _collect(e, "w", deps)
            for nm, lo, hi in wranges:
                d = track.get(nm)
                if d:
                    bz = BSZ[nm]
                    for b in range(lo // bz, (hi - 1) // bz + 1):
                        e = d.get(b)
                        if e:
                            _collect(e, "w", deps)
                            _collect(e, "r", deps)
            s_, v_ = token
            for nm, lo, hi in wranges:
                d = track.setdefault(nm, {})
                bz = BSZ[nm]
                for b in range(lo // bz, (hi - 1) // bz + 1):
                    d[b] = {"w": {s_: v_}, "r": {}}
            for nm, lo, hi in rranges:
                d = track.setdefault(nm, {})
                bz = BSZ[nm]
                for b in range(lo // bz, (hi - 1) // bz + 1):
                    e = d.setdefault(b, {"w": {}, "r": {}})
                    if e["r"].get(s_, 0) < v_:
                        e["r"][s_] = v_
            return deps

        RW = {
            "tensor_copy": (1, (1,)), "tensor_tensor": (1, (1, 2)),
            "tensor_scalar": (1, (1, 2, 3)), "tensor_single_scalar": (1, (1, 2)),
            "scalar_tensor_tensor": (1, (1, 2, 3)),
            "tensor_scalar_add": (1, (1, 2)), "tensor_scalar_sub": (1, (1, 2)),
            "tensor_scalar_mul": (1, (1, 2)), "tensor_scalar_min": (1, (1, 2)),
            "tensor_scalar_max": (1, (1, 2)),
            "reduce_max": (1, (1,)), "reduce_sum": (1, (1,)), "reciprocal": (1, (1,)),
            "memset": (1, ()), "memzero": (1, ()),
            "activation": (1, (1,)), "copy": (1, (1,)), "add": (1, (1, 2)),
            "mul": (1, (1, 2)), "square": (1, (1,)), "sqrt": (1, (1,)),
            "matmul": (1, (0, 1, 2)), "transpose": (1, (1, 2)),
        }

        def _is_ap(x):
            return hasattr(x, "tensor") and hasattr(x, "offset")

        def OP(eng, meth, *a, inc=True, **kw):
            s = SEME[eng]
            cnt[s] += 1
            v = cnt[s]
            token = (s, v)
            nw, rds = RW[meth]
            wr_aps = [x for x in a[:nw] if _is_ap(x)]
            rd_aps = [a[i] for i in rds if i < len(a) and _is_ap(a[i])]
            rd_aps += [x for x in kw.values() if _is_ap(x) and not isinstance(x, bool)]
            wr_extra = None
            if PSUM_ROUND and meth in ("matmul", "transpose") and kw.get("start", True):
                t = wr_aps[0].tensor
                if "PSum" in type(t).__name__:
                    esz = mybir.dt.size(t.dtype)
                    region = 2048 // esz
                    nm, lo, hi = _rng(wr_aps[0])
                    lo = (lo // region) * region
                    hi = ((hi + region - 1) // region) * region
                    wr_extra = (nm, lo, hi)
            deps = _deps_and_reg(rd_aps, wr_aps, token, wr_extra)
            if eng == "tensor":
                deps.pop("T", None)
            for s_, v_ in deps.items():
                _wait(eng, s_, v_)
            ops[eng].append(lambda e, m=meth, a=a, kw=kw, s=s: getattr(e, m)(*a, **kw).then_inc(sems[s], 1))
            return token

        def DMA(q, s, out, in_, record=True):
            cnt[s] += 16
            v = cnt[s]
            token = (s, v)
            if record:
                deps = _deps_and_reg([in_], [out], token)
            else:
                deps = _deps_and_reg([in_], [], token)
            for s_, v_ in deps.items():
                _wait(q, s_, v_)
            ops[q].append(lambda e, o=out, i=in_, s=s: e.dma_start(out=o, in_=i).then_inc(sems[s], 16))
            return token

        def REG(rd_aps, wr_aps, token, eng):
            deps = _deps_and_reg(rd_aps, wr_aps, token)
            for s_, v_ in deps.items():
                _wait(eng, s_, v_)

        def REREG_W(aps, token):
            s_, v_ = token
            for ap in aps:
                nm, lo, hi = _rng(ap)
                d = track.setdefault(nm, {})
                bz = BSZ[nm]
                for b in range(lo // bz, (hi - 1) // bz + 1):
                    d[b] = {"w": {s_: v_}, "r": {}}

        class Rot:
            def __init__(self, items):
                self.items = items
                self.busy = [[] for _ in items]
                self.i = 0

            def acquire(self):
                j = self.i % len(self.items)
                self.i += 1
                b = self.busy[j]
                self.busy[j] = []
                return j, self.items[j], b

            def use(self, j, tok):
                if tok:
                    self.busy[j].append(tok)

        def abr(c0, n):
            return ab[:, c0:c0 + n]

        def afr(c0, n):
            return af[:, c0:c0 + n]

        t_eye_l = DMA("sync", "ST", af[:, F_WST:F_WST + 128], eye_d)
        t_mask_l = DMA("sync", "ST", mask_f[:, :], mask_d)
        t_rw_l = DMA("sync", "ST", af[:, F_WST + 128:F_WST + 256], rw_d)
        DMA("sync", "ST", bq_sb[:, :], bq_d)
        DMA("sync", "ST", bo_sb[:, :], bo_d)
        DMA("sync", "ST", ob_sb[:, :], ob_d)
        t_sel_l = DMA("sync", "ST", sel_sb[:, :], sel_d)
        t_bk_l = DMA("sync", "ST", af[0:1, F_WST + 256:F_WST + 1280], bk_d)
        t_bv_l = DMA("sync", "ST", af[0:1, F_XST:F_XST + 1024], bv_d)
        t_rb_l = DMA("sync", "ST", af[0:1, F_Z + 8:F_Z + 16], rb_d)
        st_all = ("ST", cnt["ST"])
        t_eye_l = t_mask_l = t_rw_l = t_sel_l = t_bk_l = t_bv_l = t_rb_l = t_bq_l = st_all
        REREG_W([af[:, F_WST:F_WST + 256], mask_f[:, :], bq_sb[:, :], bo_sb[:, :],
                 ob_sb[:, :], sel_sb[:, :], af[0:1, F_WST + 256:F_WST + 1280],
                 af[0:1, F_XST:F_XST + 1024], af[0:1, F_Z + 8:F_Z + 16]], st_all)
        W("vector", st_all)
        t_eye_c = OP("vector", "tensor_copy", eye_b[:, :], af[:, F_WST:F_WST + 128], inc=True)
        t_rw_c = OP("vector", "tensor_copy", rw_b[:, :], af[:, F_WST + 128:F_WST + 256], inc=True)
        t_bk_c = OP("vector", "tensor_copy", ab[0:1, S_BK:S_BK + 1024], af[0:1, F_WST + 256:F_WST + 1280], inc=True)
        t_bv_c = OP("vector", "tensor_copy", ab[0:1, S_BV:S_BV + 1024], af[0:1, F_XST:F_XST + 1024], inc=True)
        t_rb_c = OP("vector", "tensor_copy", ab[0:1, S_RB:S_RB + 8], af[0:1, F_Z + 8:F_Z + 16], inc=True)
        t_oner = OP("vector", "memset", ab[0:1, S_ONER:S_ONER + 128], 1.0, inc=True)
        t_onec = OP("vector", "memset", onec_b[:, :], 1.0, inc=True)
        t_eps = OP("vector", "memset", af[0:1, F_Z + 4:F_Z + 5], EPS_LN, inc=True)
        vc_ones = abr(C_VC, 32 * 129).rearrange("p (s c) -> p s c", c=129)[:, :, 128:129]
        t_vcone = OP("vector", "memset", vc_ones, 1.0, inc=True)

        wst = Rot([afr(F_WST, 512), afr(F_WST + 512, 512),
                   afr(F_WST + 1024, 512), afr(F_WST + 1536, 512)])
        w_tok = {}
        first_w = True
        for name, wd, dst in (("q", wq_d, A_WQ), ("k", wk_d, A_WK), ("v", wv_d, B_WV)):
            for k in range(16):
                for half in range(2):
                    j, st, free = wst.acquire()
                    if first_w:
                        W("sync", t_bk_c, t_bv_c, t_eye_c, t_rw_c)
                        first_w = False
                    W("sync", free)
                    tl = DMA("sync", "W%d" % j, st, wd[k * 128:(k + 1) * 128, half * 512:(half + 1) * 512])
                    W("vector", tl)
                    tc = OP("vector", "tensor_copy", abr(dst + k * 1024 + half * 512, 512), st, inc=True)
                    wst.use(j, tc)
                    w_tok[(name, k, half)] = tc

        xst = Rot([afr(F_XST, 512), afr(F_XST + 512, 512)])
        xbuf = Rot([C_X0, C_X1])
        proj = Rot([ps_proj[:, 0:512], ps_proj[:, 512:1024]])
        re_rot = Rot([(afr(F_RE, 512), afr(F_RE + 512, 512))])
        kfT = Rot([ps_bf[:, 0:128], ps_bf[:, 128:256], ps_bf[:, 256:384], ps_bf[:, 384:512]])
        aT = Rot([ps_small[:, 0:128], ps_small[:, 128:256]])
        outp = Rot([ps_small[:, 256:385], ps_small[:, 512:641]])
        kff = Rot([D_KFF0, D_KFF1])
        aofm = Rot(list(D_AOF))
        asb = Rot([D_ASB0, D_ASB1])
        aotm = Rot([C_AOT0, C_AOT1])
        zscr = Rot([(af[:, F_Z:F_Z + 1], af[:, F_Z + 1:F_Z + 2]),
                    (af[:, F_Z + 2:F_Z + 3], af[:, F_Z + 3:F_Z + 4])])

        snap_ap = abr(D_SNAP, 1032)

        def state_slot(h):
            o = (h // 3) * 512 + (h % 3) * 129
            return ps_state[:, o:o + 129]

        def snap_slot(h):
            return abr(D_SNAP + h * 129, 129)

        qkv_read_tok = []
        snap_read_tok = []
        kv_mm_tok = []

        for g in range(ngroups):
            jx, xbase, xfree = xbuf.acquire()
            xcast_tok = []
            for k in range(16):
                js, st, sfree = xst.acquire()
                W("sync", sfree, t_bv_c if g == 0 and k == 0 else None)
                tl = DMA("sync", "X%d" % js, st, xT[k * 128:(k + 1) * 128, g * 512:(g + 1) * 512])
                W("vector", tl)
                if k == 0:
                    W("vector", xfree)
                tc = OP("vector", "tensor_copy", abr(xbase + k * 512, 512), st, inc=True)
                xst.use(js, tc)
                xcast_tok.append(tc)

            qf_writers = []
            for m in range(8):
                jp, pp, pfree = proj.acquire()
                W("tensor", pfree, xcast_tok)
                tmm = None
                for k in range(16):
                    W("tensor", w_tok[("q", k, m // 4)])
                    tmm = OP("tensor", "matmul", pp,
                             abr(A_WQ + k * 1024 + m * 128, 128),
                             abr(xbase + k * 512, 512),
                             start=(k == 0), stop=(k == 15),
                             skip_group_check=True, inc=(k == 15))
                jr, (rr, ee), rfree = re_rot.acquire()
                W("scalar", tmm, rfree, t_bq_l)
                ts_r = OP("scalar", "activation", rr, pp, AF.Relu, bias=bq_sb[:, m:m + 1], inc=True)
                ts_e = OP("scalar", "activation", ee, pp, AF.Exp, bias=bq_sb[:, m:m + 1], inc=True)
                W("vector", ts_r, ts_e, qkv_read_tok)
                tq = OP("vector", "scalar_tensor_tensor",
                        abr(C_QF + m * 512, 512), ee, 1.0, rr, ALU.min, ALU.add, inc=True)
                re_rot.use(jr, tq)
                proj.use(jp, ts_r)
                proj.use(jp, ts_e)
                qf_writers.append(tq)

            kf_w = [None] * (GRP * 2)
            vc_w = [None] * (GRP * 2)
            for cc in range(GRP):
                gc = g * GRP + cc
                for dt in range(2):
                    jp, pp, pfree = proj.acquire()
                    W("tensor", pfree, t_oner, t_bk_c)
                    OP("tensor", "matmul", pp,
                       ab[0:1, S_ONER:S_ONER + 128],
                       ab[0:1, S_BK + dt * 512:S_BK + dt * 512 + 512],
                       start=True, stop=False, skip_group_check=True)
                    tmm = None
                    for k in range(16):
                        W("tensor", w_tok[("k", k, dt)])
                        tmm = OP("tensor", "matmul", pp,
                                 abr(xbase + k * 512 + cc * 128, 128),
                                 abr(A_WK + k * 1024 + dt * 512, 512),
                                 start=False, stop=(k == 15),
                                 skip_group_check=True, inc=(k == 15))
                    jr, (rr, ee), rfree = re_rot.acquire()
                    W("scalar", tmm, rfree)
                    ts_r = OP("scalar", "activation", rr, pp, AF.Relu, inc=True)
                    ts_e = OP("scalar", "activation", ee, pp, AF.Exp, inc=True)
                    W("vector", ts_r, ts_e, qkv_read_tok)
                    tk = OP("vector", "scalar_tensor_tensor",
                            abr(C_KF + (cc * 8 + dt * 4) * 128, 512), ee, 1.0, rr,
                            ALU.min, ALU.add, inc=True)
                    re_rot.use(jr, tk)
                    proj.use(jp, ts_r)
                    proj.use(jp, ts_e)
                    kf_w[cc * 2 + dt] = tk
                for dt in range(2):
                    jp, pp, pfree = proj.acquire()
                    W("tensor", pfree, t_oner, t_bv_c)
                    OP("tensor", "matmul", pp,
                       ab[0:1, S_ONER:S_ONER + 128],
                       ab[0:1, S_BV + dt * 512:S_BV + dt * 512 + 512],
                       start=True, stop=False, skip_group_check=True)
                    tmm = None
                    for k in range(16):
                        W("tensor", w_tok[("v", k, dt)])
                        tmm = OP("tensor", "matmul", pp,
                                 abr(xbase + k * 512 + cc * 128, 128),
                                 abr(B_WV + k * 1024 + dt * 512, 512),
                                 start=False, stop=(k == 15),
                                 skip_group_check=True, inc=(k == 15))
                    W("vector", tmm, t_vcone, qkv_read_tok)
                    vbase = C_VC + (cc * 8 + dt * 4) * 129
                    vview = abr(vbase, 4 * 129).rearrange("p (s c) -> p s c", c=129)[:, :, 0:128]
                    tv = OP("vector", "tensor_copy", vview,
                            pp.rearrange("p (s c) -> p s c", c=128), inc=True)
                    proj.use(jp, tv)
                    vc_w[cc * 2 + dt] = tv

                if do_rec < 1:
                    continue
                if gc == 0:
                    W("vector", kv_mm_tok, snap_read_tok)
                    tsn = [OP("vector", "memset", snap_ap, 0.0, inc=True)]
                else:
                    W("scalar", kv_mm_tok, snap_read_tok)
                    tsn = [
                        OP("scalar", "copy", abr(D_SNAP, 387), ps_state[:, 0:387], inc=True),
                        OP("scalar", "copy", abr(D_SNAP + 387, 387), ps_state[:, 512:899], inc=True),
                        OP("scalar", "copy", abr(D_SNAP + 774, 258), ps_state[:, 1024:1282], inc=True),
                    ]
                snap_read_tok = []
                kv_mm_tok = []

                for h in range(HL):
                    W("tensor", tsn, kf_w[cc * 2 + h // 4], vc_w[cc * 2 + h // 4])
                    tkv = OP("tensor", "matmul", state_slot(h),
                             abr(C_KF + (cc * 8 + h) * 128, 128),
                             abr(C_VC + (cc * 8 + h) * 129, 129),
                             start=(gc == 0 and h in (0, 3, 6)),
                             stop=(gc == NCH - 1 and h in (2, 5, 7)),
                             skip_group_check=True, inc=True)
                    kv_mm_tok.append(tkv)

                if do_rec < 2:
                    continue
                ja, aob, aofree = aotm.acquire()
                norm_toks = []
                for h in range(HL):
                    jt, kps, ktfree = kfT.acquire()
                    W("tensor", ktfree, kf_w[cc * 2 + h // 4], t_eye_c)
                    ttr = OP("tensor", "transpose", kps,
                             abr(C_KF + (cc * 8 + h) * 128, 128), eye_b[:, :], inc=True)
                    jf, kfb, kffree = kff.acquire()
                    W("vector", ttr, kffree)
                    tkc = OP("vector", "tensor_copy", abr(kfb, 128), kps, inc=True)
                    kfT.use(jt, tkc)
                    jA, Aps, Afree = aT.acquire()
                    W("tensor", Afree, tkc, qf_writers[h])
                    tA = OP("tensor", "matmul", Aps,
                            abr(kfb, 128),
                            abr(C_QF + h * 512 + cc * 128, 128),
                            start=True, stop=True, skip_group_check=True, inc=True)
                    kff.use(jf, tA)
                    jm, Ab, Afree2 = asb.acquire()
                    W("vector", tA, Afree2, t_mask_l)
                    tAm = OP("vector", "tensor_tensor", abr(Ab, 128), Aps, mask_f[:, :], ALU.mult, inc=True)
                    aT.use(jA, tAm)
                    if do_rec < 3:
                        norm_toks.append(tAm)
                        continue
                    jo, ops_, ofree = outp.acquire()
                    W("tensor", ofree, tAm, tsn, qf_writers[h], vc_w[cc * 2 + h // 4])
                    OP("tensor", "matmul", ops_,
                       abr(C_QF + h * 512 + cc * 128, 128), snap_slot(h),
                       start=True, stop=False, skip_group_check=True)
                    tio = OP("tensor", "matmul", ops_,
                             abr(Ab, 128),
                             abr(C_VC + (cc * 8 + h) * 129, 129),
                             start=False, stop=True, skip_group_check=True, inc=True)
                    snap_read_tok.append(tio)
                    asb.use(jm, tio)
                    jz, (zb, rzb), zfree = zscr.acquire()
                    W("vector", tio, zfree)
                    OP("vector", "tensor_scalar_add", zb, ops_[:, 128:129], EPS_ATT)
                    OP("vector", "reciprocal", rzb, zb)
                    if h == 0:
                        W("vector", aofree)
                    tn = OP("vector", "tensor_scalar_mul",
                            abr(aob + h * 128, 128), ops_[:, 0:128], rzb, inc=True)
                    zscr.use(jz, tn)
                    outp.use(jo, tn)
                    norm_toks.append(tn)

                if do_rec < 4:
                    continue
                jaf, afb, affree = aofm.acquire()
                wj = gc // 16
                lcol = (gc % 16) * 128
                aof_toks = []
                for h in range(HL):
                    jt, kps, ktfree = kfT.acquire()
                    W("tensor", ktfree, norm_toks[h], t_eye_c)
                    ttr = OP("tensor", "transpose", kps, abr(aob + h * 128, 128), eye_b[:, :], inc=True)
                    aotm.use(ja, ttr)
                    W("vector", ttr)
                    if h == 0:
                        W("vector", affree)
                    tac = OP("vector", "tensor_copy", abr(afb + h * 128, 128), kps, inc=True)
                    kfT.use(jt, tac)
                    aof_toks.append(tac)
                if do_rec >= 5:
                    W("gpsimd", aof_toks)
                    td = None
                    for hh2 in range(2):
                        td = DMA("gpsimd", "A%d" % (jaf % 2),
                                 cc_in.ap()[wj * 1024 + hh2 * 512:wj * 1024 + (hh2 + 1) * 512,
                                            lcol:lcol + 128].rearrange("(h p) c -> p h c", p=128),
                                 abr(afb + hh2 * 512, 512).rearrange("p (h c) -> p h c", c=128))
                    aofm.use(jaf, td)
                    REG([abr(afb, 1024)], [], td, "gpsimd")
                else:
                    for tac in aof_toks:
                        aofm.use(jaf, tac)

            last_t = ("T", cnt["T"])
            xbuf.use(jx, last_t)
            qkv_read_tok = [last_t]

        _wait("gpsimd", "A0", cnt["A0"])
        _wait("gpsimd", "A1", cnt["A1"])
        if do_cc:
            cnt["C"] += 1
            REG([cc_in.ap()], [cc_out.ap()], ("C", cnt["C"]), "gpsimd")
            ops["gpsimd"].append(lambda e: e.collective_compute(
                "AllGather", ALU.bypass,
                replica_groups=[[0, 1], [2, 3], [4, 5], [6, 7]],
                ins=[cc_in.ap()], outs=[cc_out.ap()],
            ).then_inc(sems["C"], 1))
        else:
            nrows = 1024 if ngroups <= 4 else 2048
            ncols = min(ngroups, 4) * 512
            cnt["C"] += 16
            REG([cc_in.ap()], [cc_out.ap()], ("C", cnt["C"]), "gpsimd")
            ops["gpsimd"].append(lambda e, nr=nrows, nc_=ncols: e.dma_start(
                out=cc_out.ap()[0:nr, 0:nc_], in_=cc_in.ap()[0:nr, 0:nc_]).then_inc(sems["C"], 16))
            cnt["C"] += 16
            ops["gpsimd"].append(lambda e, nr=nrows, nc_=ncols: e.dma_start(
                out=cc_out.ap()[DM:DM + nr, 0:nc_], in_=cc_in.ap()[0:nr, 0:nc_]).then_inc(sems["C"], 16))
        t_cc0 = ("C", cnt["C"])
        t_cc = ("C", cnt["C"])
        attn_end = [("T", cnt["T"]), ("V", cnt["V"]), ("S", cnt["S"])]

        if debug & 1:
            _dbsc = [D_KFF0, D_AOF[0]]
            _dfsc = [F_RE, F_RE + 512]
            _prevc = [None, None]
            _prevs = [None, None]
            for _r in range(32):
                for _c in range(4):
                    _j = (_r * 4 + _c) % 2
                    W("gpsimd", t_cc, attn_end, _prevc[_j])
                    _tl = DMA("gpsimd", "DG%d" % _j, abr(_dbsc[_j], 512),
                              cc_out.ap()[_r * 128:(_r + 1) * 128, _c * 512:(_c + 1) * 512])
                    W("vector", _tl, _prevs[_j])
                    _tc = OP("vector", "tensor_copy", afr(_dfsc[_j], 512), abr(_dbsc[_j], 512), inc=True)
                    _prevc[_j] = _tc
                    W("gpsimd", _tc)
                    _ts = DMA("gpsimd", "DG%d" % _j, dbg["ao"][_r * 128:(_r + 1) * 128, _c * 512:(_c + 1) * 512],
                              afr(_dfsc[_j], 512))
                    _prevs[_j] = _ts


        if phases == "attn":
            W("vector", attn_end)
            tz = OP("vector", "memset", afr(F_RE, 512), 0.0, inc=True)
            W("sync", tz, t_cc)
            for m_ in range(16):
                for tq_ in range(4):
                    DMA("sync", "O%d" % (tq_ % 2), out_d[m_ * 128:(m_ + 1) * 128, tq_ * 512:(tq_ + 1) * 512], afr(F_RE, 512))

        if phases == "full":
            def ao_rows(j, k):
                r0 = (k // 8) * 2048 + j * 1024 + (k % 8) * 128
                return cc_out.ap()[r0:r0 + 128, :]

            wo_tok = {}
            first_post = True
            for k in range(16):
                for half in range(4):
                    j, st, free = wst.acquire()
                    if first_post and not WO_PRELOAD:
                        W("sync", attn_end)
                        W("vector", attn_end)
                        first_post = False
                    W("sync", free)
                    tl = DMA("sync", "W%d" % j, st, wo_d[k * 128:(k + 1) * 128, half * 512:(half + 1) * 512])
                    W("vector", tl)
                    tc = OP("vector", "tensor_copy", abr(C_WO + k * 2048 + half * 512, 512), st, inc=True)
                    wst.use(j, tc)
                    wo_tok[(k, half)] = tc

            aopair = Rot([(AO_W0A, AO_W1A), (AO_W0B, AO_W1B)])
            xst2 = Rot([afr(F_XST, 512), afr(F_XST + 512, 512)])
            sqr = Rot([D_SQ0, D_SQ1])
            outr = Rot([afr(F_RE, 512), afr(F_RE + 512, 512)])
            stats_s = ps_small[0:1, 0:512]
            stats_q = ps_small[0:1, 512:1024]
            bc_r = ps_state[:, 0:512]
            bc_mr = ps_state[:, 512:1024]

            h_tok = {}
            prev_b1_tmm = None
            for tt in range(4):
                ao_tok = {}
                x_tok = {}
                for k in range(16):
                    jj, (w0c, w1c), pfree = aopair.acquire()
                    W("sync", pfree, t_cc)
                    if tt == 0 and k == 0:
                        W("sync", attn_end)
                    t0l = DMA("sync", "P%d" % (2 * jj), abr(w0c, 512), ao_rows(0, k)[:, tt * 512:(tt + 1) * 512])
                    t1l = DMA("sync", "P%d" % (2 * jj + 1), abr(w1c, 512), ao_rows(1, k)[:, tt * 512:(tt + 1) * 512])
                    W("vector", t0l, t1l, t_sel_l, prev_b1_tmm)
                    OP("vector", "tensor_scalar_mul", abr(B_AO + k * 512, 512), abr(w0c, 512), sel_sb[:, 0:1])
                    tcmb = OP("vector", "scalar_tensor_tensor", abr(B_AO + k * 512, 512),
                              abr(w1c, 512), sel_sb[:, 1:2], abr(B_AO + k * 512, 512),
                              ALU.mult, ALU.add, inc=True)
                    aopair.use(jj, tcmb)
                    ao_tok[k] = tcmb
                    js, st, sfree = xst2.acquire()
                    W("sync", sfree)
                    tl = DMA("sync", "X%d" % js, st, xR[k * 128:(k + 1) * 128, tt * 512:(tt + 1) * 512])
                    W("vector", tl)
                    tc = OP("vector", "tensor_copy", abr(B_XS + k * 512, 512), st, inc=True)
                    xst2.use(js, tc)
                    x_tok[k] = tc

                stat_mm_s = stat_mm_q = None
                s1_toks = []
                for m in range(16):
                    jp, pp, pfree = proj.acquire()
                    W("tensor", pfree, x_tok[m], t_eye_c)
                    OP("tensor", "matmul", pp, eye_b[:, :], abr(B_XS + m * 512, 512),
                       start=True, stop=False, skip_group_check=True)
                    tmm = None
                    for k in range(16):
                        W("tensor", ao_tok[k], wo_tok[(k, m // 4)])
                        tmm = OP("tensor", "matmul", pp,
                                 abr(C_WO + k * 2048 + m * 128, 128),
                                 abr(B_AO + k * 512, 512),
                                 start=False, stop=(k == 15),
                                 skip_group_check=True, inc=(k == 15))
                    if m == 15:
                        prev_b1_tmm = tmm
                    W("scalar", tmm)
                    ts1 = OP("scalar", "activation", abr(D_S1 + m * 512, 512), pp, AF.Identity,
                             bias=bo_sb[:, m:m + 1], inc=True)
                    jq, sqb, sqfree = sqr.acquire()
                    W("scalar", sqfree)
                    tsq = OP("scalar", "activation", abr(sqb, 512), pp, AF.Square,
                             bias=bo_sb[:, m:m + 1], inc=True)
                    proj.use(jp, ts1)
                    proj.use(jp, tsq)
                    W("tensor", ts1, tsq, t_onec)
                    stat_mm_s = OP("tensor", "matmul", stats_s, onec_b[:, :], abr(D_S1 + m * 512, 512),
                                   start=(m == 0), stop=(m == 15), skip_group_check=True, inc=True)
                    stat_mm_q = OP("tensor", "matmul", stats_q, onec_b[:, :], abr(sqb, 512),
                                   start=(m == 0), stop=(m == 15), skip_group_check=True, inc=True)
                    sqr.use(jq, stat_mm_q)
                    s1_toks.append(ts1)

                W("scalar", stat_mm_s, stat_mm_q)
                OP("scalar", "copy", af[0:1, ST0:ST0 + 512], stats_s)
                OP("scalar", "copy", af[0:1, ST1:ST1 + 512], stats_q)
                OP("vector", "tensor_scalar_mul", af[0:1, ST0:ST0 + 512], af[0:1, ST0:ST0 + 512], 1.0 / DM)
                OP("vector", "tensor_scalar_mul", af[0:1, ST1:ST1 + 512], af[0:1, ST1:ST1 + 512], 1.0 / DM)
                OP("vector", "tensor_tensor", af[0:1, ST2:ST2 + 512], af[0:1, ST0:ST0 + 512],
                   af[0:1, ST0:ST0 + 512], ALU.mult)
                tvv = OP("vector", "tensor_tensor", af[0:1, ST1:ST1 + 512], af[0:1, ST1:ST1 + 512],
                         af[0:1, ST2:ST2 + 512], ALU.subtract, inc=True)
                W("scalar", tvv, t_eps)
                tsv = OP("scalar", "activation", af[0:1, ST1:ST1 + 512], af[0:1, ST1:ST1 + 512],
                         AF.Sqrt, bias=af[0:1, F_Z + 4:F_Z + 5], inc=True)
                W("vector", tsv)
                OP("vector", "reciprocal", af[0:1, ST1:ST1 + 512], af[0:1, ST1:ST1 + 512])
                tmr = OP("vector", "tensor_tensor", af[0:1, ST2:ST2 + 512], af[0:1, ST0:ST0 + 512],
                         af[0:1, ST1:ST1 + 512], ALU.mult, inc=True)
                W("tensor", tmr, t_mask_l)
                tb1 = OP("tensor", "matmul", bc_r, mask_f[0:1, 0:128], af[0:1, ST1:ST1 + 512],
                         start=True, stop=True, skip_group_check=True, inc=True)
                tb2 = OP("tensor", "matmul", bc_mr, mask_f[0:1, 0:128], af[0:1, ST2:ST2 + 512],
                         start=True, stop=True, skip_group_check=True, inc=True)
                for m in range(16):
                    W("vector", tb1, tb2, s1_toks[m])
                    OP("vector", "tensor_tensor", abr(B_XS + m * 512, 512),
                       abr(D_S1 + m * 512, 512), bc_r, ALU.mult)
                    th = OP("vector", "tensor_tensor", abr(A_H + m * 2048 + tt * 512, 512),
                            abr(B_XS + m * 512, 512), bc_mr, ALU.subtract, inc=True)
                    h_tok[(m, tt)] = th

            if debug & 2:
                _prevs = [None, None]
                for m in range(16):
                    for _c in range(4):
                        _j = _c % 2
                        W("vector", [h_tok[(m, q)] for q in range(4)], _prevs[_j])
                        _tc = OP("vector", "tensor_copy", afr(F_RE + _j * 512, 512),
                                 abr(A_H + m * 2048 + _c * 512, 512), inc=True)
                        W("sync", _tc)
                        _ts = DMA("sync", "DG%d" % _j, dbg["h"][:, m * 2048 + _c * 512:m * 2048 + (_c + 1) * 512],
                                  afr(F_RE + _j * 512, 512))
                        _prevs[_j] = _ts

            gate_tok = {}
            prev_texp = None
            for cc in range(16):
                jp, pp, pfree = proj.acquire()
                pp8 = pp[:, 0:8]
                W("tensor", pfree, t_oner, t_rb_c, t_rw_c,
                  [h_tok[(k, cc // 4)] for k in range(16)])
                OP("tensor", "matmul", pp8, ab[0:1, S_ONER:S_ONER + 128], ab[0:1, S_RB:S_RB + 8],
                   start=True, stop=False, skip_group_check=True)
                tmm = None
                for k in range(16):
                    tmm = OP("tensor", "matmul", pp8,
                             abr(A_H + k * 2048 + cc * 128, 128),
                             rw_b[:, k * 8:(k + 1) * 8],
                             start=False, stop=(k == 15), skip_group_check=True, inc=(k == 15))
                W("vector", tmm, prev_texp)
                m1n = af[:, F_Z:F_Z + 1]
                tm1 = OP("vector", "reduce_max", m1n, pp8, axis=AX.X, negate=True, inc=True)
                W("scalar", tm1)
                texp = OP("scalar", "activation", afr(ST0, 8), pp8, AF.Exp, bias=m1n, inc=True)
                prev_texp = texp
                proj.use(jp, tm1)
                proj.use(jp, texp)
                t_sb = afr(ST0, 8)
                W("vector", texp)
                e1 = af[:, F_Z + 1:F_Z + 2]
                OP("vector", "reduce_max", e1, t_sb, axis=AX.X)
                lt = afr(ST0 + 8, 8)
                OP("vector", "tensor_single_scalar", lt, t_sb, e1, ALU.is_lt)
                tmsk = afr(ST0 + 16, 8)
                OP("vector", "tensor_tensor", tmsk, t_sb, lt, ALU.mult)
                e2 = af[:, F_Z + 2:F_Z + 3]
                OP("vector", "reduce_max", e2, tmsk, axis=AX.X)
                keep = afr(ST0 + 8, 8)
                OP("vector", "tensor_single_scalar", keep, t_sb, e2, ALU.is_ge)
                den = af[:, F_Z + 3:F_Z + 4]
                OP("vector", "tensor_tensor", den, e1, e2, ALU.add)
                OP("vector", "reciprocal", den, den)
                OP("vector", "tensor_tensor", t_sb, t_sb, keep, ALU.mult)
                tg = OP("vector", "tensor_scalar_mul", gates_sb[:, cc * 8:cc * 8 + 8], t_sb, den, inc=True)
                gate_tok[cc] = tg

            if debug & 4:
                W("sync", [gate_tok[c] for c in range(16)])
                DMA("sync", "DG0", dbg["gates"], gates_sb[:, :])

            web = Rot([B_WE0, B_WE1])
            glr = Rot([D_GL0, D_GL1])
            acc_tok = {}
            prev_teb = None
            prev_e_last_mm = None
            for e in range(8):
                W("gpsimd", gate_tok[15] if e == 0 else None, prev_teb)
                teb_l1 = DMA("gpsimd", "EB", af[0:1, F_XST:F_XST + 1024], eb_d[0:1, e * 2048:e * 2048 + 1024])
                W("vector", teb_l1, prev_e_last_mm)
                teb1 = OP("vector", "tensor_copy", ab[0:1, S_EBST:S_EBST + 1024],
                          af[0:1, F_XST:F_XST + 1024], inc=True)
                W("gpsimd", teb1)
                teb_l2 = DMA("gpsimd", "EB", af[0:1, F_XST:F_XST + 1024], eb_d[0:1, e * 2048 + 1024:(e + 1) * 2048])
                W("vector", teb_l2)
                teb = OP("vector", "tensor_copy", ab[0:1, S_EBST + 1024:S_EBST + 2048],
                         af[0:1, F_XST:F_XST + 1024], inc=True)
                prev_teb = teb
                teb = [teb1, teb]
                for dt in range(4):
                    jw, wbase, wfree = web.acquire()
                    wk_toks = []
                    for k in range(16):
                        js, st, sfree = wst.acquire()
                        W("sync", sfree)
                        if k == 0:
                            W("sync", wfree)
                        tl = DMA("sync", "W%d" % js, st, ew_d[e, k * 128:(k + 1) * 128, dt * 512:(dt + 1) * 512])
                        W("vector", tl)
                        tc = OP("vector", "tensor_copy", abr(wbase + k * 512, 512), st, inc=True)
                        wst.use(js, tc)
                        wk_toks.append(tc)
                    for cc in range(16):
                        jp, pp, pfree = proj.acquire()
                        W("tensor", pfree, teb, t_oner, wk_toks)
                        OP("tensor", "matmul", pp, ab[0:1, S_ONER:S_ONER + 128],
                           ab[0:1, S_EBST + dt * 512:S_EBST + dt * 512 + 512],
                           start=True, stop=False, skip_group_check=True)
                        tmm = None
                        for k in range(16):
                            tmm = OP("tensor", "matmul", pp,
                                     abr(A_H + k * 2048 + cc * 128, 128),
                                     abr(wbase + k * 512, 512),
                                     start=False, stop=(k == 15), skip_group_check=True, inc=(k == 15))
                        web.use(jw, tmm)
                        prev_e_last_mm = tmm
                        jg, glb, glfree = glr.acquire()
                        W("scalar", tmm, glfree)
                        tgl = OP("scalar", "activation", abr(glb, 512), pp, AF.Gelu_apprx_tanh, inc=True)
                        proj.use(jp, tgl)
                        W("vector", tgl, gate_tok[cc], acc_tok.get((cc, dt)))
                        gcol = gates_sb[:, cc * 8 + e:cc * 8 + e + 1]
                        dst = abr(C_ACC + cc * 2048 + dt * 512, 512)
                        if e == 0:
                            ta = OP("vector", "tensor_scalar_mul", dst, abr(glb, 512), gcol, inc=True)
                        else:
                            ta = OP("vector", "scalar_tensor_tensor", dst, abr(glb, 512), gcol, dst,
                                    ALU.mult, ALU.add, inc=True)
                        glr.use(jg, ta)
                        acc_tok[(cc, dt)] = ta

            if debug & 8:
                _prevs = [None, None]
                for _m in range(16):
                    for _c in range(4):
                        _j = _c % 2
                        W("vector", acc_tok[(_m, _c)], _prevs[_j])
                        _tc = OP("vector", "tensor_copy", afr(F_RE + _j * 512, 512),
                                 abr(C_ACC + _m * 2048 + _c * 512, 512), inc=True)
                        W("sync", _tc)
                        _ts = DMA("sync", "DG%d" % _j, dbg["acc"][:, _m * 2048 + _c * 512:_m * 2048 + (_c + 1) * 512],
                                  afr(F_RE + _j * 512, 512))
                        _prevs[_j] = _ts

            owr = Rot([D_OW0, D_OW1])
            cfT = Rot([ps_bf[:, 0:128], ps_bf[:, 128:256], ps_bf[:, 256:384], ps_bf[:, 384:512]])
            comb_tok = {}
            for tt in range(4):
                for d in range(16):
                    for ccl in range(4):
                        cc = tt * 4 + ccl
                        jt, kps, ktfree = cfT.acquire()
                        W("tensor", ktfree, acc_tok[(cc, d // 4)], t_eye_c)
                        ttr = OP("tensor", "transpose", kps,
                                 abr(C_ACC + cc * 2048 + d * 128, 128), eye_b[:, :], inc=True)
                        W("vector", ttr)
                        tcp = OP("vector", "tensor_copy", abr(B_COMB + d * 512 + ccl * 128, 128), kps, inc=True)
                        cfT.use(jt, tcp)
                        comb_tok[(d, ccl)] = tcp
                comb_all = [comb_tok[(d, c)] for d in range(16) for c in range(4)]
                stat_mm_s = stat_mm_q = None
                s2_toks = []
                for m in range(16):
                    jw, owb, owfree = owr.acquire()
                    ow_toks = []
                    for piece in range(4):
                        js, st, sfree = wst.acquire()
                        W("sync", sfree)
                        if piece == 0:
                            W("sync", owfree)
                        tl = DMA("sync", "W%d" % js, st.rearrange("p (k c) -> p k c", c=128),
                                 ow_d[piece * 512:(piece + 1) * 512, m * 128:(m + 1) * 128].rearrange(
                                     "(k p) c -> p k c", p=128))
                        W("vector", tl)
                        tc = OP("vector", "tensor_copy", abr(owb + piece * 512, 512), st, inc=True)
                        wst.use(js, tc)
                        ow_toks.append(tc)
                    jp, pp, pfree = proj.acquire()
                    W("tensor", pfree, t_eye_c, h_tok[(m, tt)], ow_toks, comb_all)
                    OP("tensor", "matmul", pp, eye_b[:, :], abr(A_H + m * 2048 + tt * 512, 512),
                       start=True, stop=False, skip_group_check=True)
                    tmm = None
                    for k in range(16):
                        tmm = OP("tensor", "matmul", pp,
                                 abr(owb + k * 128, 128),
                                 abr(B_COMB + k * 512, 512),
                                 start=False, stop=(k == 15), skip_group_check=True, inc=(k == 15))
                    owr.use(jw, tmm)
                    W("scalar", tmm)
                    ts2 = OP("scalar", "activation", abr(B_S2 + m * 512, 512), pp, AF.Identity,
                             bias=ob_sb[:, m:m + 1], inc=True)
                    jq, sqb, sqfree = sqr.acquire()
                    W("scalar", sqfree)
                    tsq = OP("scalar", "activation", abr(sqb, 512), pp, AF.Square,
                             bias=ob_sb[:, m:m + 1], inc=True)
                    proj.use(jp, ts2)
                    proj.use(jp, tsq)
                    W("tensor", ts2, tsq, t_onec)
                    stat_mm_s = OP("tensor", "matmul", stats_s, onec_b[:, :], abr(B_S2 + m * 512, 512),
                                   start=(m == 0), stop=(m == 15), skip_group_check=True, inc=True)
                    stat_mm_q = OP("tensor", "matmul", stats_q, onec_b[:, :], abr(sqb, 512),
                                   start=(m == 0), stop=(m == 15), skip_group_check=True, inc=True)
                    sqr.use(jq, stat_mm_q)
                    s2_toks.append(ts2)

                W("scalar", stat_mm_s, stat_mm_q)
                OP("scalar", "copy", af[0:1, ST0:ST0 + 512], stats_s)
                OP("scalar", "copy", af[0:1, ST1:ST1 + 512], stats_q)
                OP("vector", "tensor_scalar_mul", af[0:1, ST0:ST0 + 512], af[0:1, ST0:ST0 + 512], 1.0 / DM)
                OP("vector", "tensor_scalar_mul", af[0:1, ST1:ST1 + 512], af[0:1, ST1:ST1 + 512], 1.0 / DM)
                OP("vector", "tensor_tensor", af[0:1, ST2:ST2 + 512], af[0:1, ST0:ST0 + 512],
                   af[0:1, ST0:ST0 + 512], ALU.mult)
                tvv = OP("vector", "tensor_tensor", af[0:1, ST1:ST1 + 512], af[0:1, ST1:ST1 + 512],
                         af[0:1, ST2:ST2 + 512], ALU.subtract, inc=True)
                W("scalar", tvv, t_eps)
                tsv = OP("scalar", "activation", af[0:1, ST1:ST1 + 512], af[0:1, ST1:ST1 + 512],
                         AF.Sqrt, bias=af[0:1, F_Z + 4:F_Z + 5], inc=True)
                W("vector", tsv)
                OP("vector", "reciprocal", af[0:1, ST1:ST1 + 512], af[0:1, ST1:ST1 + 512])
                tmr = OP("vector", "tensor_tensor", af[0:1, ST2:ST2 + 512], af[0:1, ST0:ST0 + 512],
                         af[0:1, ST1:ST1 + 512], ALU.mult, inc=True)
                W("tensor", tmr, t_mask_l)
                tb1 = OP("tensor", "matmul", bc_r, mask_f[0:1, 0:128], af[0:1, ST1:ST1 + 512],
                         start=True, stop=True, skip_group_check=True, inc=True)
                tb2 = OP("tensor", "matmul", bc_mr, mask_f[0:1, 0:128], af[0:1, ST2:ST2 + 512],
                         start=True, stop=True, skip_group_check=True, inc=True)
                for m in range(16):
                    jo, ob_, ofree = outr.acquire()
                    W("vector", tb1, tb2, s2_toks[m], ofree)
                    OP("vector", "tensor_tensor", ob_, abr(B_S2 + m * 512, 512), bc_r, ALU.mult)
                    tf = OP("vector", "tensor_tensor", ob_, ob_, bc_mr, ALU.subtract, inc=True)
                    W("sync", tf)
                    tso = DMA("sync", "O%d" % jo, out_d[m * 128:(m + 1) * 128, tt * 512:(tt + 1) * 512], ob_)
                    outr.use(jo, tso)


        _wait("sync", "O0", cnt["O0"])
        _wait("sync", "O1", cnt["O1"])

        with nc.Block() as block:
            @block.sync
            def _(e):
                for f in ops["sync"]:
                    f(e)

            @block.gpsimd
            def _(e):
                for f in ops["gpsimd"]:
                    f(e)

            @block.tensor
            def _(e):
                for f in ops["tensor"]:
                    f(e)

            @block.vector
            def _(e):
                for f in ops["vector"]:
                    f(e)

            @block.scalar
            def _(e):
                for f in ops["scalar"]:
                    f(e)

    return nc


def make_in_maps(inputs):
    x = np.asarray(inputs["L0_attn_x"], np.float32)
    Wq = np.asarray(inputs["L0_attn_Wq"], np.float32)
    bq = np.asarray(inputs["L0_attn_bq"], np.float32)
    Wk = np.asarray(inputs["L0_attn_Wk"], np.float32)
    bk = np.asarray(inputs["L0_attn_bk"], np.float32)
    Wv = np.asarray(inputs["L0_attn_Wv"], np.float32)
    bv = np.asarray(inputs["L0_attn_bv"], np.float32)
    Wo = np.ascontiguousarray(np.asarray(inputs["L0_attn_Wo"], np.float32))
    bo = np.asarray(inputs["L0_attn_bo"], np.float32)
    rw = np.asarray(inputs["L0_ffn_router_w"], np.float32)
    rb = np.asarray(inputs["L0_ffn_router_b"], np.float32)
    ew = np.ascontiguousarray(np.asarray(inputs["L0_ffn_expert_w"], np.float32))
    eb = np.asarray(inputs["L0_ffn_expert_b"], np.float32)
    ow = np.ascontiguousarray(np.asarray(inputs["L0_ffn_out_w"], np.float32))
    ob = np.asarray(inputs["L0_ffn_out_b"], np.float32)

    eye = np.eye(128, dtype=np.float32)
    maskup = np.ascontiguousarray(np.triu(np.ones((128, 128), np.float32)))
    rw_t = np.ascontiguousarray(rw.reshape(16, 128, 8).transpose(1, 0, 2).reshape(128, 128))
    bo_col = np.ascontiguousarray(bo.reshape(16, 128).T)
    ob_col = np.ascontiguousarray(ob.reshape(16, 128).T)
    eb_row = np.ascontiguousarray(eb.reshape(1, 8 * 2048))
    rb_row = np.ascontiguousarray(rb.reshape(1, 8))

    in_maps = []
    for core in range(8):
        b, hh = core // 2, core % 2
        cols = slice(hh * 1024, (hh + 1) * 1024)
        sel = np.zeros((128, 2), np.float32)
        sel[:, hh] = 1.0
        in_maps.append(dict(
            xT=np.ascontiguousarray(x[b].T),
            xR=np.ascontiguousarray(x[b, hh * 2048:(hh + 1) * 2048].T),
            wq=np.ascontiguousarray(Wq[:, cols]),
            wk=np.ascontiguousarray(Wk[:, cols]),
            wv=np.ascontiguousarray(Wv[:, cols]),
            bq_col=np.ascontiguousarray(bq[cols].reshape(8, 128).T),
            bk_row=np.ascontiguousarray(bk[cols].reshape(1, 1024)),
            bv_row=np.ascontiguousarray(bv[cols].reshape(1, 1024)),
            wo=Wo,
            bo_col=bo_col,
            rw_t=rw_t,
            rb_row=rb_row,
            ew=ew,
            eb_row=eb_row,
            ow=ow,
            ob_col=ob_col,
            maskup=maskup,
            eye=eye,
            sel_col=sel,
        ))
    return in_maps


def kernel(**inputs):
    in_maps = make_in_maps(inputs)
    nc = build(debug=0)
    res = run_bass_kernel_spmd(nc, in_maps, core_ids=list(range(8)))
    out = np.empty((4, 4096, 2048), np.float32)
    for core in range(8):
        b, hh = core // 2, core % 2
        out[b, hh * 2048:(hh + 1) * 2048, :] = res.results[core]["out_fm"].T
    return out


# version 11 (final)
# speedup vs baseline: 1.1366x; 1.1011x over previous
import numpy as np
from contextlib import ExitStack
import concourse.bass as bass
import concourse.mybir as mybir
from concourse.bass_utils import run_bass_kernel_spmd

F32 = mybir.dt.float32
BF16 = mybir.dt.bfloat16
AF = mybir.ActivationFunctionType
ALU = mybir.AluOpType
AX = mybir.AxisListType

DM = 2048
SEQ = 4096
TLOC = 2048
HL = 8
NCH = 32
GRP = 4
NG = NCH // GRP
EPS_ATT = 1e-6
EPS_LN = 1e-5


def build(debug=0, phases='full', ngroups=NG, do_rec=5, do_cc=True,
          PSUM_ROUND=True, WO_PRELOAD=False):
    nc = bass.Bass("TRN2", target_bir_lowering=False)

    xT = nc.dram_tensor("xT", [DM, SEQ], F32, kind="ExternalInput").ap()
    xR = nc.dram_tensor("xR", [DM, TLOC], F32, kind="ExternalInput").ap()
    wq_d = nc.dram_tensor("wq", [DM, 1024], F32, kind="ExternalInput").ap()
    wk_d = nc.dram_tensor("wk", [DM, 1024], F32, kind="ExternalInput").ap()
    wv_d = nc.dram_tensor("wv", [DM, 1024], F32, kind="ExternalInput").ap()
    bq_d = nc.dram_tensor("bq_col", [128, 8], F32, kind="ExternalInput").ap()
    bk_d = nc.dram_tensor("bk_row", [1, 1024], F32, kind="ExternalInput").ap()
    bv_d = nc.dram_tensor("bv_row", [1, 1024], F32, kind="ExternalInput").ap()
    wo_d = nc.dram_tensor("wo", [DM, DM], F32, kind="ExternalInput").ap()
    bo_d = nc.dram_tensor("bo_col", [128, 16], F32, kind="ExternalInput").ap()
    rw_d = nc.dram_tensor("rw_t", [128, 128], F32, kind="ExternalInput").ap()
    rb_d = nc.dram_tensor("rb_row", [1, 8], F32, kind="ExternalInput").ap()
    ew_d = nc.dram_tensor("ew", [8, DM, DM], F32, kind="ExternalInput").ap()
    eb_d = nc.dram_tensor("eb_row", [1, 8 * DM], F32, kind="ExternalInput").ap()
    ow_d = nc.dram_tensor("ow", [DM, DM], F32, kind="ExternalInput").ap()
    ob_d = nc.dram_tensor("ob_col", [128, 16], F32, kind="ExternalInput").ap()
    mask_d = nc.dram_tensor("maskup", [128, 128], F32, kind="ExternalInput").ap()
    eye_d = nc.dram_tensor("eye", [128, 128], F32, kind="ExternalInput").ap()
    sel_d = nc.dram_tensor("sel_col", [128, 2], F32, kind="ExternalInput").ap()
    out_d = nc.dram_tensor("out_fm", [DM, TLOC], F32, kind="ExternalOutput").ap()

    cc_in = nc.dram_tensor("cc_in", [DM, TLOC], BF16, kind="Internal")
    cc_out = nc.dram_tensor("cc_out", [2 * DM, TLOC], BF16, kind="Internal")

    dbg = {}
    if debug & 1:
        dbg["ao"] = nc.dram_tensor("dbg_ao", [2 * DM, TLOC], F32, kind="ExternalOutput").ap()
    if debug & 2:
        dbg["h"] = nc.dram_tensor("dbg_h", [128, 16 * TLOC], F32, kind="ExternalOutput").ap()
    if debug & 4:
        dbg["gates"] = nc.dram_tensor("dbg_gates", [128, 16 * 8], F32, kind="ExternalOutput").ap()
    if debug & 8:
        dbg["acc"] = nc.dram_tensor("dbg_acc", [128, 16 * DM], F32, kind="ExternalOutput").ap()

    with ExitStack() as ctx:
        ab = ctx.enter_context(nc.sbuf_tensor("ab", [128, 93320], BF16))
        af = ctx.enter_context(nc.sbuf_tensor("af", [128, 5648], F32))
        eye_b = ctx.enter_context(nc.sbuf_tensor("eye_b", [128, 128], BF16))
        mask_f = ctx.enter_context(nc.sbuf_tensor("mask_f", [128, 128], F32))
        rw_b = ctx.enter_context(nc.sbuf_tensor("rw_b", [128, 128], BF16))
        gates_sb = ctx.enter_context(nc.sbuf_tensor("gates_sb", [128, 128], F32))
        bq_sb = ctx.enter_context(nc.sbuf_tensor("bq_sb", [128, 8], F32))
        bo_sb = ctx.enter_context(nc.sbuf_tensor("bo_sb", [128, 16], F32))
        ob_sb = ctx.enter_context(nc.sbuf_tensor("ob_sb", [128, 16], F32))
        sel_sb = ctx.enter_context(nc.sbuf_tensor("sel_sb", [128, 2], F32))
        onec_b = ctx.enter_context(nc.sbuf_tensor("onec_b", [128, 1], BF16))

        A_WQ, A_WK = 0, 16384
        A_H = 0
        B_WV = 32768
        B_AO, B_XS = 32768, 40960
        B_WE0, B_WE1 = 32768, 40960
        B_COMB, B_S2 = 32768, 40960
        C_X0, C_X1 = 49152, 57344
        C_QF, C_KF, C_VC = 65536, 69632, 73728
        C_AOT0, C_AOT1 = 77888, 78912
        C_WO = 49152
        C_ACC = 49152
        D_SNAP = 81920
        D_KFF0, D_KFF1 = 83008, 83520
        D_AOF = (84032, 85056, 86080, 87104)
        D_ASB0, D_ASB1 = 88192, 88704
        D_S1 = 81920
        D_SQ0, D_SQ1 = 90112, 90624
        D_GL0, D_GL1 = 81920, 82432
        D_OW0, D_OW1 = 81920, 83968
        S_EBST = 91136
        AO_W0A, AO_W1A = 91136, 91648
        AO_W0B, AO_W1B = 92160, 92672
        S_BK = 91136
        S_BV = 92160
        S_ONER = 93320 - 136
        S_RB = 93320 - 8

        F_XST = 0
        F_WST = 1024
        F_RE = 3072
        ST0, ST1, ST2 = 4096, 4608, 5120
        F_Z = 5632

        ps_state = ctx.enter_context(nc.psum_tensor("ps_state", [128, 1536], F32))
        ps_proj = ctx.enter_context(nc.psum_tensor("ps_proj", [128, 1024], F32))
        ps_small = ctx.enter_context(nc.psum_tensor("ps_small", [128, 1024], F32))
        ps_bf = ctx.enter_context(nc.psum_tensor("ps_bf", [128, 512], BF16))

        sems = {}
        SEM_NAMES = ("T", "V", "S", "C", "ST", "W0", "W1", "W2", "W3",
                     "X0", "X1", "P0", "P1", "P2", "P3", "O0", "O1", "A0", "A1", "EB", "DG0", "DG1")
        for s in SEM_NAMES:
            sems[s] = ctx.enter_context(nc.semaphore("sem" + s))

        SEME = {"tensor": "T", "vector": "V", "scalar": "S"}
        cnt = {s: 0 for s in SEM_NAMES}
        ops = {k: [] for k in ("sync", "gpsimd", "tensor", "vector", "scalar")}
        lastw = {}

        def _wait(eng, sem, val):
            if val <= 0:
                return
            if lastw.get((eng, sem), 0) >= val:
                return
            lastw[(eng, sem)] = val
            ops[eng].append(lambda e, s=sem, v=val: e.wait_ge(sems[s], v))

        def W(eng, *toks):
            best = {}
            for t in toks:
                if not t:
                    continue
                for u in (t if isinstance(t, list) else [t]):
                    if u:
                        best[u[0]] = max(best.get(u[0], 0), u[1])
            for s, v in best.items():
                _wait(eng, s, v)

        BUCK = 64
        track = {}
        BSZ = {}

        def _rng(ap):
            t = ap.tensor
            nm = t.name
            is_dram = "DRam" in type(t).__name__
            if nm not in BSZ:
                tot = 1
                for d in t.shape:
                    tot *= int(d)
                BSZ[nm] = max(BUCK, tot >> 6) if is_dram else BUCK
            rowlen = int(t.shape[-1])
            lo = int(ap.offset)
            hi = lo + 1
            skipped = is_dram
            for step, count in ap.ap:
                step = int(step); count = int(count)
                if not skipped and step == rowlen and len(t.shape) > 1:
                    skipped = True
                    continue
                hi += (count - 1) * step
            return (nm, lo, hi)

        def _collect(entry, key, deps):
            d = entry.get(key)
            if d:
                for s_, v_ in d.items():
                    if deps.get(s_, 0) < v_:
                        deps[s_] = v_

        def _deps_and_reg(rd_aps, wr_aps, token, wr_extra=None):
            deps = {}
            rranges = [_rng(ap) for ap in rd_aps]
            wranges = [_rng(ap) for ap in wr_aps]
            if wr_extra is not None:
                wranges = [wr_extra] + [w for w in wranges]
            for nm, lo, hi in rranges:
                d = track.get(nm)
                if d:
                    bz = BSZ[nm]
                    for b in range(lo // bz, (hi - 1) // bz + 1):
                        e = d.get(b)
                        if e:
                            _collect(e, "w", deps)
            for nm, lo, hi in wranges:
                d = track.get(nm)
                if d:
                    bz = BSZ[nm]
                    for b in range(lo // bz, (hi - 1) // bz + 1):
                        e = d.get(b)
                        if e:
                            _collect(e, "w", deps)
                            _collect(e, "r", deps)
            s_, v_ = token
            for nm, lo, hi in wranges:
                d = track.setdefault(nm, {})
                bz = BSZ[nm]
                for b in range(lo // bz, (hi - 1) // bz + 1):
                    d[b] = {"w": {s_: v_}, "r": {}}
            for nm, lo, hi in rranges:
                d = track.setdefault(nm, {})
                bz = BSZ[nm]
                for b in range(lo // bz, (hi - 1) // bz + 1):
                    e = d.setdefault(b, {"w": {}, "r": {}})
                    if e["r"].get(s_, 0) < v_:
                        e["r"][s_] = v_
            return deps

        RW = {
            "tensor_copy": (1, (1,)), "tensor_tensor": (1, (1, 2)),
            "tensor_scalar": (1, (1, 2, 3)), "tensor_single_scalar": (1, (1, 2)),
            "scalar_tensor_tensor": (1, (1, 2, 3)),
            "tensor_scalar_add": (1, (1, 2)), "tensor_scalar_sub": (1, (1, 2)),
            "tensor_scalar_mul": (1, (1, 2)), "tensor_scalar_min": (1, (1, 2)),
            "tensor_scalar_max": (1, (1, 2)),
            "reduce_max": (1, (1,)), "reduce_sum": (1, (1,)), "reciprocal": (1, (1,)),
            "memset": (1, ()), "memzero": (1, ()),
            "activation": (1, (1,)), "copy": (1, (1,)), "add": (1, (1, 2)),
            "mul": (1, (1, 2)), "square": (1, (1,)), "sqrt": (1, (1,)),
            "matmul": (1, (0, 1, 2)), "transpose": (1, (1, 2)),
        }

        def _is_ap(x):
            return hasattr(x, "tensor") and hasattr(x, "offset")

        def OP(eng, meth, *a, inc=True, **kw):
            s = SEME[eng]
            cnt[s] += 1
            v = cnt[s]
            token = (s, v)
            nw, rds = RW[meth]
            wr_aps = [x for x in a[:nw] if _is_ap(x)]
            rd_aps = [a[i] for i in rds if i < len(a) and _is_ap(a[i])]
            rd_aps += [x for x in kw.values() if _is_ap(x) and not isinstance(x, bool)]
            wr_extra = None
            if PSUM_ROUND and meth in ("matmul", "transpose") and kw.get("start", True):
                t = wr_aps[0].tensor
                if "PSum" in type(t).__name__:
                    esz = mybir.dt.size(t.dtype)
                    region = 2048 // esz
                    nm, lo, hi = _rng(wr_aps[0])
                    lo = (lo // region) * region
                    hi = ((hi + region - 1) // region) * region
                    wr_extra = (nm, lo, hi)
            deps = _deps_and_reg(rd_aps, wr_aps, token, wr_extra)
            if eng == "tensor":
                deps.pop("T", None)
            for s_, v_ in deps.items():
                _wait(eng, s_, v_)
            ops[eng].append(lambda e, m=meth, a=a, kw=kw, s=s: getattr(e, m)(*a, **kw).then_inc(sems[s], 1))
            return token

        def DMA(q, s, out, in_, record=True):
            cnt[s] += 16
            v = cnt[s]
            token = (s, v)
            if record:
                deps = _deps_and_reg([in_], [out], token)
            else:
                deps = _deps_and_reg([in_], [], token)
            for s_, v_ in deps.items():
                _wait(q, s_, v_)
            ops[q].append(lambda e, o=out, i=in_, s=s: e.dma_start(out=o, in_=i).then_inc(sems[s], 16))
            return token

        def REG(rd_aps, wr_aps, token, eng):
            deps = _deps_and_reg(rd_aps, wr_aps, token)
            for s_, v_ in deps.items():
                _wait(eng, s_, v_)

        def REREG_W(aps, token):
            s_, v_ = token
            for ap in aps:
                nm, lo, hi = _rng(ap)
                d = track.setdefault(nm, {})
                bz = BSZ[nm]
                for b in range(lo // bz, (hi - 1) // bz + 1):
                    d[b] = {"w": {s_: v_}, "r": {}}

        class Rot:
            def __init__(self, items):
                self.items = items
                self.busy = [[] for _ in items]
                self.i = 0

            def acquire(self):
                j = self.i % len(self.items)
                self.i += 1
                b = self.busy[j]
                self.busy[j] = []
                return j, self.items[j], b

            def use(self, j, tok):
                if tok:
                    self.busy[j].append(tok)

        def abr(c0, n):
            return ab[:, c0:c0 + n]

        def afr(c0, n):
            return af[:, c0:c0 + n]

        t_eye_l = DMA("sync", "ST", af[:, F_WST:F_WST + 128], eye_d)
        t_mask_l = DMA("sync", "ST", mask_f[:, :], mask_d)
        t_rw_l = DMA("sync", "ST", af[:, F_WST + 128:F_WST + 256], rw_d)
        DMA("sync", "ST", bq_sb[:, :], bq_d)
        DMA("sync", "ST", bo_sb[:, :], bo_d)
        DMA("sync", "ST", ob_sb[:, :], ob_d)
        t_sel_l = DMA("sync", "ST", sel_sb[:, :], sel_d)
        t_bk_l = DMA("sync", "ST", af[0:1, F_WST + 256:F_WST + 1280], bk_d)
        t_bv_l = DMA("sync", "ST", af[0:1, F_XST:F_XST + 1024], bv_d)
        t_rb_l = DMA("sync", "ST", af[0:1, F_Z + 8:F_Z + 16], rb_d)
        st_all = ("ST", cnt["ST"])
        t_eye_l = t_mask_l = t_rw_l = t_sel_l = t_bk_l = t_bv_l = t_rb_l = t_bq_l = st_all
        REREG_W([af[:, F_WST:F_WST + 256], mask_f[:, :], bq_sb[:, :], bo_sb[:, :],
                 ob_sb[:, :], sel_sb[:, :], af[0:1, F_WST + 256:F_WST + 1280],
                 af[0:1, F_XST:F_XST + 1024], af[0:1, F_Z + 8:F_Z + 16]], st_all)
        W("vector", st_all)
        t_eye_c = OP("vector", "tensor_copy", eye_b[:, :], af[:, F_WST:F_WST + 128], inc=True)
        t_rw_c = OP("vector", "tensor_copy", rw_b[:, :], af[:, F_WST + 128:F_WST + 256], inc=True)
        t_bk_c = OP("vector", "tensor_copy", ab[0:1, S_BK:S_BK + 1024], af[0:1, F_WST + 256:F_WST + 1280], inc=True)
        t_bv_c = OP("vector", "tensor_copy", ab[0:1, S_BV:S_BV + 1024], af[0:1, F_XST:F_XST + 1024], inc=True)
        t_rb_c = OP("vector", "tensor_copy", ab[0:1, S_RB:S_RB + 8], af[0:1, F_Z + 8:F_Z + 16], inc=True)
        t_oner = OP("vector", "memset", ab[0:1, S_ONER:S_ONER + 128], 1.0, inc=True)
        t_onec = OP("vector", "memset", onec_b[:, :], 1.0, inc=True)
        t_eps = OP("vector", "memset", af[0:1, F_Z + 4:F_Z + 5], EPS_LN, inc=True)
        vc_ones = abr(C_VC, 32 * 129).rearrange("p (s c) -> p s c", c=129)[:, :, 128:129]
        t_vcone = OP("vector", "memset", vc_ones, 1.0, inc=True)

        wst = Rot([afr(F_WST, 512), afr(F_WST + 512, 512),
                   afr(F_WST + 1024, 512), afr(F_WST + 1536, 512)])
        w_tok = {}
        first_w = True
        for name, wd, dst in (("q", wq_d, A_WQ), ("k", wk_d, A_WK), ("v", wv_d, B_WV)):
            for k in range(16):
                for half in range(2):
                    j, st, free = wst.acquire()
                    if first_w:
                        W("sync", t_bk_c, t_bv_c, t_eye_c, t_rw_c)
                        first_w = False
                    W("sync", free)
                    tl = DMA("sync", "W%d" % j, st, wd[k * 128:(k + 1) * 128, half * 512:(half + 1) * 512])
                    W("vector", tl)
                    tc = OP("vector", "tensor_copy", abr(dst + k * 1024 + half * 512, 512), st, inc=True)
                    wst.use(j, tc)
                    w_tok[(name, k, half)] = tc

        xst = Rot([afr(F_XST, 512), afr(F_XST + 512, 512)])
        xbuf = Rot([C_X0, C_X1])
        proj = Rot([ps_proj[:, 0:512], ps_proj[:, 512:1024]])
        re_rot = Rot([(afr(F_RE, 512), afr(F_RE + 512, 512))])
        kfT = Rot([ps_bf[:, 0:128], ps_bf[:, 128:256], ps_bf[:, 256:384], ps_bf[:, 384:512]])
        aT = Rot([ps_small[:, 0:128], ps_small[:, 128:256]])
        outp = Rot([ps_small[:, 256:385], ps_small[:, 512:641]])
        kff = Rot([D_KFF0, D_KFF1])
        aofm = Rot(list(D_AOF))
        asb = Rot([D_ASB0, D_ASB1])
        aotm = Rot([C_AOT0, C_AOT1])
        zscr = Rot([(af[:, F_Z:F_Z + 1], af[:, F_Z + 1:F_Z + 2]),
                    (af[:, F_Z + 2:F_Z + 3], af[:, F_Z + 3:F_Z + 4])])

        snap_ap = abr(D_SNAP, 1032)

        def state_slot(h):
            o = (h // 3) * 512 + (h % 3) * 129
            return ps_state[:, o:o + 129]

        def snap_slot(h):
            return abr(D_SNAP + h * 129, 129)

        qkv_read_tok = []
        snap_read_tok = []
        kv_mm_tok = []

        for g in range(ngroups):
            jx, xbase, xfree = xbuf.acquire()
            xcast_tok = []
            for k in range(16):
                js, st, sfree = xst.acquire()
                W("sync", sfree, t_bv_c if g == 0 and k == 0 else None)
                tl = DMA("sync", "X%d" % js, st, xT[k * 128:(k + 1) * 128, g * 512:(g + 1) * 512])
                W("vector", tl)
                if k == 0:
                    W("vector", xfree)
                tc = OP("vector", "tensor_copy", abr(xbase + k * 512, 512), st, inc=True)
                xst.use(js, tc)
                xcast_tok.append(tc)

            qf_writers = []
            for m in range(8):
                jp, pp, pfree = proj.acquire()
                W("tensor", pfree, xcast_tok)
                tmm = None
                for k in range(16):
                    W("tensor", w_tok[("q", k, m // 4)])
                    tmm = OP("tensor", "matmul", pp,
                             abr(A_WQ + k * 1024 + m * 128, 128),
                             abr(xbase + k * 512, 512),
                             start=(k == 0), stop=(k == 15),
                             skip_group_check=True, inc=(k == 15))
                jr, (rr, ee), rfree = re_rot.acquire()
                W("scalar", tmm, rfree, t_bq_l)
                ts_r = OP("scalar", "activation", rr, pp, AF.Relu, bias=bq_sb[:, m:m + 1], inc=True)
                ts_e = OP("scalar", "activation", ee, pp, AF.Exp, bias=bq_sb[:, m:m + 1], inc=True)
                W("vector", ts_r, ts_e, qkv_read_tok)
                tq = OP("vector", "scalar_tensor_tensor",
                        abr(C_QF + m * 512, 512), ee, 1.0, rr, ALU.min, ALU.add, inc=True)
                re_rot.use(jr, tq)
                proj.use(jp, ts_r)
                proj.use(jp, ts_e)
                qf_writers.append(tq)

            kf_w = [None] * (GRP * 2)
            vc_w = [None] * (GRP * 2)
            for cc in range(GRP):
                gc = g * GRP + cc
                for dt in range(2):
                    jp, pp, pfree = proj.acquire()
                    W("tensor", pfree, t_oner, t_bk_c)
                    OP("tensor", "matmul", pp,
                       ab[0:1, S_ONER:S_ONER + 128],
                       ab[0:1, S_BK + dt * 512:S_BK + dt * 512 + 512],
                       start=True, stop=False, skip_group_check=True)
                    tmm = None
                    for k in range(16):
                        W("tensor", w_tok[("k", k, dt)])
                        tmm = OP("tensor", "matmul", pp,
                                 abr(xbase + k * 512 + cc * 128, 128),
                                 abr(A_WK + k * 1024 + dt * 512, 512),
                                 start=False, stop=(k == 15),
                                 skip_group_check=True, inc=(k == 15))
                    jr, (rr, ee), rfree = re_rot.acquire()
                    W("scalar", tmm, rfree)
                    ts_r = OP("scalar", "activation", rr, pp, AF.Relu, inc=True)
                    ts_e = OP("scalar", "activation", ee, pp, AF.Exp, inc=True)
                    W("vector", ts_r, ts_e, qkv_read_tok)
                    tk = OP("vector", "scalar_tensor_tensor",
                            abr(C_KF + (cc * 8 + dt * 4) * 128, 512), ee, 1.0, rr,
                            ALU.min, ALU.add, inc=True)
                    re_rot.use(jr, tk)
                    proj.use(jp, ts_r)
                    proj.use(jp, ts_e)
                    kf_w[cc * 2 + dt] = tk
                for dt in range(2):
                    jp, pp, pfree = proj.acquire()
                    W("tensor", pfree, t_oner, t_bv_c)
                    OP("tensor", "matmul", pp,
                       ab[0:1, S_ONER:S_ONER + 128],
                       ab[0:1, S_BV + dt * 512:S_BV + dt * 512 + 512],
                       start=True, stop=False, skip_group_check=True)
                    tmm = None
                    for k in range(16):
                        W("tensor", w_tok[("v", k, dt)])
                        tmm = OP("tensor", "matmul", pp,
                                 abr(xbase + k * 512 + cc * 128, 128),
                                 abr(B_WV + k * 1024 + dt * 512, 512),
                                 start=False, stop=(k == 15),
                                 skip_group_check=True, inc=(k == 15))
                    W("vector", tmm, t_vcone, qkv_read_tok)
                    vbase = C_VC + (cc * 8 + dt * 4) * 129
                    vview = abr(vbase, 4 * 129).rearrange("p (s c) -> p s c", c=129)[:, :, 0:128]
                    tv = OP("vector", "tensor_copy", vview,
                            pp.rearrange("p (s c) -> p s c", c=128), inc=True)
                    proj.use(jp, tv)
                    vc_w[cc * 2 + dt] = tv

                if do_rec < 1:
                    continue
                if gc == 0:
                    W("vector", kv_mm_tok, snap_read_tok)
                    tsn = [OP("vector", "memset", snap_ap, 0.0, inc=True)]
                else:
                    W("scalar", kv_mm_tok, snap_read_tok)
                    tsn = [
                        OP("scalar", "copy", abr(D_SNAP, 387), ps_state[:, 0:387], inc=True),
                        OP("scalar", "copy", abr(D_SNAP + 387, 387), ps_state[:, 512:899], inc=True),
                        OP("scalar", "copy", abr(D_SNAP + 774, 258), ps_state[:, 1024:1282], inc=True),
                    ]
                snap_read_tok = []
                kv_mm_tok = []

                for h in range(HL):
                    W("tensor", tsn, kf_w[cc * 2 + h // 4], vc_w[cc * 2 + h // 4])
                    tkv = OP("tensor", "matmul", state_slot(h),
                             abr(C_KF + (cc * 8 + h) * 128, 128),
                             abr(C_VC + (cc * 8 + h) * 129, 129),
                             start=(gc == 0 and h in (0, 3, 6)),
                             stop=(gc == NCH - 1 and h in (2, 5, 7)),
                             skip_group_check=True, inc=True)
                    kv_mm_tok.append(tkv)

                if do_rec < 2:
                    continue
                ja, aob, aofree = aotm.acquire()
                kfb_of = {}
                for b in range(2):
                    ttr = None
                    for hh in range(4):
                        h = b * 4 + hh
                        W("tensor", kf_w[cc * 2 + h // 4], t_eye_c)
                        ttr = OP("tensor", "transpose", ps_bf[:, hh * 128:(hh + 1) * 128],
                                 abr(C_KF + (cc * 8 + h) * 128, 128), eye_b[:, :], inc=True)
                    jf, kfb, kffree = kff.acquire()
                    W("vector", ttr, kffree)
                    tkc = OP("vector", "tensor_copy", abr(kfb, 512), ps_bf[:, 0:512], inc=True)
                    for hh in range(4):
                        kfb_of[b * 4 + hh] = (jf, kfb + hh * 128, tkc)
                amask = {}
                for h in range(HL):
                    jf, kfb_off, tkc = kfb_of[h]
                    jA, Aps, Afree = aT.acquire()
                    W("tensor", Afree, tkc, qf_writers[h])
                    tA = OP("tensor", "matmul", Aps,
                            abr(kfb_off, 128),
                            abr(C_QF + h * 512 + cc * 128, 128),
                            start=True, stop=True, skip_group_check=True, inc=True)
                    kff.use(jf, tA)
                    jm, Ab, Afree2 = asb.acquire()
                    W("vector", tA, Afree2, t_mask_l)
                    tAm = OP("vector", "tensor_tensor", abr(Ab, 128), Aps, mask_f[:, :], ALU.mult, inc=True)
                    aT.use(jA, tAm)
                    amask[h] = (jm, Ab, tAm)
                norm_toks = []
                for h in range(HL):
                    jm, Ab, tAm = amask[h]
                    if do_rec < 3:
                        norm_toks.append(tAm)
                        continue
                    jo, ops_, ofree = outp.acquire()
                    W("tensor", ofree, tAm, tsn, qf_writers[h], vc_w[cc * 2 + h // 4])
                    OP("tensor", "matmul", ops_,
                       abr(C_QF + h * 512 + cc * 128, 128), snap_slot(h),
                       start=True, stop=False, skip_group_check=True)
                    tio = OP("tensor", "matmul", ops_,
                             abr(Ab, 128),
                             abr(C_VC + (cc * 8 + h) * 129, 129),
                             start=False, stop=True, skip_group_check=True, inc=True)
                    snap_read_tok.append(tio)
                    asb.use(jm, tio)
                    jz, (zb, rzb), zfree = zscr.acquire()
                    W("vector", tio, zfree)
                    OP("vector", "tensor_scalar_add", zb, ops_[:, 128:129], EPS_ATT)
                    OP("vector", "reciprocal", rzb, zb)
                    if h == 0:
                        W("vector", aofree)
                    tn = OP("vector", "tensor_scalar_mul",
                            abr(aob + h * 128, 128), ops_[:, 0:128], rzb, inc=True)
                    zscr.use(jz, tn)
                    outp.use(jo, tn)
                    norm_toks.append(tn)

                if do_rec < 4:
                    continue
                jaf, afb, affree = aofm.acquire()
                wj = gc // 16
                lcol = (gc % 16) * 128
                aof_toks = []
                for b in range(2):
                    ttr = None
                    for hh in range(4):
                        h = b * 4 + hh
                        W("tensor", norm_toks[h], t_eye_c)
                        ttr = OP("tensor", "transpose", ps_bf[:, hh * 128:(hh + 1) * 128],
                                 abr(aob + h * 128, 128), eye_b[:, :], inc=True)
                    aotm.use(ja, ttr)
                    W("vector", ttr)
                    if b == 0:
                        W("vector", affree)
                    tac = OP("vector", "tensor_copy", abr(afb + b * 512, 512), ps_bf[:, 0:512], inc=True)
                    aof_toks.append(tac)
                if do_rec >= 5:
                    W("gpsimd", aof_toks)
                    td = None
                    for hh2 in range(2):
                        td = DMA("gpsimd", "A%d" % (jaf % 2),
                                 cc_in.ap()[wj * 1024 + hh2 * 512:wj * 1024 + (hh2 + 1) * 512,
                                            lcol:lcol + 128].rearrange("(h p) c -> p h c", p=128),
                                 abr(afb + hh2 * 512, 512).rearrange("p (h c) -> p h c", c=128))
                    aofm.use(jaf, td)
                    REG([abr(afb, 1024)], [], td, "gpsimd")
                else:
                    for tac in aof_toks:
                        aofm.use(jaf, tac)

            last_t = ("T", cnt["T"])
            xbuf.use(jx, last_t)
            qkv_read_tok = [last_t]

        _wait("gpsimd", "A0", cnt["A0"])
        _wait("gpsimd", "A1", cnt["A1"])
        if do_cc:
            cnt["C"] += 1
            REG([cc_in.ap()], [cc_out.ap()], ("C", cnt["C"]), "gpsimd")
            ops["gpsimd"].append(lambda e: e.collective_compute(
                "AllGather", ALU.bypass,
                replica_groups=[[0, 1], [2, 3], [4, 5], [6, 7]],
                ins=[cc_in.ap()], outs=[cc_out.ap()],
            ).then_inc(sems["C"], 1))
        else:
            nrows = 1024 if ngroups <= 4 else 2048
            ncols = min(ngroups, 4) * 512
            cnt["C"] += 16
            REG([cc_in.ap()], [cc_out.ap()], ("C", cnt["C"]), "gpsimd")
            ops["gpsimd"].append(lambda e, nr=nrows, nc_=ncols: e.dma_start(
                out=cc_out.ap()[0:nr, 0:nc_], in_=cc_in.ap()[0:nr, 0:nc_]).then_inc(sems["C"], 16))
            cnt["C"] += 16
            ops["gpsimd"].append(lambda e, nr=nrows, nc_=ncols: e.dma_start(
                out=cc_out.ap()[DM:DM + nr, 0:nc_], in_=cc_in.ap()[0:nr, 0:nc_]).then_inc(sems["C"], 16))
        t_cc0 = ("C", cnt["C"])
        t_cc = ("C", cnt["C"])
        attn_end = [("T", cnt["T"]), ("V", cnt["V"]), ("S", cnt["S"])]

        if debug & 1:
            _dbsc = [D_KFF0, D_AOF[0]]
            _dfsc = [F_RE, F_RE + 512]
            _prevc = [None, None]
            _prevs = [None, None]
            for _r in range(32):
                for _c in range(4):
                    _j = (_r * 4 + _c) % 2
                    W("gpsimd", t_cc, attn_end, _prevc[_j])
                    _tl = DMA("gpsimd", "DG%d" % _j, abr(_dbsc[_j], 512),
                              cc_out.ap()[_r * 128:(_r + 1) * 128, _c * 512:(_c + 1) * 512])
                    W("vector", _tl, _prevs[_j])
                    _tc = OP("vector", "tensor_copy", afr(_dfsc[_j], 512), abr(_dbsc[_j], 512), inc=True)
                    _prevc[_j] = _tc
                    W("gpsimd", _tc)
                    _ts = DMA("gpsimd", "DG%d" % _j, dbg["ao"][_r * 128:(_r + 1) * 128, _c * 512:(_c + 1) * 512],
                              afr(_dfsc[_j], 512))
                    _prevs[_j] = _ts


        if phases == "attn":
            W("vector", attn_end)
            tz = OP("vector", "memset", afr(F_RE, 512), 0.0, inc=True)
            W("sync", tz, t_cc)
            for m_ in range(16):
                for tq_ in range(4):
                    DMA("sync", "O%d" % (tq_ % 2), out_d[m_ * 128:(m_ + 1) * 128, tq_ * 512:(tq_ + 1) * 512], afr(F_RE, 512))

        if phases == "full":
            def ao_rows(j, k):
                r0 = (k // 8) * 2048 + j * 1024 + (k % 8) * 128
                return cc_out.ap()[r0:r0 + 128, :]

            wo_tok = {}
            first_post = True
            for k in range(16):
                for half in range(4):
                    j, st, free = wst.acquire()
                    if first_post and not WO_PRELOAD:
                        W("sync", attn_end)
                        W("vector", attn_end)
                        first_post = False
                    W("sync", free)
                    tl = DMA("sync", "W%d" % j, st, wo_d[k * 128:(k + 1) * 128, half * 512:(half + 1) * 512])
                    W("vector", tl)
                    tc = OP("vector", "tensor_copy", abr(C_WO + k * 2048 + half * 512, 512), st, inc=True)
                    wst.use(j, tc)
                    wo_tok[(k, half)] = tc

            aopair = Rot([(AO_W0A, AO_W1A), (AO_W0B, AO_W1B)])
            xst2 = Rot([afr(F_XST, 512), afr(F_XST + 512, 512)])
            sqr = Rot([D_SQ0, D_SQ1])
            outr = Rot([afr(F_RE, 512), afr(F_RE + 512, 512)])
            stats_s = ps_small[0:1, 0:512]
            stats_q = ps_small[0:1, 512:1024]
            bc_r = ps_state[:, 0:512]
            bc_mr = ps_state[:, 512:1024]

            h_tok = {}
            prev_b1_tmm = None
            for tt in range(4):
                ao_tok = {}
                x_tok = {}
                for k in range(16):
                    jj, (w0c, w1c), pfree = aopair.acquire()
                    W("sync", pfree, t_cc)
                    if tt == 0 and k == 0:
                        W("sync", attn_end)
                    t0l = DMA("sync", "P%d" % (2 * jj), abr(w0c, 512), ao_rows(0, k)[:, tt * 512:(tt + 1) * 512])
                    t1l = DMA("sync", "P%d" % (2 * jj + 1), abr(w1c, 512), ao_rows(1, k)[:, tt * 512:(tt + 1) * 512])
                    W("vector", t0l, t1l, t_sel_l, prev_b1_tmm)
                    OP("vector", "tensor_scalar_mul", abr(B_AO + k * 512, 512), abr(w0c, 512), sel_sb[:, 0:1])
                    tcmb = OP("vector", "scalar_tensor_tensor", abr(B_AO + k * 512, 512),
                              abr(w1c, 512), sel_sb[:, 1:2], abr(B_AO + k * 512, 512),
                              ALU.mult, ALU.add, inc=True)
                    aopair.use(jj, tcmb)
                    ao_tok[k] = tcmb
                    js, st, sfree = xst2.acquire()
                    W("sync", sfree)
                    tl = DMA("sync", "X%d" % js, st, xR[k * 128:(k + 1) * 128, tt * 512:(tt + 1) * 512])
                    W("vector", tl)
                    tc = OP("vector", "tensor_copy", abr(B_XS + k * 512, 512), st, inc=True)
                    xst2.use(js, tc)
                    x_tok[k] = tc

                stat_mm_s = stat_mm_q = None
                s1_toks = []
                for m in range(16):
                    jp, pp, pfree = proj.acquire()
                    W("tensor", pfree, x_tok[m], t_eye_c)
                    OP("tensor", "matmul", pp, eye_b[:, :], abr(B_XS + m * 512, 512),
                       start=True, stop=False, skip_group_check=True)
                    tmm = None
                    for k in range(16):
                        W("tensor", ao_tok[k], wo_tok[(k, m // 4)])
                        tmm = OP("tensor", "matmul", pp,
                                 abr(C_WO + k * 2048 + m * 128, 128),
                                 abr(B_AO + k * 512, 512),
                                 start=False, stop=(k == 15),
                                 skip_group_check=True, inc=(k == 15))
                    if m == 15:
                        prev_b1_tmm = tmm
                    W("scalar", tmm)
                    ts1 = OP("scalar", "activation", abr(D_S1 + m * 512, 512), pp, AF.Identity,
                             bias=bo_sb[:, m:m + 1], inc=True)
                    jq, sqb, sqfree = sqr.acquire()
                    W("scalar", sqfree)
                    tsq = OP("scalar", "activation", abr(sqb, 512), pp, AF.Square,
                             bias=bo_sb[:, m:m + 1], inc=True)
                    proj.use(jp, ts1)
                    proj.use(jp, tsq)
                    W("tensor", ts1, tsq, t_onec)
                    stat_mm_s = OP("tensor", "matmul", stats_s, onec_b[:, :], abr(D_S1 + m * 512, 512),
                                   start=(m == 0), stop=(m == 15), skip_group_check=True, inc=True)
                    stat_mm_q = OP("tensor", "matmul", stats_q, onec_b[:, :], abr(sqb, 512),
                                   start=(m == 0), stop=(m == 15), skip_group_check=True, inc=True)
                    sqr.use(jq, stat_mm_q)
                    s1_toks.append(ts1)

                W("scalar", stat_mm_s, stat_mm_q)
                OP("scalar", "copy", af[0:1, ST0:ST0 + 512], stats_s)
                OP("scalar", "copy", af[0:1, ST1:ST1 + 512], stats_q)
                OP("vector", "tensor_scalar_mul", af[0:1, ST0:ST0 + 512], af[0:1, ST0:ST0 + 512], 1.0 / DM)
                OP("vector", "tensor_scalar_mul", af[0:1, ST1:ST1 + 512], af[0:1, ST1:ST1 + 512], 1.0 / DM)
                OP("vector", "tensor_tensor", af[0:1, ST2:ST2 + 512], af[0:1, ST0:ST0 + 512],
                   af[0:1, ST0:ST0 + 512], ALU.mult)
                tvv = OP("vector", "tensor_tensor", af[0:1, ST1:ST1 + 512], af[0:1, ST1:ST1 + 512],
                         af[0:1, ST2:ST2 + 512], ALU.subtract, inc=True)
                W("scalar", tvv, t_eps)
                tsv = OP("scalar", "activation", af[0:1, ST1:ST1 + 512], af[0:1, ST1:ST1 + 512],
                         AF.Sqrt, bias=af[0:1, F_Z + 4:F_Z + 5], inc=True)
                W("vector", tsv)
                OP("vector", "reciprocal", af[0:1, ST1:ST1 + 512], af[0:1, ST1:ST1 + 512])
                tmr = OP("vector", "tensor_tensor", af[0:1, ST2:ST2 + 512], af[0:1, ST0:ST0 + 512],
                         af[0:1, ST1:ST1 + 512], ALU.mult, inc=True)
                W("tensor", tmr, t_mask_l)
                tb1 = OP("tensor", "matmul", bc_r, mask_f[0:1, 0:128], af[0:1, ST1:ST1 + 512],
                         start=True, stop=True, skip_group_check=True, inc=True)
                tb2 = OP("tensor", "matmul", bc_mr, mask_f[0:1, 0:128], af[0:1, ST2:ST2 + 512],
                         start=True, stop=True, skip_group_check=True, inc=True)
                for m in range(16):
                    W("vector", tb1, tb2, s1_toks[m])
                    OP("vector", "tensor_tensor", abr(B_XS + m * 512, 512),
                       abr(D_S1 + m * 512, 512), bc_r, ALU.mult)
                    th = OP("vector", "tensor_tensor", abr(A_H + m * 2048 + tt * 512, 512),
                            abr(B_XS + m * 512, 512), bc_mr, ALU.subtract, inc=True)
                    h_tok[(m, tt)] = th

            if debug & 2:
                _prevs = [None, None]
                for m in range(16):
                    for _c in range(4):
                        _j = _c % 2
                        W("vector", [h_tok[(m, q)] for q in range(4)], _prevs[_j])
                        _tc = OP("vector", "tensor_copy", afr(F_RE + _j * 512, 512),
                                 abr(A_H + m * 2048 + _c * 512, 512), inc=True)
                        W("sync", _tc)
                        _ts = DMA("sync", "DG%d" % _j, dbg["h"][:, m * 2048 + _c * 512:m * 2048 + (_c + 1) * 512],
                                  afr(F_RE + _j * 512, 512))
                        _prevs[_j] = _ts

            gate_tok = {}
            prev_texp = None
            for cc in range(16):
                jp, pp, pfree = proj.acquire()
                pp8 = pp[:, 0:8]
                W("tensor", pfree, t_oner, t_rb_c, t_rw_c,
                  [h_tok[(k, cc // 4)] for k in range(16)])
                OP("tensor", "matmul", pp8, ab[0:1, S_ONER:S_ONER + 128], ab[0:1, S_RB:S_RB + 8],
                   start=True, stop=False, skip_group_check=True)
                tmm = None
                for k in range(16):
                    tmm = OP("tensor", "matmul", pp8,
                             abr(A_H + k * 2048 + cc * 128, 128),
                             rw_b[:, k * 8:(k + 1) * 8],
                             start=False, stop=(k == 15), skip_group_check=True, inc=(k == 15))
                W("vector", tmm, prev_texp)
                m1n = af[:, F_Z:F_Z + 1]
                tm1 = OP("vector", "reduce_max", m1n, pp8, axis=AX.X, negate=True, inc=True)
                W("scalar", tm1)
                texp = OP("scalar", "activation", afr(ST0, 8), pp8, AF.Exp, bias=m1n, inc=True)
                prev_texp = texp
                proj.use(jp, tm1)
                proj.use(jp, texp)
                t_sb = afr(ST0, 8)
                W("vector", texp)
                e1 = af[:, F_Z + 1:F_Z + 2]
                OP("vector", "reduce_max", e1, t_sb, axis=AX.X)
                lt = afr(ST0 + 8, 8)
                OP("vector", "tensor_single_scalar", lt, t_sb, e1, ALU.is_lt)
                tmsk = afr(ST0 + 16, 8)
                OP("vector", "tensor_tensor", tmsk, t_sb, lt, ALU.mult)
                e2 = af[:, F_Z + 2:F_Z + 3]
                OP("vector", "reduce_max", e2, tmsk, axis=AX.X)
                keep = afr(ST0 + 8, 8)
                OP("vector", "tensor_single_scalar", keep, t_sb, e2, ALU.is_ge)
                den = af[:, F_Z + 3:F_Z + 4]
                OP("vector", "tensor_tensor", den, e1, e2, ALU.add)
                OP("vector", "reciprocal", den, den)
                OP("vector", "tensor_tensor", t_sb, t_sb, keep, ALU.mult)
                tg = OP("vector", "tensor_scalar_mul", gates_sb[:, cc * 8:cc * 8 + 8], t_sb, den, inc=True)
                gate_tok[cc] = tg

            if debug & 4:
                W("sync", [gate_tok[c] for c in range(16)])
                DMA("sync", "DG0", dbg["gates"], gates_sb[:, :])

            web = Rot([B_WE0, B_WE1])
            glr = Rot([D_GL0, D_GL1])
            acc_tok = {}
            prev_teb = None
            prev_e_last_mm = None
            for e in range(8):
                W("gpsimd", gate_tok[15] if e == 0 else None, prev_teb)
                teb_l1 = DMA("gpsimd", "EB", af[0:1, F_XST:F_XST + 1024], eb_d[0:1, e * 2048:e * 2048 + 1024])
                W("vector", teb_l1, prev_e_last_mm)
                teb1 = OP("vector", "tensor_copy", ab[0:1, S_EBST:S_EBST + 1024],
                          af[0:1, F_XST:F_XST + 1024], inc=True)
                W("gpsimd", teb1)
                teb_l2 = DMA("gpsimd", "EB", af[0:1, F_XST:F_XST + 1024], eb_d[0:1, e * 2048 + 1024:(e + 1) * 2048])
                W("vector", teb_l2)
                teb = OP("vector", "tensor_copy", ab[0:1, S_EBST + 1024:S_EBST + 2048],
                         af[0:1, F_XST:F_XST + 1024], inc=True)
                prev_teb = teb
                teb = [teb1, teb]
                for dt in range(4):
                    jw, wbase, wfree = web.acquire()
                    wk_toks = []
                    for k in range(16):
                        js, st, sfree = wst.acquire()
                        W("sync", sfree)
                        if k == 0:
                            W("sync", wfree)
                        tl = DMA("sync", "W%d" % js, st, ew_d[e, k * 128:(k + 1) * 128, dt * 512:(dt + 1) * 512])
                        W("scalar", tl)
                        tc = OP("scalar", "copy", abr(wbase + k * 512, 512), st, inc=True)
                        wst.use(js, tc)
                        wk_toks.append(tc)
                    for cc in range(16):
                        jp, pp, pfree = proj.acquire()
                        W("tensor", pfree, teb, t_oner, wk_toks)
                        OP("tensor", "matmul", pp, ab[0:1, S_ONER:S_ONER + 128],
                           ab[0:1, S_EBST + dt * 512:S_EBST + dt * 512 + 512],
                           start=True, stop=False, skip_group_check=True)
                        tmm = None
                        for k in range(16):
                            tmm = OP("tensor", "matmul", pp,
                                     abr(A_H + k * 2048 + cc * 128, 128),
                                     abr(wbase + k * 512, 512),
                                     start=False, stop=(k == 15), skip_group_check=True, inc=(k == 15))
                        web.use(jw, tmm)
                        prev_e_last_mm = tmm
                        jg, glb, glfree = glr.acquire()
                        W("scalar", tmm, glfree)
                        tgl = OP("scalar", "activation", abr(glb, 512), pp, AF.Gelu_apprx_tanh, inc=True)
                        proj.use(jp, tgl)
                        W("vector", tgl, gate_tok[cc], acc_tok.get((cc, dt)))
                        gcol = gates_sb[:, cc * 8 + e:cc * 8 + e + 1]
                        dst = abr(C_ACC + cc * 2048 + dt * 512, 512)
                        if e == 0:
                            ta = OP("vector", "tensor_scalar_mul", dst, abr(glb, 512), gcol, inc=True)
                        else:
                            ta = OP("vector", "scalar_tensor_tensor", dst, abr(glb, 512), gcol, dst,
                                    ALU.mult, ALU.add, inc=True)
                        glr.use(jg, ta)
                        acc_tok[(cc, dt)] = ta

            if debug & 8:
                _prevs = [None, None]
                for _m in range(16):
                    for _c in range(4):
                        _j = _c % 2
                        W("vector", acc_tok[(_m, _c)], _prevs[_j])
                        _tc = OP("vector", "tensor_copy", afr(F_RE + _j * 512, 512),
                                 abr(C_ACC + _m * 2048 + _c * 512, 512), inc=True)
                        W("sync", _tc)
                        _ts = DMA("sync", "DG%d" % _j, dbg["acc"][:, _m * 2048 + _c * 512:_m * 2048 + (_c + 1) * 512],
                                  afr(F_RE + _j * 512, 512))
                        _prevs[_j] = _ts

            owr = Rot([D_OW0, D_OW1])
            cfT = Rot([ps_bf[:, 0:128], ps_bf[:, 128:256], ps_bf[:, 256:384], ps_bf[:, 384:512]])
            comb_tok = {}
            for tt in range(4):
                for d in range(16):
                    ttr = None
                    for ccl in range(4):
                        cc = tt * 4 + ccl
                        W("tensor", acc_tok[(cc, d // 4)], t_eye_c)
                        ttr = OP("tensor", "transpose", ps_bf[:, ccl * 128:(ccl + 1) * 128],
                                 abr(C_ACC + cc * 2048 + d * 128, 128), eye_b[:, :], inc=True)
                    W("vector", ttr)
                    tcp = OP("vector", "tensor_copy", abr(B_COMB + d * 512, 512), ps_bf[:, 0:512], inc=True)
                    for ccl in range(4):
                        comb_tok[(d, ccl)] = tcp
                comb_all = [comb_tok[(d, c)] for d in range(16) for c in range(4)]
                stat_mm_s = stat_mm_q = None
                s2_toks = []
                for m in range(16):
                    jw, owb, owfree = owr.acquire()
                    ow_toks = []
                    for piece in range(4):
                        js, st, sfree = wst.acquire()
                        W("sync", sfree)
                        if piece == 0:
                            W("sync", owfree)
                        tl = DMA("sync", "W%d" % js, st.rearrange("p (k c) -> p k c", c=128),
                                 ow_d[piece * 512:(piece + 1) * 512, m * 128:(m + 1) * 128].rearrange(
                                     "(k p) c -> p k c", p=128))
                        W("scalar", tl)
                        tc = OP("scalar", "copy", abr(owb + piece * 512, 512), st, inc=True)
                        wst.use(js, tc)
                        ow_toks.append(tc)
                    jp, pp, pfree = proj.acquire()
                    W("tensor", pfree, t_eye_c, h_tok[(m, tt)], ow_toks, comb_all)
                    OP("tensor", "matmul", pp, eye_b[:, :], abr(A_H + m * 2048 + tt * 512, 512),
                       start=True, stop=False, skip_group_check=True)
                    tmm = None
                    for k in range(16):
                        tmm = OP("tensor", "matmul", pp,
                                 abr(owb + k * 128, 128),
                                 abr(B_COMB + k * 512, 512),
                                 start=False, stop=(k == 15), skip_group_check=True, inc=(k == 15))
                    owr.use(jw, tmm)
                    W("scalar", tmm)
                    ts2 = OP("scalar", "activation", abr(B_S2 + m * 512, 512), pp, AF.Identity,
                             bias=ob_sb[:, m:m + 1], inc=True)
                    jq, sqb, sqfree = sqr.acquire()
                    W("scalar", sqfree)
                    tsq = OP("scalar", "activation", abr(sqb, 512), pp, AF.Square,
                             bias=ob_sb[:, m:m + 1], inc=True)
                    proj.use(jp, ts2)
                    proj.use(jp, tsq)
                    W("tensor", ts2, tsq, t_onec)
                    stat_mm_s = OP("tensor", "matmul", stats_s, onec_b[:, :], abr(B_S2 + m * 512, 512),
                                   start=(m == 0), stop=(m == 15), skip_group_check=True, inc=True)
                    stat_mm_q = OP("tensor", "matmul", stats_q, onec_b[:, :], abr(sqb, 512),
                                   start=(m == 0), stop=(m == 15), skip_group_check=True, inc=True)
                    sqr.use(jq, stat_mm_q)
                    s2_toks.append(ts2)

                W("scalar", stat_mm_s, stat_mm_q)
                OP("scalar", "copy", af[0:1, ST0:ST0 + 512], stats_s)
                OP("scalar", "copy", af[0:1, ST1:ST1 + 512], stats_q)
                OP("vector", "tensor_scalar_mul", af[0:1, ST0:ST0 + 512], af[0:1, ST0:ST0 + 512], 1.0 / DM)
                OP("vector", "tensor_scalar_mul", af[0:1, ST1:ST1 + 512], af[0:1, ST1:ST1 + 512], 1.0 / DM)
                OP("vector", "tensor_tensor", af[0:1, ST2:ST2 + 512], af[0:1, ST0:ST0 + 512],
                   af[0:1, ST0:ST0 + 512], ALU.mult)
                tvv = OP("vector", "tensor_tensor", af[0:1, ST1:ST1 + 512], af[0:1, ST1:ST1 + 512],
                         af[0:1, ST2:ST2 + 512], ALU.subtract, inc=True)
                W("scalar", tvv, t_eps)
                tsv = OP("scalar", "activation", af[0:1, ST1:ST1 + 512], af[0:1, ST1:ST1 + 512],
                         AF.Sqrt, bias=af[0:1, F_Z + 4:F_Z + 5], inc=True)
                W("vector", tsv)
                OP("vector", "reciprocal", af[0:1, ST1:ST1 + 512], af[0:1, ST1:ST1 + 512])
                tmr = OP("vector", "tensor_tensor", af[0:1, ST2:ST2 + 512], af[0:1, ST0:ST0 + 512],
                         af[0:1, ST1:ST1 + 512], ALU.mult, inc=True)
                W("tensor", tmr, t_mask_l)
                tb1 = OP("tensor", "matmul", bc_r, mask_f[0:1, 0:128], af[0:1, ST1:ST1 + 512],
                         start=True, stop=True, skip_group_check=True, inc=True)
                tb2 = OP("tensor", "matmul", bc_mr, mask_f[0:1, 0:128], af[0:1, ST2:ST2 + 512],
                         start=True, stop=True, skip_group_check=True, inc=True)
                for m in range(16):
                    jo, ob_, ofree = outr.acquire()
                    W("vector", tb1, tb2, s2_toks[m], ofree)
                    OP("vector", "tensor_tensor", ob_, abr(B_S2 + m * 512, 512), bc_r, ALU.mult)
                    tf = OP("vector", "tensor_tensor", ob_, ob_, bc_mr, ALU.subtract, inc=True)
                    W("sync", tf)
                    tso = DMA("sync", "O%d" % jo, out_d[m * 128:(m + 1) * 128, tt * 512:(tt + 1) * 512], ob_)
                    outr.use(jo, tso)


        _wait("sync", "O0", cnt["O0"])
        _wait("sync", "O1", cnt["O1"])

        with nc.Block() as block:
            @block.sync
            def _(e):
                for f in ops["sync"]:
                    f(e)

            @block.gpsimd
            def _(e):
                for f in ops["gpsimd"]:
                    f(e)

            @block.tensor
            def _(e):
                for f in ops["tensor"]:
                    f(e)

            @block.vector
            def _(e):
                for f in ops["vector"]:
                    f(e)

            @block.scalar
            def _(e):
                for f in ops["scalar"]:
                    f(e)

    return nc


def make_in_maps(inputs):
    x = np.asarray(inputs["L0_attn_x"], np.float32)
    Wq = np.asarray(inputs["L0_attn_Wq"], np.float32)
    bq = np.asarray(inputs["L0_attn_bq"], np.float32)
    Wk = np.asarray(inputs["L0_attn_Wk"], np.float32)
    bk = np.asarray(inputs["L0_attn_bk"], np.float32)
    Wv = np.asarray(inputs["L0_attn_Wv"], np.float32)
    bv = np.asarray(inputs["L0_attn_bv"], np.float32)
    Wo = np.ascontiguousarray(np.asarray(inputs["L0_attn_Wo"], np.float32))
    bo = np.asarray(inputs["L0_attn_bo"], np.float32)
    rw = np.asarray(inputs["L0_ffn_router_w"], np.float32)
    rb = np.asarray(inputs["L0_ffn_router_b"], np.float32)
    ew = np.ascontiguousarray(np.asarray(inputs["L0_ffn_expert_w"], np.float32))
    eb = np.asarray(inputs["L0_ffn_expert_b"], np.float32)
    ow = np.ascontiguousarray(np.asarray(inputs["L0_ffn_out_w"], np.float32))
    ob = np.asarray(inputs["L0_ffn_out_b"], np.float32)

    eye = np.eye(128, dtype=np.float32)
    maskup = np.ascontiguousarray(np.triu(np.ones((128, 128), np.float32)))
    rw_t = np.ascontiguousarray(rw.reshape(16, 128, 8).transpose(1, 0, 2).reshape(128, 128))
    bo_col = np.ascontiguousarray(bo.reshape(16, 128).T)
    ob_col = np.ascontiguousarray(ob.reshape(16, 128).T)
    eb_row = np.ascontiguousarray(eb.reshape(1, 8 * 2048))
    rb_row = np.ascontiguousarray(rb.reshape(1, 8))

    in_maps = []
    for core in range(8):
        b, hh = core // 2, core % 2
        cols = slice(hh * 1024, (hh + 1) * 1024)
        sel = np.zeros((128, 2), np.float32)
        sel[:, hh] = 1.0
        in_maps.append(dict(
            xT=np.ascontiguousarray(x[b].T),
            xR=np.ascontiguousarray(x[b, hh * 2048:(hh + 1) * 2048].T),
            wq=np.ascontiguousarray(Wq[:, cols]),
            wk=np.ascontiguousarray(Wk[:, cols]),
            wv=np.ascontiguousarray(Wv[:, cols]),
            bq_col=np.ascontiguousarray(bq[cols].reshape(8, 128).T),
            bk_row=np.ascontiguousarray(bk[cols].reshape(1, 1024)),
            bv_row=np.ascontiguousarray(bv[cols].reshape(1, 1024)),
            wo=Wo,
            bo_col=bo_col,
            rw_t=rw_t,
            rb_row=rb_row,
            ew=ew,
            eb_row=eb_row,
            ow=ow,
            ob_col=ob_col,
            maskup=maskup,
            eye=eye,
            sel_col=sel,
        ))
    return in_maps


def kernel(**inputs):
    in_maps = make_in_maps(inputs)
    nc = build(debug=0)
    res = run_bass_kernel_spmd(nc, in_maps, core_ids=list(range(8)))
    out = np.empty((4, 4096, 2048), np.float32)
    for core in range(8):
        b, hh = core // 2, core % 2
        out[b, hh * 2048:(hh + 1) * 2048, :] = res.results[core]["out_fm"].T
    return out


# version 12
# speedup vs baseline: 1.1458x; 1.0081x over previous
import numpy as np
from contextlib import ExitStack
import concourse.bass as bass
import concourse.mybir as mybir
from concourse.bass_utils import run_bass_kernel_spmd

F32 = mybir.dt.float32
BF16 = mybir.dt.bfloat16
AF = mybir.ActivationFunctionType
ALU = mybir.AluOpType
AX = mybir.AxisListType

DM = 2048
SEQ = 4096
TLOC = 2048
HL = 8
NCH = 32
GRP = 4
NG = NCH // GRP
EPS_ATT = 1e-6
EPS_LN = 1e-5


def build(debug=0, phases='full', ngroups=NG, do_rec=5, do_cc=True,
          PSUM_ROUND=True, WO_PRELOAD=True):
    nc = bass.Bass("TRN2", target_bir_lowering=False)

    xT = nc.dram_tensor("xT", [DM, SEQ], F32, kind="ExternalInput").ap()
    xR = nc.dram_tensor("xR", [DM, TLOC], F32, kind="ExternalInput").ap()
    wq_d = nc.dram_tensor("wq", [DM, 1024], F32, kind="ExternalInput").ap()
    wk_d = nc.dram_tensor("wk", [DM, 1024], F32, kind="ExternalInput").ap()
    wv_d = nc.dram_tensor("wv", [DM, 1024], F32, kind="ExternalInput").ap()
    bq_d = nc.dram_tensor("bq_col", [128, 8], F32, kind="ExternalInput").ap()
    bk_d = nc.dram_tensor("bk_row", [1, 1024], F32, kind="ExternalInput").ap()
    bv_d = nc.dram_tensor("bv_row", [1, 1024], F32, kind="ExternalInput").ap()
    wo_d = nc.dram_tensor("wo", [DM, DM], F32, kind="ExternalInput").ap()
    bo_d = nc.dram_tensor("bo_col", [128, 16], F32, kind="ExternalInput").ap()
    rw_d = nc.dram_tensor("rw_t", [128, 128], F32, kind="ExternalInput").ap()
    rb_d = nc.dram_tensor("rb_row", [1, 8], F32, kind="ExternalInput").ap()
    ew_d = nc.dram_tensor("ew", [8, DM, DM], F32, kind="ExternalInput").ap()
    eb_d = nc.dram_tensor("eb_row", [1, 8 * DM], F32, kind="ExternalInput").ap()
    ow_d = nc.dram_tensor("ow", [DM, DM], F32, kind="ExternalInput").ap()
    ob_d = nc.dram_tensor("ob_col", [128, 16], F32, kind="ExternalInput").ap()
    mask_d = nc.dram_tensor("maskup", [128, 128], F32, kind="ExternalInput").ap()
    eye_d = nc.dram_tensor("eye", [128, 128], F32, kind="ExternalInput").ap()
    sel_d = nc.dram_tensor("sel_col", [128, 2], F32, kind="ExternalInput").ap()
    out_d = nc.dram_tensor("out_fm", [DM, TLOC], F32, kind="ExternalOutput").ap()

    cc_in = nc.dram_tensor("cc_in", [DM, TLOC], BF16, kind="Internal")
    cc_out_w = [nc.dram_tensor("cc_out_w%d" % j, [DM, TLOC], BF16, kind="Internal")
                for j in range(2)]

    dbg = {}
    if debug & 1:
        dbg["ao"] = nc.dram_tensor("dbg_ao", [2 * DM, TLOC], F32, kind="ExternalOutput").ap()
    if debug & 2:
        dbg["h"] = nc.dram_tensor("dbg_h", [128, 16 * TLOC], F32, kind="ExternalOutput").ap()
    if debug & 4:
        dbg["gates"] = nc.dram_tensor("dbg_gates", [128, 16 * 8], F32, kind="ExternalOutput").ap()
    if debug & 8:
        dbg["acc"] = nc.dram_tensor("dbg_acc", [128, 16 * DM], F32, kind="ExternalOutput").ap()

    with ExitStack() as ctx:
        ab = ctx.enter_context(nc.sbuf_tensor("ab", [128, 93320], BF16))
        af = ctx.enter_context(nc.sbuf_tensor("af", [128, 5648], F32))
        eye_b = ctx.enter_context(nc.sbuf_tensor("eye_b", [128, 128], BF16))
        mask_f = ctx.enter_context(nc.sbuf_tensor("mask_f", [128, 128], F32))
        rw_b = ctx.enter_context(nc.sbuf_tensor("rw_b", [128, 128], BF16))
        gates_sb = ctx.enter_context(nc.sbuf_tensor("gates_sb", [128, 128], F32))
        bq_sb = ctx.enter_context(nc.sbuf_tensor("bq_sb", [128, 8], F32))
        bo_sb = ctx.enter_context(nc.sbuf_tensor("bo_sb", [128, 16], F32))
        ob_sb = ctx.enter_context(nc.sbuf_tensor("ob_sb", [128, 16], F32))
        sel_sb = ctx.enter_context(nc.sbuf_tensor("sel_sb", [128, 2], F32))
        onec_b = ctx.enter_context(nc.sbuf_tensor("onec_b", [128, 1], BF16))

        A_WQ, A_WK = 0, 16384
        A_H = 0
        B_WV = 32768
        B_AO, B_XS = 32768, 40960
        B_WE0, B_WE1 = 32768, 40960
        B_COMB, B_S2 = 32768, 40960
        C_X0, C_X1 = 49152, 57344
        C_QF, C_KF, C_VC = 65536, 69632, 73728
        C_AOT0, C_AOT1 = 77888, 78912
        C_WO = 49152
        C_ACC = 49152
        D_SNAP = 81920
        D_KFF0, D_KFF1 = 83008, 83520
        D_AOF = (84032, 85056, 86080, 87104)
        D_ASB0, D_ASB1 = 88192, 88704
        D_S1 = 81920
        D_SQ0, D_SQ1 = 90112, 90624
        D_GL0, D_GL1 = 81920, 82432
        D_OW0, D_OW1 = 81920, 83968
        S_EBST = 91136
        AO_W0A, AO_W1A = 91136, 91648
        AO_W0B, AO_W1B = 92160, 92672
        S_BK = 91136
        S_BV = 92160
        S_ONER = 93320 - 136
        S_RB = 93320 - 8

        F_XST = 0
        F_WST = 1024
        F_RE = 3072
        ST0, ST1, ST2 = 4096, 4608, 5120
        F_Z = 5632

        ps_state = ctx.enter_context(nc.psum_tensor("ps_state", [128, 1536], F32))
        ps_proj = ctx.enter_context(nc.psum_tensor("ps_proj", [128, 1024], F32))
        ps_small = ctx.enter_context(nc.psum_tensor("ps_small", [128, 1024], F32))
        ps_bf = ctx.enter_context(nc.psum_tensor("ps_bf", [128, 512], BF16))

        sems = {}
        SEM_NAMES = ("T", "V", "S", "C", "ST", "W0", "W1", "W2", "W3",
                     "X0", "X1", "P0", "P1", "P2", "P3", "O0", "O1", "A0", "A1", "EB", "DG0", "DG1")
        for s in SEM_NAMES:
            sems[s] = ctx.enter_context(nc.semaphore("sem" + s))

        SEME = {"tensor": "T", "vector": "V", "scalar": "S"}
        cnt = {s: 0 for s in SEM_NAMES}
        ops = {k: [] for k in ("sync", "gpsimd", "tensor", "vector", "scalar")}
        lastw = {}

        def _wait(eng, sem, val):
            if val <= 0:
                return
            if lastw.get((eng, sem), 0) >= val:
                return
            lastw[(eng, sem)] = val
            ops[eng].append(lambda e, s=sem, v=val: e.wait_ge(sems[s], v))

        def W(eng, *toks):
            best = {}
            for t in toks:
                if not t:
                    continue
                for u in (t if isinstance(t, list) else [t]):
                    if u:
                        best[u[0]] = max(best.get(u[0], 0), u[1])
            for s, v in best.items():
                _wait(eng, s, v)

        BUCK = 64
        track = {}
        BSZ = {}

        def _rng(ap):
            t = ap.tensor
            nm = t.name
            is_dram = "DRam" in type(t).__name__
            if nm not in BSZ:
                tot = 1
                for d in t.shape:
                    tot *= int(d)
                BSZ[nm] = max(BUCK, tot >> 6) if is_dram else BUCK
            rowlen = int(t.shape[-1])
            lo = int(ap.offset)
            hi = lo + 1
            skipped = is_dram
            for step, count in ap.ap:
                step = int(step); count = int(count)
                if not skipped and step == rowlen and len(t.shape) > 1:
                    skipped = True
                    continue
                hi += (count - 1) * step
            return (nm, lo, hi)

        def _collect(entry, key, deps):
            d = entry.get(key)
            if d:
                for s_, v_ in d.items():
                    if deps.get(s_, 0) < v_:
                        deps[s_] = v_

        def _deps_and_reg(rd_aps, wr_aps, token, wr_extra=None):
            deps = {}
            rranges = [_rng(ap) for ap in rd_aps]
            wranges = [_rng(ap) for ap in wr_aps]
            if wr_extra is not None:
                wranges = [wr_extra] + [w for w in wranges]
            for nm, lo, hi in rranges:
                d = track.get(nm)
                if d:
                    bz = BSZ[nm]
                    for b in range(lo // bz, (hi - 1) // bz + 1):
                        e = d.get(b)
                        if e:
                            _collect(e, "w", deps)
            for nm, lo, hi in wranges:
                d = track.get(nm)
                if d:
                    bz = BSZ[nm]
                    for b in range(lo // bz, (hi - 1) // bz + 1):
                        e = d.get(b)
                        if e:
                            _collect(e, "w", deps)
                            _collect(e, "r", deps)
            s_, v_ = token
            for nm, lo, hi in wranges:
                d = track.setdefault(nm, {})
                bz = BSZ[nm]
                for b in range(lo // bz, (hi - 1) // bz + 1):
                    d[b] = {"w": {s_: v_}, "r": {}}
            for nm, lo, hi in rranges:
                d = track.setdefault(nm, {})
                bz = BSZ[nm]
                for b in range(lo // bz, (hi - 1) // bz + 1):
                    e = d.setdefault(b, {"w": {}, "r": {}})
                    if e["r"].get(s_, 0) < v_:
                        e["r"][s_] = v_
            return deps

        RW = {
            "tensor_copy": (1, (1,)), "tensor_tensor": (1, (1, 2)),
            "tensor_scalar": (1, (1, 2, 3)), "tensor_single_scalar": (1, (1, 2)),
            "scalar_tensor_tensor": (1, (1, 2, 3)),
            "tensor_scalar_add": (1, (1, 2)), "tensor_scalar_sub": (1, (1, 2)),
            "tensor_scalar_mul": (1, (1, 2)), "tensor_scalar_min": (1, (1, 2)),
            "tensor_scalar_max": (1, (1, 2)),
            "reduce_max": (1, (1,)), "reduce_sum": (1, (1,)), "reciprocal": (1, (1,)),
            "memset": (1, ()), "memzero": (1, ()),
            "activation": (1, (1,)), "copy": (1, (1,)), "add": (1, (1, 2)),
            "mul": (1, (1, 2)), "square": (1, (1,)), "sqrt": (1, (1,)),
            "matmul": (1, (0, 1, 2)), "transpose": (1, (1, 2)),
        }

        def _is_ap(x):
            return hasattr(x, "tensor") and hasattr(x, "offset")

        def OP(eng, meth, *a, inc=True, **kw):
            s = SEME[eng]
            cnt[s] += 1
            v = cnt[s]
            token = (s, v)
            nw, rds = RW[meth]
            wr_aps = [x for x in a[:nw] if _is_ap(x)]
            rd_aps = [a[i] for i in rds if i < len(a) and _is_ap(a[i])]
            rd_aps += [x for x in kw.values() if _is_ap(x) and not isinstance(x, bool)]
            wr_extra = None
            if PSUM_ROUND and meth in ("matmul", "transpose") and kw.get("start", True):
                t = wr_aps[0].tensor
                if "PSum" in type(t).__name__:
                    esz = mybir.dt.size(t.dtype)
                    region = 2048 // esz
                    nm, lo, hi = _rng(wr_aps[0])
                    lo = (lo // region) * region
                    hi = ((hi + region - 1) // region) * region
                    wr_extra = (nm, lo, hi)
            deps = _deps_and_reg(rd_aps, wr_aps, token, wr_extra)
            if eng == "tensor":
                deps.pop("T", None)
            for s_, v_ in deps.items():
                _wait(eng, s_, v_)
            ops[eng].append(lambda e, m=meth, a=a, kw=kw, s=s: getattr(e, m)(*a, **kw).then_inc(sems[s], 1))
            return token

        def DMA(q, s, out, in_, record=True):
            cnt[s] += 16
            v = cnt[s]
            token = (s, v)
            if record:
                deps = _deps_and_reg([in_], [out], token)
            else:
                deps = _deps_and_reg([in_], [], token)
            for s_, v_ in deps.items():
                _wait(q, s_, v_)
            ops[q].append(lambda e, o=out, i=in_, s=s: e.dma_start(out=o, in_=i).then_inc(sems[s], 16))
            return token

        def REG(rd_aps, wr_aps, token, eng):
            deps = _deps_and_reg(rd_aps, wr_aps, token)
            for s_, v_ in deps.items():
                _wait(eng, s_, v_)

        def REREG_W(aps, token):
            s_, v_ = token
            for ap in aps:
                nm, lo, hi = _rng(ap)
                d = track.setdefault(nm, {})
                bz = BSZ[nm]
                for b in range(lo // bz, (hi - 1) // bz + 1):
                    d[b] = {"w": {s_: v_}, "r": {}}

        class Rot:
            def __init__(self, items):
                self.items = items
                self.busy = [[] for _ in items]
                self.i = 0

            def acquire(self):
                j = self.i % len(self.items)
                self.i += 1
                b = self.busy[j]
                self.busy[j] = []
                return j, self.items[j], b

            def use(self, j, tok):
                if tok:
                    self.busy[j].append(tok)

        def abr(c0, n):
            return ab[:, c0:c0 + n]

        def afr(c0, n):
            return af[:, c0:c0 + n]

        t_eye_l = DMA("sync", "ST", af[:, F_WST:F_WST + 128], eye_d)
        t_mask_l = DMA("sync", "ST", mask_f[:, :], mask_d)
        t_rw_l = DMA("sync", "ST", af[:, F_WST + 128:F_WST + 256], rw_d)
        DMA("sync", "ST", bq_sb[:, :], bq_d)
        DMA("sync", "ST", bo_sb[:, :], bo_d)
        DMA("sync", "ST", ob_sb[:, :], ob_d)
        t_sel_l = DMA("sync", "ST", sel_sb[:, :], sel_d)
        t_bk_l = DMA("sync", "ST", af[0:1, F_WST + 256:F_WST + 1280], bk_d)
        t_bv_l = DMA("sync", "ST", af[0:1, F_XST:F_XST + 1024], bv_d)
        t_rb_l = DMA("sync", "ST", af[0:1, F_Z + 8:F_Z + 16], rb_d)
        st_all = ("ST", cnt["ST"])
        t_eye_l = t_mask_l = t_rw_l = t_sel_l = t_bk_l = t_bv_l = t_rb_l = t_bq_l = st_all
        REREG_W([af[:, F_WST:F_WST + 256], mask_f[:, :], bq_sb[:, :], bo_sb[:, :],
                 ob_sb[:, :], sel_sb[:, :], af[0:1, F_WST + 256:F_WST + 1280],
                 af[0:1, F_XST:F_XST + 1024], af[0:1, F_Z + 8:F_Z + 16]], st_all)
        W("vector", st_all)
        t_eye_c = OP("vector", "tensor_copy", eye_b[:, :], af[:, F_WST:F_WST + 128], inc=True)
        t_rw_c = OP("vector", "tensor_copy", rw_b[:, :], af[:, F_WST + 128:F_WST + 256], inc=True)
        t_bk_c = OP("vector", "tensor_copy", ab[0:1, S_BK:S_BK + 1024], af[0:1, F_WST + 256:F_WST + 1280], inc=True)
        t_bv_c = OP("vector", "tensor_copy", ab[0:1, S_BV:S_BV + 1024], af[0:1, F_XST:F_XST + 1024], inc=True)
        t_rb_c = OP("vector", "tensor_copy", ab[0:1, S_RB:S_RB + 8], af[0:1, F_Z + 8:F_Z + 16], inc=True)
        t_oner = OP("vector", "memset", ab[0:1, S_ONER:S_ONER + 128], 1.0, inc=True)
        t_onec = OP("vector", "memset", onec_b[:, :], 1.0, inc=True)
        t_eps = OP("vector", "memset", af[0:1, F_Z + 4:F_Z + 5], EPS_LN, inc=True)
        vc_ones = abr(C_VC, 32 * 129).rearrange("p (s c) -> p s c", c=129)[:, :, 128:129]
        t_vcone = OP("vector", "memset", vc_ones, 1.0, inc=True)

        wst = Rot([afr(F_WST, 512), afr(F_WST + 512, 512),
                   afr(F_WST + 1024, 512), afr(F_WST + 1536, 512)])
        w_tok = {}
        first_w = True
        for name, wd, dst in (("q", wq_d, A_WQ), ("k", wk_d, A_WK), ("v", wv_d, B_WV)):
            for k in range(16):
                for half in range(2):
                    j, st, free = wst.acquire()
                    if first_w:
                        W("sync", t_bk_c, t_bv_c, t_eye_c, t_rw_c)
                        first_w = False
                    W("sync", free)
                    tl = DMA("sync", "W%d" % j, st, wd[k * 128:(k + 1) * 128, half * 512:(half + 1) * 512])
                    W("vector", tl)
                    tc = OP("vector", "tensor_copy", abr(dst + k * 1024 + half * 512, 512), st, inc=True)
                    wst.use(j, tc)
                    w_tok[(name, k, half)] = tc

        xst = Rot([afr(F_XST, 512), afr(F_XST + 512, 512)])
        xbuf = Rot([C_X0, C_X1])
        proj = Rot([ps_proj[:, 0:512], ps_proj[:, 512:1024]])
        re_rot = Rot([(afr(F_RE, 512), afr(F_RE + 512, 512))])
        kfT = Rot([ps_bf[:, 0:128], ps_bf[:, 128:256], ps_bf[:, 256:384], ps_bf[:, 384:512]])
        aT = Rot([ps_small[:, 0:128], ps_small[:, 128:256]])
        outp = Rot([ps_small[:, 256:385], ps_small[:, 512:641]])
        kff = Rot([D_KFF0, D_KFF1])
        aofm = Rot(list(D_AOF))
        asb = Rot([D_ASB0, D_ASB1])
        aotm = Rot([C_AOT0, C_AOT1])
        zscr = Rot([(af[:, F_Z:F_Z + 1], af[:, F_Z + 1:F_Z + 2]),
                    (af[:, F_Z + 2:F_Z + 3], af[:, F_Z + 3:F_Z + 4])])

        snap_ap = abr(D_SNAP, 1032)

        def state_slot(h):
            o = (h // 3) * 512 + (h % 3) * 129
            return ps_state[:, o:o + 129]

        def snap_slot(h):
            return abr(D_SNAP + h * 129, 129)

        qkv_read_tok = []
        snap_read_tok = []
        kv_mm_tok = []

        def emit_xload(g):
            jx, xbase, xfree = xbuf.acquire()
            toks = []
            for k in range(16):
                js, st, sfree = xst.acquire()
                W("sync", sfree, t_bv_c if g == 0 and k == 0 else None)
                tl = DMA("sync", "X%d" % js, st, xT[k * 128:(k + 1) * 128, g * 512:(g + 1) * 512])
                W("vector", tl)
                if k == 0:
                    W("vector", xfree)
                tc = OP("vector", "tensor_copy", abr(xbase + k * 512, 512), st, inc=True)
                xst.use(js, tc)
                toks.append(tc)
            return jx, xbase, toks

        xq = emit_xload(0)
        for g in range(ngroups):
            jx, xbase, xcast_tok = xq

            qf_writers = []
            for m in range(8):
                jp, pp, pfree = proj.acquire()
                W("tensor", pfree, xcast_tok)
                tmm = None
                for k in range(16):
                    W("tensor", w_tok[("q", k, m // 4)])
                    tmm = OP("tensor", "matmul", pp,
                             abr(A_WQ + k * 1024 + m * 128, 128),
                             abr(xbase + k * 512, 512),
                             start=(k == 0), stop=(k == 15),
                             skip_group_check=True, inc=(k == 15))
                jr, (rr, ee), rfree = re_rot.acquire()
                W("scalar", tmm, rfree, t_bq_l)
                ts_r = OP("scalar", "activation", rr, pp, AF.Relu, bias=bq_sb[:, m:m + 1], inc=True)
                ts_e = OP("scalar", "activation", ee, pp, AF.Exp, bias=bq_sb[:, m:m + 1], inc=True)
                W("vector", ts_r, ts_e, qkv_read_tok)
                tq = OP("vector", "scalar_tensor_tensor",
                        abr(C_QF + m * 512, 512), ee, 1.0, rr, ALU.min, ALU.add, inc=True)
                re_rot.use(jr, tq)
                proj.use(jp, ts_r)
                proj.use(jp, ts_e)
                qf_writers.append(tq)

            if g + 1 < ngroups:
                xq = emit_xload(g + 1)

            kf_w = [None] * (GRP * 2)
            vc_w = [None] * (GRP * 2)
            for cc in range(GRP):
                gc = g * GRP + cc
                for dt in range(2):
                    jp, pp, pfree = proj.acquire()
                    W("tensor", pfree, t_oner, t_bk_c)
                    OP("tensor", "matmul", pp,
                       ab[0:1, S_ONER:S_ONER + 128],
                       ab[0:1, S_BK + dt * 512:S_BK + dt * 512 + 512],
                       start=True, stop=False, skip_group_check=True)
                    tmm = None
                    for k in range(16):
                        W("tensor", w_tok[("k", k, dt)])
                        tmm = OP("tensor", "matmul", pp,
                                 abr(xbase + k * 512 + cc * 128, 128),
                                 abr(A_WK + k * 1024 + dt * 512, 512),
                                 start=False, stop=(k == 15),
                                 skip_group_check=True, inc=(k == 15))
                    jr, (rr, ee), rfree = re_rot.acquire()
                    W("scalar", tmm, rfree)
                    ts_r = OP("scalar", "activation", rr, pp, AF.Relu, inc=True)
                    ts_e = OP("scalar", "activation", ee, pp, AF.Exp, inc=True)
                    W("vector", ts_r, ts_e, qkv_read_tok)
                    tk = OP("vector", "scalar_tensor_tensor",
                            abr(C_KF + (cc * 8 + dt * 4) * 128, 512), ee, 1.0, rr,
                            ALU.min, ALU.add, inc=True)
                    re_rot.use(jr, tk)
                    proj.use(jp, ts_r)
                    proj.use(jp, ts_e)
                    kf_w[cc * 2 + dt] = tk
                for dt in range(2):
                    jp, pp, pfree = proj.acquire()
                    W("tensor", pfree, t_oner, t_bv_c)
                    OP("tensor", "matmul", pp,
                       ab[0:1, S_ONER:S_ONER + 128],
                       ab[0:1, S_BV + dt * 512:S_BV + dt * 512 + 512],
                       start=True, stop=False, skip_group_check=True)
                    tmm = None
                    for k in range(16):
                        W("tensor", w_tok[("v", k, dt)])
                        tmm = OP("tensor", "matmul", pp,
                                 abr(xbase + k * 512 + cc * 128, 128),
                                 abr(B_WV + k * 1024 + dt * 512, 512),
                                 start=False, stop=(k == 15),
                                 skip_group_check=True, inc=(k == 15))
                    W("vector", tmm, t_vcone, qkv_read_tok)
                    vbase = C_VC + (cc * 8 + dt * 4) * 129
                    vview = abr(vbase, 4 * 129).rearrange("p (s c) -> p s c", c=129)[:, :, 0:128]
                    tv = OP("vector", "tensor_copy", vview,
                            pp.rearrange("p (s c) -> p s c", c=128), inc=True)
                    proj.use(jp, tv)
                    vc_w[cc * 2 + dt] = tv

                if do_rec < 1:
                    continue
                if gc == 0:
                    W("vector", kv_mm_tok, snap_read_tok)
                    tsn = [OP("vector", "memset", snap_ap, 0.0, inc=True)]
                else:
                    W("scalar", kv_mm_tok, snap_read_tok)
                    tsn = [
                        OP("scalar", "copy", abr(D_SNAP, 387), ps_state[:, 0:387], inc=True),
                        OP("scalar", "copy", abr(D_SNAP + 387, 387), ps_state[:, 512:899], inc=True),
                        OP("scalar", "copy", abr(D_SNAP + 774, 258), ps_state[:, 1024:1282], inc=True),
                    ]
                snap_read_tok = []
                kv_mm_tok = []

                for h in range(HL):
                    W("tensor", tsn, kf_w[cc * 2 + h // 4], vc_w[cc * 2 + h // 4])
                    tkv = OP("tensor", "matmul", state_slot(h),
                             abr(C_KF + (cc * 8 + h) * 128, 128),
                             abr(C_VC + (cc * 8 + h) * 129, 129),
                             start=(gc == 0 and h in (0, 3, 6)),
                             stop=(gc == NCH - 1 and h in (2, 5, 7)),
                             skip_group_check=True, inc=True)
                    kv_mm_tok.append(tkv)

                if do_rec < 2:
                    continue
                ja, aob, aofree = aotm.acquire()
                kfb_of = {}
                for b in range(2):
                    ttr = None
                    for hh in range(4):
                        h = b * 4 + hh
                        W("tensor", kf_w[cc * 2 + h // 4], t_eye_c)
                        ttr = OP("tensor", "transpose", ps_bf[:, hh * 128:(hh + 1) * 128],
                                 abr(C_KF + (cc * 8 + h) * 128, 128), eye_b[:, :], inc=True)
                    jf, kfb, kffree = kff.acquire()
                    W("vector", ttr, kffree)
                    tkc = OP("vector", "tensor_copy", abr(kfb, 512), ps_bf[:, 0:512], inc=True)
                    for hh in range(4):
                        kfb_of[b * 4 + hh] = (jf, kfb + hh * 128, tkc)
                amask = {}
                for h in range(HL):
                    jf, kfb_off, tkc = kfb_of[h]
                    jA, Aps, Afree = aT.acquire()
                    W("tensor", Afree, tkc, qf_writers[h])
                    tA = OP("tensor", "matmul", Aps,
                            abr(kfb_off, 128),
                            abr(C_QF + h * 512 + cc * 128, 128),
                            start=True, stop=True, skip_group_check=True, inc=True)
                    kff.use(jf, tA)
                    jm, Ab, Afree2 = asb.acquire()
                    W("vector", tA, Afree2, t_mask_l)
                    tAm = OP("vector", "tensor_tensor", abr(Ab, 128), Aps, mask_f[:, :], ALU.mult, inc=True)
                    aT.use(jA, tAm)
                    amask[h] = (jm, Ab, tAm)
                norm_toks = []
                for h in range(HL):
                    jm, Ab, tAm = amask[h]
                    if do_rec < 3:
                        norm_toks.append(tAm)
                        continue
                    jo, ops_, ofree = outp.acquire()
                    W("tensor", ofree, tAm, tsn, qf_writers[h], vc_w[cc * 2 + h // 4])
                    OP("tensor", "matmul", ops_,
                       abr(C_QF + h * 512 + cc * 128, 128), snap_slot(h),
                       start=True, stop=False, skip_group_check=True)
                    tio = OP("tensor", "matmul", ops_,
                             abr(Ab, 128),
                             abr(C_VC + (cc * 8 + h) * 129, 129),
                             start=False, stop=True, skip_group_check=True, inc=True)
                    snap_read_tok.append(tio)
                    asb.use(jm, tio)
                    jz, (zb, rzb), zfree = zscr.acquire()
                    W("vector", tio, zfree)
                    OP("vector", "tensor_scalar_add", zb, ops_[:, 128:129], EPS_ATT)
                    OP("vector", "reciprocal", rzb, zb)
                    if h == 0:
                        W("vector", aofree)
                    tn = OP("vector", "tensor_scalar_mul",
                            abr(aob + h * 128, 128), ops_[:, 0:128], rzb, inc=True)
                    zscr.use(jz, tn)
                    outp.use(jo, tn)
                    norm_toks.append(tn)

                if do_rec < 4:
                    continue
                jaf, afb, affree = aofm.acquire()
                wj = gc // 16
                lcol = (gc % 16) * 128
                aof_toks = []
                for b in range(2):
                    ttr = None
                    for hh in range(4):
                        h = b * 4 + hh
                        W("tensor", norm_toks[h], t_eye_c)
                        ttr = OP("tensor", "transpose", ps_bf[:, hh * 128:(hh + 1) * 128],
                                 abr(aob + h * 128, 128), eye_b[:, :], inc=True)
                    aotm.use(ja, ttr)
                    W("vector", ttr)
                    if b == 0:
                        W("vector", affree)
                    tac = OP("vector", "tensor_copy", abr(afb + b * 512, 512), ps_bf[:, 0:512], inc=True)
                    aof_toks.append(tac)
                if do_rec >= 5:
                    W("gpsimd", aof_toks)
                    td = None
                    for hh2 in range(2):
                        td = DMA("gpsimd", "A%d" % (jaf % 2),
                                 cc_in.ap()[wj * 1024 + hh2 * 512:wj * 1024 + (hh2 + 1) * 512,
                                            lcol:lcol + 128].rearrange("(h p) c -> p h c", p=128),
                                 abr(afb + hh2 * 512, 512).rearrange("p (h c) -> p h c", c=128))
                    aofm.use(jaf, td)
                    REG([abr(afb, 1024)], [], td, "gpsimd")
                else:
                    for tac in aof_toks:
                        aofm.use(jaf, tac)

            last_t = ("T", cnt["T"])
            xbuf.use(jx, last_t)
            qkv_read_tok = [last_t]

            if do_cc and g == 3:
                _wait("gpsimd", "A0", cnt["A0"])
                _wait("gpsimd", "A1", cnt["A1"])
                cnt["C"] += 1
                REG([cc_in.ap()[0:1024, :]], [cc_out_w[0].ap()], ("C", cnt["C"]), "gpsimd")
                ops["gpsimd"].append(lambda e: e.collective_compute(
                    "AllGather", ALU.bypass,
                    replica_groups=[[0, 1], [2, 3], [4, 5], [6, 7]],
                    ins=[cc_in.ap()[0:1024, :]], outs=[cc_out_w[0].ap()],
                ).then_inc(sems["C"], 1))
                t_cc_w0 = ("C", cnt["C"])

        _wait("gpsimd", "A0", cnt["A0"])
        _wait("gpsimd", "A1", cnt["A1"])
        if do_cc:
            cnt["C"] += 1
            REG([cc_in.ap()[1024:2048, :]], [cc_out_w[1].ap()], ("C", cnt["C"]), "gpsimd")
            ops["gpsimd"].append(lambda e: e.collective_compute(
                "AllGather", ALU.bypass,
                replica_groups=[[0, 1], [2, 3], [4, 5], [6, 7]],
                ins=[cc_in.ap()[1024:2048, :]], outs=[cc_out_w[1].ap()],
            ).then_inc(sems["C"], 1))
            t_cc_w1 = ("C", cnt["C"])
            t_cc_w = {0: t_cc_w0, 1: t_cc_w1}
        else:
            nrows = 1024
            ncols = min(ngroups, 4) * 512
            for j in range(2):
                for half in range(2):
                    cnt["C"] += 16
                    REG([cc_in.ap()], [cc_out_w[j].ap()], ("C", cnt["C"]), "gpsimd")
                    ops["gpsimd"].append(lambda e, j=j, h=half, nr=nrows, nc_=ncols: e.dma_start(
                        out=cc_out_w[j].ap()[h * 1024:h * 1024 + nr, 0:nc_],
                        in_=cc_in.ap()[j * 1024:j * 1024 + nr, 0:nc_]).then_inc(sems["C"], 16))
            t_cc_w = {0: ("C", cnt["C"]), 1: ("C", cnt["C"])}
        t_cc = t_cc_w[1]
        attn_end = [("T", cnt["T"]), ("V", cnt["V"]), ("S", cnt["S"])]


        if phases == "attn":
            W("vector", attn_end)
            tz = OP("vector", "memset", afr(F_RE, 512), 0.0, inc=True)
            W("sync", tz, t_cc)
            for m_ in range(16):
                for tq_ in range(4):
                    DMA("sync", "O%d" % (tq_ % 2), out_d[m_ * 128:(m_ + 1) * 128, tq_ * 512:(tq_ + 1) * 512], afr(F_RE, 512))

        if phases == "full":
            def ao_rows(j, k):
                r0 = (k // 8) * 1024 + (k % 8) * 128
                return cc_out_w[j].ap()[r0:r0 + 128, :]

            wo_tok = {}
            first_post = True
            for k in range(16):
                for half in range(4):
                    j, st, free = wst.acquire()
                    if first_post and not WO_PRELOAD:
                        W("sync", attn_end)
                        W("vector", attn_end)
                        first_post = False
                    W("sync", free)
                    tl = DMA("sync", "W%d" % j, st, wo_d[k * 128:(k + 1) * 128, half * 512:(half + 1) * 512])
                    W("vector", tl)
                    tc = OP("vector", "tensor_copy", abr(C_WO + k * 2048 + half * 512, 512), st, inc=True)
                    wst.use(j, tc)
                    wo_tok[(k, half)] = tc

            aopair = Rot([(AO_W0A, AO_W1A), (AO_W0B, AO_W1B)])
            xst2 = Rot([afr(F_XST, 512), afr(F_XST + 512, 512)])
            sqr = Rot([D_SQ0, D_SQ1])
            outr = Rot([afr(F_RE, 512), afr(F_RE + 512, 512)])
            stats_s = ps_small[0:1, 0:512]
            stats_q = ps_small[0:1, 512:1024]
            bc_r = ps_state[:, 0:512]
            bc_mr = ps_state[:, 512:1024]

            h_tok = {}
            prev_b1_tmm = None
            for tt in range(4):
                ao_tok = {}
                x_tok = {}
                for k in range(16):
                    jj, (w0c, w1c), pfree = aopair.acquire()
                    W("sync", pfree, t_cc_w[0], t_cc_w[1])
                    if tt == 0 and k == 0:
                        W("sync", attn_end)
                    t0l = DMA("sync", "P%d" % (2 * jj), abr(w0c, 512), ao_rows(0, k)[:, tt * 512:(tt + 1) * 512])
                    t1l = DMA("sync", "P%d" % (2 * jj + 1), abr(w1c, 512), ao_rows(1, k)[:, tt * 512:(tt + 1) * 512])
                    W("vector", t0l, t1l, t_sel_l, prev_b1_tmm)
                    OP("vector", "tensor_scalar_mul", abr(B_AO + k * 512, 512), abr(w0c, 512), sel_sb[:, 0:1])
                    tcmb = OP("vector", "scalar_tensor_tensor", abr(B_AO + k * 512, 512),
                              abr(w1c, 512), sel_sb[:, 1:2], abr(B_AO + k * 512, 512),
                              ALU.mult, ALU.add, inc=True)
                    aopair.use(jj, tcmb)
                    ao_tok[k] = tcmb
                    js, st, sfree = xst2.acquire()
                    W("sync", sfree)
                    tl = DMA("sync", "X%d" % js, st, xR[k * 128:(k + 1) * 128, tt * 512:(tt + 1) * 512])
                    W("vector", tl)
                    tc = OP("vector", "tensor_copy", abr(B_XS + k * 512, 512), st, inc=True)
                    xst2.use(js, tc)
                    x_tok[k] = tc

                stat_mm_s = stat_mm_q = None
                s1_toks = []
                for m in range(16):
                    jp, pp, pfree = proj.acquire()
                    W("tensor", pfree, x_tok[m], t_eye_c)
                    OP("tensor", "matmul", pp, eye_b[:, :], abr(B_XS + m * 512, 512),
                       start=True, stop=False, skip_group_check=True)
                    tmm = None
                    for k in range(16):
                        W("tensor", ao_tok[k], wo_tok[(k, m // 4)])
                        tmm = OP("tensor", "matmul", pp,
                                 abr(C_WO + k * 2048 + m * 128, 128),
                                 abr(B_AO + k * 512, 512),
                                 start=False, stop=(k == 15),
                                 skip_group_check=True, inc=(k == 15))
                    if m == 15:
                        prev_b1_tmm = tmm
                    W("scalar", tmm)
                    ts1 = OP("scalar", "activation", abr(D_S1 + m * 512, 512), pp, AF.Identity,
                             bias=bo_sb[:, m:m + 1], inc=True)
                    jq, sqb, sqfree = sqr.acquire()
                    W("scalar", sqfree)
                    tsq = OP("scalar", "activation", abr(sqb, 512), pp, AF.Square,
                             bias=bo_sb[:, m:m + 1], inc=True)
                    proj.use(jp, ts1)
                    proj.use(jp, tsq)
                    W("tensor", ts1, tsq, t_onec)
                    stat_mm_s = OP("tensor", "matmul", stats_s, onec_b[:, :], abr(D_S1 + m * 512, 512),
                                   start=(m == 0), stop=(m == 15), skip_group_check=True, inc=True)
                    stat_mm_q = OP("tensor", "matmul", stats_q, onec_b[:, :], abr(sqb, 512),
                                   start=(m == 0), stop=(m == 15), skip_group_check=True, inc=True)
                    sqr.use(jq, stat_mm_q)
                    s1_toks.append(ts1)

                W("scalar", stat_mm_s, stat_mm_q)
                OP("scalar", "copy", af[0:1, ST0:ST0 + 512], stats_s)
                OP("scalar", "copy", af[0:1, ST1:ST1 + 512], stats_q)
                OP("vector", "tensor_scalar_mul", af[0:1, ST0:ST0 + 512], af[0:1, ST0:ST0 + 512], 1.0 / DM)
                OP("vector", "tensor_scalar_mul", af[0:1, ST1:ST1 + 512], af[0:1, ST1:ST1 + 512], 1.0 / DM)
                OP("vector", "tensor_tensor", af[0:1, ST2:ST2 + 512], af[0:1, ST0:ST0 + 512],
                   af[0:1, ST0:ST0 + 512], ALU.mult)
                tvv = OP("vector", "tensor_tensor", af[0:1, ST1:ST1 + 512], af[0:1, ST1:ST1 + 512],
                         af[0:1, ST2:ST2 + 512], ALU.subtract, inc=True)
                W("scalar", tvv, t_eps)
                tsv = OP("scalar", "activation", af[0:1, ST1:ST1 + 512], af[0:1, ST1:ST1 + 512],
                         AF.Sqrt, bias=af[0:1, F_Z + 4:F_Z + 5], inc=True)
                W("vector", tsv)
                OP("vector", "reciprocal", af[0:1, ST1:ST1 + 512], af[0:1, ST1:ST1 + 512])
                tmr = OP("vector", "tensor_tensor", af[0:1, ST2:ST2 + 512], af[0:1, ST0:ST0 + 512],
                         af[0:1, ST1:ST1 + 512], ALU.mult, inc=True)
                W("tensor", tmr, t_mask_l)
                tb1 = OP("tensor", "matmul", bc_r, mask_f[0:1, 0:128], af[0:1, ST1:ST1 + 512],
                         start=True, stop=True, skip_group_check=True, inc=True)
                tb2 = OP("tensor", "matmul", bc_mr, mask_f[0:1, 0:128], af[0:1, ST2:ST2 + 512],
                         start=True, stop=True, skip_group_check=True, inc=True)
                for m in range(16):
                    W("vector", tb1, tb2, s1_toks[m])
                    OP("vector", "tensor_tensor", abr(B_XS + m * 512, 512),
                       abr(D_S1 + m * 512, 512), bc_r, ALU.mult)
                    th = OP("vector", "tensor_tensor", abr(A_H + m * 2048 + tt * 512, 512),
                            abr(B_XS + m * 512, 512), bc_mr, ALU.subtract, inc=True)
                    h_tok[(m, tt)] = th

            if debug & 2:
                _prevs = [None, None]
                for m in range(16):
                    for _c in range(4):
                        _j = _c % 2
                        W("vector", [h_tok[(m, q)] for q in range(4)], _prevs[_j])
                        _tc = OP("vector", "tensor_copy", afr(F_RE + _j * 512, 512),
                                 abr(A_H + m * 2048 + _c * 512, 512), inc=True)
                        W("sync", _tc)
                        _ts = DMA("sync", "DG%d" % _j, dbg["h"][:, m * 2048 + _c * 512:m * 2048 + (_c + 1) * 512],
                                  afr(F_RE + _j * 512, 512))
                        _prevs[_j] = _ts

            gate_tok = {}
            prev_texp = None
            for cc in range(16):
                jp, pp, pfree = proj.acquire()
                pp8 = pp[:, 0:8]
                W("tensor", pfree, t_oner, t_rb_c, t_rw_c,
                  [h_tok[(k, cc // 4)] for k in range(16)])
                OP("tensor", "matmul", pp8, ab[0:1, S_ONER:S_ONER + 128], ab[0:1, S_RB:S_RB + 8],
                   start=True, stop=False, skip_group_check=True)
                tmm = None
                for k in range(16):
                    tmm = OP("tensor", "matmul", pp8,
                             abr(A_H + k * 2048 + cc * 128, 128),
                             rw_b[:, k * 8:(k + 1) * 8],
                             start=False, stop=(k == 15), skip_group_check=True, inc=(k == 15))
                W("vector", tmm, prev_texp)
                m1n = af[:, F_Z:F_Z + 1]
                tm1 = OP("vector", "reduce_max", m1n, pp8, axis=AX.X, negate=True, inc=True)
                W("scalar", tm1)
                texp = OP("scalar", "activation", afr(ST0, 8), pp8, AF.Exp, bias=m1n, inc=True)
                prev_texp = texp
                proj.use(jp, tm1)
                proj.use(jp, texp)
                t_sb = afr(ST0, 8)
                W("vector", texp)
                e1 = af[:, F_Z + 1:F_Z + 2]
                OP("vector", "reduce_max", e1, t_sb, axis=AX.X)
                lt = afr(ST0 + 8, 8)
                OP("vector", "tensor_single_scalar", lt, t_sb, e1, ALU.is_lt)
                tmsk = afr(ST0 + 16, 8)
                OP("vector", "tensor_tensor", tmsk, t_sb, lt, ALU.mult)
                e2 = af[:, F_Z + 2:F_Z + 3]
                OP("vector", "reduce_max", e2, tmsk, axis=AX.X)
                keep = afr(ST0 + 8, 8)
                OP("vector", "tensor_single_scalar", keep, t_sb, e2, ALU.is_ge)
                den = af[:, F_Z + 3:F_Z + 4]
                OP("vector", "tensor_tensor", den, e1, e2, ALU.add)
                OP("vector", "reciprocal", den, den)
                OP("vector", "tensor_tensor", t_sb, t_sb, keep, ALU.mult)
                tg = OP("vector", "tensor_scalar_mul", gates_sb[:, cc * 8:cc * 8 + 8], t_sb, den, inc=True)
                gate_tok[cc] = tg

            if debug & 4:
                W("sync", [gate_tok[c] for c in range(16)])
                DMA("sync", "DG0", dbg["gates"], gates_sb[:, :])

            web = Rot([B_WE0, B_WE1])
            glr = Rot([D_GL0, D_GL1])
            EBSLOT = (83968, 86016)
            EBSTG = (F_XST, F_RE)
            acc_tok = {}
            prev_e_last_mm = None
            for e in range(8):
                ebst = EBSLOT[e % 2]
                stg = EBSTG[e % 2]
                W("gpsimd", gate_tok[15] if e == 0 else None)
                teb_l1 = DMA("gpsimd", "EB", af[0:1, stg:stg + 1024], eb_d[0:1, e * 2048:e * 2048 + 1024])
                W("vector", teb_l1)
                teb1 = OP("vector", "tensor_copy", ab[0:1, ebst:ebst + 1024],
                          af[0:1, stg:stg + 1024], inc=True)
                W("gpsimd", teb1)
                teb_l2 = DMA("gpsimd", "EB", af[0:1, stg:stg + 1024], eb_d[0:1, e * 2048 + 1024:(e + 1) * 2048])
                W("vector", teb_l2)
                teb = OP("vector", "tensor_copy", ab[0:1, ebst + 1024:ebst + 2048],
                         af[0:1, stg:stg + 1024], inc=True)
                teb = [teb1, teb]
                for dt in range(4):
                    jw, wbase, wfree = web.acquire()
                    wk_toks = []
                    for k in range(16):
                        js, st, sfree = wst.acquire()
                        W("sync", sfree)
                        if k == 0:
                            W("sync", wfree)
                        tl = DMA("sync", "W%d" % js, st, ew_d[e, k * 128:(k + 1) * 128, dt * 512:(dt + 1) * 512])
                        W("scalar", tl)
                        tc = OP("scalar", "copy", abr(wbase + k * 512, 512), st, inc=True)
                        wst.use(js, tc)
                        wk_toks.append(tc)
                    for cc in range(16):
                        jp, pp, pfree = proj.acquire()
                        W("tensor", pfree, teb, t_oner, wk_toks)
                        OP("tensor", "matmul", pp, ab[0:1, S_ONER:S_ONER + 128],
                           ab[0:1, ebst + dt * 512:ebst + dt * 512 + 512],
                           start=True, stop=False, skip_group_check=True)
                        tmm = None
                        for k in range(16):
                            tmm = OP("tensor", "matmul", pp,
                                     abr(A_H + k * 2048 + cc * 128, 128),
                                     abr(wbase + k * 512, 512),
                                     start=False, stop=(k == 15), skip_group_check=True, inc=(k == 15))
                        web.use(jw, tmm)
                        prev_e_last_mm = tmm
                        jg, glb, glfree = glr.acquire()
                        W("scalar", tmm, glfree)
                        tgl = OP("scalar", "activation", abr(glb, 512), pp, AF.Gelu_apprx_tanh, inc=True)
                        proj.use(jp, tgl)
                        W("vector", tgl, gate_tok[cc], acc_tok.get((cc, dt)))
                        gcol = gates_sb[:, cc * 8 + e:cc * 8 + e + 1]
                        dst = abr(C_ACC + cc * 2048 + dt * 512, 512)
                        if e == 0:
                            ta = OP("vector", "tensor_scalar_mul", dst, abr(glb, 512), gcol, inc=True)
                        else:
                            ta = OP("vector", "scalar_tensor_tensor", dst, abr(glb, 512), gcol, dst,
                                    ALU.mult, ALU.add, inc=True)
                        glr.use(jg, ta)
                        acc_tok[(cc, dt)] = ta

            if debug & 8:
                _prevs = [None, None]
                for _m in range(16):
                    for _c in range(4):
                        _j = _c % 2
                        W("vector", acc_tok[(_m, _c)], _prevs[_j])
                        _tc = OP("vector", "tensor_copy", afr(F_RE + _j * 512, 512),
                                 abr(C_ACC + _m * 2048 + _c * 512, 512), inc=True)
                        W("sync", _tc)
                        _ts = DMA("sync", "DG%d" % _j, dbg["acc"][:, _m * 2048 + _c * 512:_m * 2048 + (_c + 1) * 512],
                                  afr(F_RE + _j * 512, 512))
                        _prevs[_j] = _ts

            owr = Rot([D_OW0, D_OW1])
            cfT = Rot([ps_bf[:, 0:128], ps_bf[:, 128:256], ps_bf[:, 256:384], ps_bf[:, 384:512]])
            comb_tok = {}
            for tt in range(4):
                for d in range(16):
                    ttr = None
                    for ccl in range(4):
                        cc = tt * 4 + ccl
                        W("tensor", acc_tok[(cc, d // 4)], t_eye_c)
                        ttr = OP("tensor", "transpose", ps_bf[:, ccl * 128:(ccl + 1) * 128],
                                 abr(C_ACC + cc * 2048 + d * 128, 128), eye_b[:, :], inc=True)
                    W("vector", ttr)
                    tcp = OP("vector", "tensor_copy", abr(B_COMB + d * 512, 512), ps_bf[:, 0:512], inc=True)
                    for ccl in range(4):
                        comb_tok[(d, ccl)] = tcp
                comb_all = [comb_tok[(d, c)] for d in range(16) for c in range(4)]
                stat_mm_s = stat_mm_q = None
                s2_toks = []
                for m in range(16):
                    jw, owb, owfree = owr.acquire()
                    ow_toks = []
                    for piece in range(4):
                        js, st, sfree = wst.acquire()
                        W("sync", sfree)
                        if piece == 0:
                            W("sync", owfree)
                        tl = DMA("sync", "W%d" % js, st.rearrange("p (k c) -> p k c", c=128),
                                 ow_d[piece * 512:(piece + 1) * 512, m * 128:(m + 1) * 128].rearrange(
                                     "(k p) c -> p k c", p=128))
                        W("scalar", tl)
                        tc = OP("scalar", "copy", abr(owb + piece * 512, 512), st, inc=True)
                        wst.use(js, tc)
                        ow_toks.append(tc)
                    jp, pp, pfree = proj.acquire()
                    W("tensor", pfree, t_eye_c, h_tok[(m, tt)], ow_toks, comb_all)
                    OP("tensor", "matmul", pp, eye_b[:, :], abr(A_H + m * 2048 + tt * 512, 512),
                       start=True, stop=False, skip_group_check=True)
                    tmm = None
                    for k in range(16):
                        tmm = OP("tensor", "matmul", pp,
                                 abr(owb + k * 128, 128),
                                 abr(B_COMB + k * 512, 512),
                                 start=False, stop=(k == 15), skip_group_check=True, inc=(k == 15))
                    owr.use(jw, tmm)
                    W("scalar", tmm)
                    ts2 = OP("scalar", "activation", abr(B_S2 + m * 512, 512), pp, AF.Identity,
                             bias=ob_sb[:, m:m + 1], inc=True)
                    jq, sqb, sqfree = sqr.acquire()
                    W("scalar", sqfree)
                    tsq = OP("scalar", "activation", abr(sqb, 512), pp, AF.Square,
                             bias=ob_sb[:, m:m + 1], inc=True)
                    proj.use(jp, ts2)
                    proj.use(jp, tsq)
                    W("tensor", ts2, tsq, t_onec)
                    stat_mm_s = OP("tensor", "matmul", stats_s, onec_b[:, :], abr(B_S2 + m * 512, 512),
                                   start=(m == 0), stop=(m == 15), skip_group_check=True, inc=True)
                    stat_mm_q = OP("tensor", "matmul", stats_q, onec_b[:, :], abr(sqb, 512),
                                   start=(m == 0), stop=(m == 15), skip_group_check=True, inc=True)
                    sqr.use(jq, stat_mm_q)
                    s2_toks.append(ts2)

                W("scalar", stat_mm_s, stat_mm_q)
                OP("scalar", "copy", af[0:1, ST0:ST0 + 512], stats_s)
                OP("scalar", "copy", af[0:1, ST1:ST1 + 512], stats_q)
                OP("vector", "tensor_scalar_mul", af[0:1, ST0:ST0 + 512], af[0:1, ST0:ST0 + 512], 1.0 / DM)
                OP("vector", "tensor_scalar_mul", af[0:1, ST1:ST1 + 512], af[0:1, ST1:ST1 + 512], 1.0 / DM)
                OP("vector", "tensor_tensor", af[0:1, ST2:ST2 + 512], af[0:1, ST0:ST0 + 512],
                   af[0:1, ST0:ST0 + 512], ALU.mult)
                tvv = OP("vector", "tensor_tensor", af[0:1, ST1:ST1 + 512], af[0:1, ST1:ST1 + 512],
                         af[0:1, ST2:ST2 + 512], ALU.subtract, inc=True)
                W("scalar", tvv, t_eps)
                tsv = OP("scalar", "activation", af[0:1, ST1:ST1 + 512], af[0:1, ST1:ST1 + 512],
                         AF.Sqrt, bias=af[0:1, F_Z + 4:F_Z + 5], inc=True)
                W("vector", tsv)
                OP("vector", "reciprocal", af[0:1, ST1:ST1 + 512], af[0:1, ST1:ST1 + 512])
                tmr = OP("vector", "tensor_tensor", af[0:1, ST2:ST2 + 512], af[0:1, ST0:ST0 + 512],
                         af[0:1, ST1:ST1 + 512], ALU.mult, inc=True)
                W("tensor", tmr, t_mask_l)
                tb1 = OP("tensor", "matmul", bc_r, mask_f[0:1, 0:128], af[0:1, ST1:ST1 + 512],
                         start=True, stop=True, skip_group_check=True, inc=True)
                tb2 = OP("tensor", "matmul", bc_mr, mask_f[0:1, 0:128], af[0:1, ST2:ST2 + 512],
                         start=True, stop=True, skip_group_check=True, inc=True)
                for m in range(16):
                    jo, ob_, ofree = outr.acquire()
                    W("vector", tb1, tb2, s2_toks[m], ofree)
                    OP("vector", "tensor_tensor", ob_, abr(B_S2 + m * 512, 512), bc_r, ALU.mult)
                    tf = OP("vector", "tensor_tensor", ob_, ob_, bc_mr, ALU.subtract, inc=True)
                    W("sync", tf)
                    tso = DMA("sync", "O%d" % jo, out_d[m * 128:(m + 1) * 128, tt * 512:(tt + 1) * 512], ob_)
                    outr.use(jo, tso)


        _wait("sync", "O0", cnt["O0"])
        _wait("sync", "O1", cnt["O1"])

        with nc.Block() as block:
            @block.sync
            def _(e):
                for f in ops["sync"]:
                    f(e)

            @block.gpsimd
            def _(e):
                for f in ops["gpsimd"]:
                    f(e)

            @block.tensor
            def _(e):
                for f in ops["tensor"]:
                    f(e)

            @block.vector
            def _(e):
                for f in ops["vector"]:
                    f(e)

            @block.scalar
            def _(e):
                for f in ops["scalar"]:
                    f(e)

    return nc


def make_in_maps(inputs):
    x = np.asarray(inputs["L0_attn_x"], np.float32)
    Wq = np.asarray(inputs["L0_attn_Wq"], np.float32)
    bq = np.asarray(inputs["L0_attn_bq"], np.float32)
    Wk = np.asarray(inputs["L0_attn_Wk"], np.float32)
    bk = np.asarray(inputs["L0_attn_bk"], np.float32)
    Wv = np.asarray(inputs["L0_attn_Wv"], np.float32)
    bv = np.asarray(inputs["L0_attn_bv"], np.float32)
    Wo = np.ascontiguousarray(np.asarray(inputs["L0_attn_Wo"], np.float32))
    bo = np.asarray(inputs["L0_attn_bo"], np.float32)
    rw = np.asarray(inputs["L0_ffn_router_w"], np.float32)
    rb = np.asarray(inputs["L0_ffn_router_b"], np.float32)
    ew = np.ascontiguousarray(np.asarray(inputs["L0_ffn_expert_w"], np.float32))
    eb = np.asarray(inputs["L0_ffn_expert_b"], np.float32)
    ow = np.ascontiguousarray(np.asarray(inputs["L0_ffn_out_w"], np.float32))
    ob = np.asarray(inputs["L0_ffn_out_b"], np.float32)

    eye = np.eye(128, dtype=np.float32)
    maskup = np.ascontiguousarray(np.triu(np.ones((128, 128), np.float32)))
    rw_t = np.ascontiguousarray(rw.reshape(16, 128, 8).transpose(1, 0, 2).reshape(128, 128))
    bo_col = np.ascontiguousarray(bo.reshape(16, 128).T)
    ob_col = np.ascontiguousarray(ob.reshape(16, 128).T)
    eb_row = np.ascontiguousarray(eb.reshape(1, 8 * 2048))
    rb_row = np.ascontiguousarray(rb.reshape(1, 8))

    in_maps = []
    for core in range(8):
        b, hh = core // 2, core % 2
        cols = slice(hh * 1024, (hh + 1) * 1024)
        sel = np.zeros((128, 2), np.float32)
        sel[:, hh] = 1.0
        in_maps.append(dict(
            xT=np.ascontiguousarray(x[b].T),
            xR=np.ascontiguousarray(x[b, hh * 2048:(hh + 1) * 2048].T),
            wq=np.ascontiguousarray(Wq[:, cols]),
            wk=np.ascontiguousarray(Wk[:, cols]),
            wv=np.ascontiguousarray(Wv[:, cols]),
            bq_col=np.ascontiguousarray(bq[cols].reshape(8, 128).T),
            bk_row=np.ascontiguousarray(bk[cols].reshape(1, 1024)),
            bv_row=np.ascontiguousarray(bv[cols].reshape(1, 1024)),
            wo=Wo,
            bo_col=bo_col,
            rw_t=rw_t,
            rb_row=rb_row,
            ew=ew,
            eb_row=eb_row,
            ow=ow,
            ob_col=ob_col,
            maskup=maskup,
            eye=eye,
            sel_col=sel,
        ))
    return in_maps


def kernel(**inputs):
    in_maps = make_in_maps(inputs)
    nc = build(debug=0)
    res = run_bass_kernel_spmd(nc, in_maps, core_ids=list(range(8)))
    out = np.empty((4, 4096, 2048), np.float32)
    for core in range(8):
        b, hh = core // 2, core % 2
        out[b, hh * 2048:(hh + 1) * 2048, :] = res.results[core]["out_fm"].T
    return out
